# version 15
# baseline (speedup 1.0000x reference)
import os
import numpy as np
import concourse.bass as bass
import concourse.mybir as mybir
from concourse.bass_utils import run_bass_kernel_spmd

F32 = mybir.dt.float32
BF16 = mybir.dt.bfloat16
AF = mybir.ActivationFunctionType
ALU = mybir.AluOpType
AX = mybir.AxisListType

T = 8192
D = 2048
NH = 4
HD = 128
FF = 8192
E = 8
TTN = 16
CH = 16
QTN = 16
LTOK = 2048
LTT = 4
SCALE = float(HD) ** -0.5
RMS_EPS = 1e-6
LN_EPS = 1e-5

KPH = int(os.environ.get("KPH", "6"))
KSUB = os.environ.get("KSUB", "")
KDBG = os.environ.get("KDBG", "")
Q2N = 8
FT2 = 288
TILE2QK = []
for _qt in range(Q2N):
    for _kb in range(8 * _qt + 8):
        TILE2QK.append((_qt, _kb))
assert len(TILE2QK) == FT2


def lidx(qt, kb):
    return 4 * qt * (qt + 1) + kb


def dcnt(h, qt, kb):
    return 8 * (Q2N * h + qt) + max(0, kb - 8 * qt)


def pcix(h, qt, kb):
    return 144 * h + 2 * qt * (qt + 1) + kb // 2


DIAG_OF_HEAD = []
for _qt in range(Q2N):
    for _kb in range(8 * _qt, 8 * _qt + 8):
        DIAG_OF_HEAD.append((lidx(_qt, _kb), _kb - 8 * _qt))


class Bar:

    def __init__(self, eng, sem):
        self.eng, self.sem, self.n = eng, sem, 0

    def mark(self, instr):
        self.n += 1
        instr.then_inc(self.sem, 1)

    def wait(self):
        self.eng.wait_ge(self.sem, self.n)


def build_program():
    nc = bass.Bass("TRN2", target_bir_lowering=False)

    xTt = nc.dram_tensor("xTt", [TTN, 128, CH * 512], F32, kind="ExternalInput")
    xTr = nc.dram_tensor("xTr", [D, LTOK], F32, kind="ExternalInput").ap()
    wqT = nc.dram_tensor("wqT", [D, 512], F32, kind="ExternalInput").ap()
    wkT = nc.dram_tensor("wkT", [D, 512], F32, kind="ExternalInput").ap()
    wvT = nc.dram_tensor("wvT", [D, 512], F32, kind="ExternalInput").ap()
    bqkv = nc.dram_tensor("bqkv", [128, 12], F32, kind="ExternalInput").ap()
    woTt = nc.dram_tensor("woTt", [16, 128, CH * 128], F32, kind="ExternalInput")
    bo_c = nc.dram_tensor("bo_c", [128, 16], F32, kind="ExternalInput").ap()
    nw_row = nc.dram_tensor("nw_row", [1, D], F32, kind="ExternalInput").ap()
    rwT = nc.dram_tensor("rwT", [D, E], F32, kind="ExternalInput").ap()
    w1t = nc.dram_tensor("w1t", [64, 128, CH * 128], F32, kind="ExternalInput")
    b1_c = nc.dram_tensor("b1_c", [128, 64], F32, kind="ExternalInput").ap()
    w2t = nc.dram_tensor("w2t", [16, 128, 64 * 128], F32, kind="ExternalInput")
    b2_c = nc.dram_tensor("b2_c", [128, 16], F32, kind="ExternalInput").ap()
    tri_in = nc.dram_tensor("tri_in", [128, 128], F32, kind="ExternalInput").ap()
    masks_in = nc.dram_tensor("masks_in", [128, 2048], F32, kind="ExternalInput").ap()
    iota_in = nc.dram_tensor("iota_in", [128, E], F32, kind="ExternalInput").ap()
    cmask_in = nc.dram_tensor("cmask_in", [8, 2], F32, kind="ExternalInput").ap()
    id_in = nc.dram_tensor("id_in", [128, 128], F32, kind="ExternalInput").ap()

    out = nc.dram_tensor("out", [LTOK, D], F32, kind="ExternalOutput").ap()

    xn_dram = nc.dram_tensor("xn_dram", [TTN, 128, CH * 512], BF16, kind="Internal")
    wqd = nc.dram_tensor("wqd", [D, 512], BF16, kind="Internal").ap()
    wkd = nc.dram_tensor("wkd", [D, 512], BF16, kind="Internal").ap()
    wvd = nc.dram_tensor("wvd", [D, 512], BF16, kind="Internal").ap()
    rwd = nc.dram_tensor("rwd", [D, E], BF16, kind="Internal").ap()
    wobf = nc.dram_tensor("wobf", [16, 128, CH * 128], BF16, kind="Internal")
    w1bf = nc.dram_tensor("w1bf", [64, 128, CH * 128], BF16, kind="Internal")
    w2bf = nc.dram_tensor("w2bf", [16, 128, 64 * 128], BF16, kind="Internal")
    ao_h = [nc.dram_tensor(f"ao{j}", [HD, T], BF16, kind="Internal") for j in range(NH)]
    aog_h = [nc.dram_tensor(f"aog{j}", [4 * HD, T], BF16, kind="Internal") for j in range(NH)]
    rt_in = nc.dram_tensor("rt_in", [1, 16], F32, kind="Internal")
    rt_out = nc.dram_tensor("rt_out", [8, 16], F32, kind="Internal")
    x1f_dram = nc.dram_tensor("x1f_dram", [D, LTOK], F32, kind="Internal").ap()
    ffn_dram = nc.dram_tensor("ffn_dram", [D, LTOK], F32, kind="Internal").ap()
    g_dram = nc.dram_tensor("g_dram", [1, LTOK], F32, kind="Internal").ap()

    G4 = [[0, 1, 2, 3], [4, 5, 6, 7]]
    G8 = [[0, 1, 2, 3, 4, 5, 6, 7]]
    CST_N = 13

    from contextlib import ExitStack
    with ExitStack() as ctx:
        En = ctx.enter_context

        def place(name, shape, dtype, off):
            assert off % 32 == 0, (name, off)
            return nc.alloc_sbuf_tensor_at(name, shape, dtype, offset=off)

        _A = [18432]
        def pa(name, shape, dtype, sz):
            off = _A[0]
            _A[0] = (off + sz + 63) & ~63
            return place(name, shape, dtype, off)

        tri_f = pa("tri_f", [128, 128], F32, 512)
        id_f = pa("id_f", [128, 128], F32, 512)
        iota8 = pa("iota8", [128, E], F32, 32)
        cmask = pa("cmask", [8, 2], F32, 32)
        ones_col_b = pa("ones_col_b", [128, 1], BF16, 32)
        ones_col_f = pa("ones_col_f", [128, 1], F32, 32)
        ones_row_f = pa("ones_row_f", [1, 128], F32, 512)
        bqkv_s = pa("bqkv_s", [128, 12], F32, 64)
        bo_s = pa("bo_s", [128, 16], F32, 64)
        b1_s = pa("b1_s", [128, 64], F32, 256)
        b2_s = pa("b2_s", [128, 16], F32, 64)
        rw_s = pa("rw_s", [128, CH * E], BF16, 256)
        ms_sb = pa("ms_sb", [1, 512], F32, 2048)
        sm_sb = pa("sm_sb", [1, 512], F32, 2048)
        rstd_sb = pa("rstd_sb", [1, 512], F32, 2048)
        me_sb = pa("me_sb", [1, 512], F32, 2048)
        v1_sb = pa("v1_sb", [1, 512], F32, 2048)
        var_sb = pa("var_sb", [1, 512], F32, 2048)
        mr_sb = pa("mr_sb", [1, 512], F32, 2048)
        lvar_sb = place("lvar_sb", [1, 512], F32,
                        nc.lookup_mloc(sm_sb).addr)
        m2_sb = place("m2_sb", [1, 512], F32, nc.lookup_mloc(sm_sb).addr)
        rst2_sb = place("rst2_sb", [1, 512], F32,
                        nc.lookup_mloc(rstd_sb).addr)
        _nm = _A[0]
        nwr = place("nwr", [1, D], F32, _nm)
        masks = place("masks", [128, 2048], F32, _nm)
        fev = [place(f"fev{i}", [128, 512], F32, _nm + i * 2048) for i in range(2)]
        ffb = [place(f"ffb{i}", [128, 512], F32, _nm + 4096 + i * 2048) for i in range(2)]
        x6a = [place(f"x6a{i}", [128, 512], F32, _nm + i * 2048) for i in range(2)]
        x6b = [place(f"x6b{i}", [128, 512], F32, _nm + 4096 + i * 2048) for i in range(2)]
        _A[0] = _nm + 8192
        AEND = _A[0]
        assert AEND <= 46336, AEND

        BB = 46336
        xnb = [place(f"xnb{i}", [128, CH * 512], BF16, BB + i * 16384)
               for i in range(2)]
        xb = [place(f"xb{i}", [128, CH * 512], F32, BB + 32768 + i * 32768)
              for i in range(2)]
        sqt2 = [place(f"sqt2{i}", [128, 512], BF16, BB + 98304 + i * 1024)
                for i in range(2)]
        nwr_bf = place("nwr_bf", [1, D], BF16, BB + 100352)
        rstd_bf = place("rstd_bf", [1, 512], BF16, BB + 104448)
        pp_sb = [place(f"pp{i}", [128, 1024], BF16, BB + 175616 + i * 2048)
                 for i in range(2)]
        praw = place("praw", [128, 1024], BF16, BB + 179712)
        wqb = place("wqb", [128, CH * 512], BF16, BB + 32768)
        wkb = place("wkb", [128, CH * 512], BF16, BB + 49152)
        wvb = place("wvb", [128, CH * 512], BF16, BB + 65536)
        qT_s = place("qT_s", [128, T], BF16, BB + 81920)
        kT_s = place("kT_s", [128, T], BF16, BB + 98304)
        V_s = place("V_s", [128, T], BF16, BB + 114688)
        OT_s = [place("OT0", [128, T], BF16, BB + 131072)]
        pt = [place(f"pt{i}", [128, 1024], BF16, BB + 147456 + i * 2048)
              for i in range(4)]
        stile = [place(f"stile{i}", [128, 1024], F32, BB + 155648 + i * 4096)
                 for i in range(2)]
        vtmp = place("vtmp", [128, 512], F32, BB + 163840)
        sqh = place("sqh", [128, 512], BF16, BB + 165888)
        lb_s = place("lb_s", [128, 1024], F32, BB + 166912)
        linv_s = place("linv_s", [1, 1024], F32, BB + 171008)
        q2m = place("q2m", [1, 8], F32, BB + 175104)
        _qb = BB + 175136
        qn2m = place("qn2m", [1, QTN], F32, _qb + 0)
        kn2m = place("kn2m", [1, QTN], F32, _qb + 64)
        kprefm = place("kprefm", [1, QTN], F32, _qb + 128)
        bbv = place("bbv", [1, QTN], F32, _qb + 192)
        bbl = place("bbl", [1, QTN], F32, _qb + 256)
        bbe = place("bbe", [1, QTN], F32, _qb + 320)
        brow = place("brow", [1, QTN], F32, _qb + 384)
        biasq = place("biasq", [128, 8], F32, _qb + 448)
        assert _qb + 512 <= 229344

        CC = 46336
        x1T_s = place("x1T_s", [128, CH * LTOK], BF16, CC + 0)
        y_sb = place("y_sb", [128, CH * 512], F32, CC + 65536)
        y2_sb = place("y2_sb", [128, CH * 512], F32, CC + 65536)
        hT_s = place("hT_s", [128, 64 * 512], BF16, CC + 65536)
        _sm = CC + 98304
        ytmp = place("ytmp", [128, 512], F32, _sm + 0)
        ybf = place("ybf", [128, 512], BF16, _sm + 2048)
        sqy = place("sqy", [128, 512], BF16, _sm + 3072)
        rb_sb = place("rb_sb", [128, 512], F32, _sm + 4096)
        mrb_sb = place("mrb_sb", [128, 512], F32, _sm + 6144)
        x1f_t = place("x1f_t", [128, 512], F32, _sm + 8192)
        x1g_t = place("x1g_t", [128, 512], F32, _sm + 10240)
        x2_sb = place("x2_sb", [128, 512], F32, _sm + 12288)
        _rt = CC + 112640
        _r1 = ("negmax", "nmx", "ssum", "si_sb", "idx_sb", "g1_sb", "g2_sb",
               "ga_sb", "gb2_sb")
        _rh = {}
        for _i, _n in enumerate(_r1):
            _rh[_n] = place(_n, [128, 1], F32, _rt + 32 * _i)
        negmax, nmx, ssum = _rh["negmax"], _rh["nmx"], _rh["ssum"]
        si_sb, idx_sb = _rh["si_sb"], _rh["idx_sb"]
        g1_sb, g2_sb, ga_sb, gb2_sb = (_rh["g1_sb"], _rh["g2_sb"],
                                       _rh["ga_sb"], _rh["gb2_sb"])
        _r2 = ("pe_sb", "probs", "eqt", "at_sb", "t1_sb", "idxf_sb", "maskd")
        for _i, _n in enumerate(_r2):
            _rh[_n] = place(_n, [128, E], F32, _rt + 288 + 32 * _i)
        pe_sb, probs, eqt = _rh["pe_sb"], _rh["probs"], _rh["eqt"]
        at_sb, t1_sb = _rh["at_sb"], _rh["t1_sb"]
        idxf_sb, maskd = _rh["idxf_sb"], _rh["maskd"]
        pv_sb = place("pv_sb", [128, 32], F32, _rt + 512)
        ohf_sb = place("ohf_sb", [128, 256], F32, _rt + 640)
        c_sb = place("c_sb", [128, 256], F32, _rt + 1664)
        runtot = [place(f"runtot{i}", [1, 16], F32, _rt + 2688 + i * 64)
                  for i in range(2)]
        rto_sb = place("rto_sb", [8, 16], F32, _rt + 2816)
        offr_sb = place("offr_sb", [1, 16], F32, _rt + 2880)
        offb_sb = place("offb_sb", [128, 16], F32, _rt + 2944)
        glob_sb = place("glob_sb", [128, 16], F32, _rt + 3008)
        keep_sb = place("keep_sb", [128, 16], F32, _rt + 3072)
        sel_sb = place("sel_sb", [128, 16], F32, _rt + 3136)
        g_sb = place("g_sb", [128, 16], F32, _rt + 3200)
        gt_sb = place("gt_sb", [16, 128], F32, _rt + 3264)
        wob = [place(f"wob{i}", [128, CH * 128], BF16, CC + 131072 + i * 4096)
               for i in range(2)]
        w1b = [place(f"w1b{i}", [128, CH * 128], BF16, CC + 131072 + i * 4096)
               for i in range(2)]
        aob = [place(f"aob{i}", [128, CH * 512], BF16, CC + 139264 + i * 16384)
               for i in range(2)]
        w2b = [place(f"w2b{i}", [128, 64 * 128], BF16, CC + 139264 + i * 16384)
               for i in range(2)]
        grow = place("grow", [1, LTOK], F32, CC + 172032)
        xrb = [place(f"xrb{i}", [128, 512], F32, CC + 172032 + i * 2048)
               for i in range(2)]
        xt2_sb = [place(f"xt2_{i}", [128, 512], F32, CC + 178176 + i * 2048)
                  for i in range(2)]
        x6ab = place("x6ab", [128, CH * 512], F32, CC + 112640)
        x6bb = place("x6bb", [128, CH * 512], F32, CC + 145408)
        gbc_sb = place("gbc_sb", [128, 512], F32, CC + 180224)
        assert CC + 182272 <= 229344

        ps_a = [En(nc.psum_tensor(f"ps_a{i}", [128, 1024], F32)) for i in range(2)]
        ps_b = En(nc.psum_tensor("ps_b", [128, 1024], F32))
        ps_c = En(nc.psum_tensor("ps_c", [128, 1024], F32))
        def pA(i, c0=0, c1=512, p0=0, p1=128):
            return ps_a[i % 2][p0:p1, 512 * (i // 2) + c0:512 * (i // 2) + c1]

        def pC(i, c0=0, c1=512, p0=0, p1=128):
            return ps_c[p0:p1, 512 * i + c0:512 * i + c1]

        S = {}
        for name in ("pcwo", "pcw1", "pcw2", "cst", "cqv", "p6o0", "p6o1",
                     "p1x0", "p1x1", "p1sq", "p1ss", "p1ms", "p1sm", "p1r", "p1wb",
                     "p1vk", "p1o0", "p1o1",
                     "p2x0", "p2x1", "p2mm", "p2e", "p2sq", "p2v", "p2nm",
                     "p2pm", "p2l2", "p2br", "p2bb", "p2b", "p2ao", "p2ag",
                     "fst", "fexp", "fpv", "fmsk", "flv", "flb", "flbc", "fqd",
                     "fnm",
                     "p3x0", "p3x1", "p3w0", "p3w1", "p3xr0", "p3xr1",
                     "p3m", "p3e", "p3y", "p3s", "p3st",
                     "p3lnb", "p3ln", "p3mr", "p3bc", "p3bce", "p3n",
                     "p3xo0", "p3xo1",
                     "p4m", "p4sf", "p4ex", "p4oh", "p4cm", "p4ce", "p4rt",
                     "p4ti", "p4ag", "p4ro", "p4ov", "p4om", "p4ob", "p4obe",
                     "p4g", "p4gt", "p4gte", "p4go", "p4gi",
                     "p5a", "p5h",
                     "p5o", "p5ev", "p5e",
                     "p5gm", "p5gb",
                     "p6y", "p6s", "p6st",
                     "p6lnb", "p6ln",
                     "p6mr", "p6bc", "p6bce", "p6n", "p6tr", "p6tv", "msk0", "dbg",
                     "vbr", "sbr"):
            S[name] = En(nc.semaphore(name))
        for a, b in (("p6x0", "p2x0"), ("p6x1", "p2x1"),
                     ("p6f0", "p1o0"), ("p6f1", "p1o1"),
                     ("p5fo0", "p3xo0"), ("p5fo1", "p3xo1"),
                     ("p5w10", "p3w0"), ("p5w11", "p3w1"),
                     ("p5w20", "p3xr0"), ("p5w21", "p3xr1"),
                     ("fpp", "p1sq"), ("flp", "p1wb"),
                     ("cqd", "p2ao"), ("cqh", "p1ms")):
            S[a] = S[b]
        BFP = 256
        BP1X = 16 * 32
        BP1O = 16 * 8
        BP2X = 16 * 32
        BP3 = 16 * 32

        block = En(nc.Block())

        @block.sync
        def _(sy):
            for dst, src in ((tri_f, tri_in), (iota8, iota_in),
                             (cmask, cmask_in), (id_f, id_in), (nwr, nw_row),
                             (bqkv_s, bqkv), (bo_s, bo_c), (b1_s, b1_c),
                             (b2_s, b2_c)):
                sy.dma_start(out=dst[:], in_=src).then_inc(S["cst"], 16)
            for tt in range(0, TTN, 2):
                if tt >= 2:
                    sy.wait_ge(S["p1vk"], 16 * (tt - 1))
                    sy.wait_ge(S["p1sq"], 16 * (tt - 1))
                sy.dma_start(
                    out=xb[0][:], in_=xTt.ap()[tt],
                ).then_inc(S["p1x0"], 16)
            if KPH < 2:
                return
            sy.wait_ge(S["p1vk"], CH * TTN)
            sy.dma_start(out=masks[:], in_=masks_in).then_inc(S["msk0"], 16)
            for h in range(NH):
                for tt in range(TTN):
                    m = TTN * h + tt
                    sy.wait_ge(S["p1o0"], 16 * 8)
                    sy.wait_ge(S["p1o1"], 16 * 8)
                    if m >= 2:
                        sy.wait_ge(S["p2mm"], 6 * (m - 2) + 3)
                    sy.dma_start(
                        out=xnb[m % 2][:], in_=xn_dram.ap()[tt],
                    ).then_inc(S["p2x0" if m % 2 == 0 else "p2x1"], 16)
            if KPH < 3:
                return
            sy.wait_ge(S["p2ag"], NH)
            pid = nc.partition_id([mybir.EngineType.SP])
            tko = (pid % 4) * LTOK
            for tt in range(LTT):
                if tt >= 2:
                    sy.wait_ge(S["p3m"], (tt - 1) * CH)
                for j in range(NH):
                    sy.dma_start(
                        out=aob[tt % 2][:, (4 * j) * 512:(4 * j + 4) * 512]
                            .rearrange("p (r t) -> p r t", r=4),
                        in_=aog_h[j].ap().rearrange("(r p) t -> p r t", p=128)
                            [:, :, bass.ds(tko + tt * 512, 512)],
                    ).then_inc(S["p3x0" if tt % 2 == 0 else "p3x1"], 16)
                if tt == 0:
                    sy.wait_ge(S["pcwo"], 16 * 4)
                for ec in range(CH):
                    w = tt * CH + ec
                    if w >= 2:
                        sy.wait_ge(S["p3m"], w - 1)
                    sy.dma_start(
                        out=wob[ec % 2][:], in_=wobf.ap()[ec],
                    ).then_inc(S["p3w0" if w % 2 == 0 else "p3w1"], 16)
                    if w >= 2:
                        sy.wait_ge(S["p3y"], w - 1)
                    sy.dma_start(
                        out=xrb[w % 2][:],
                        in_=xTr[ec * 128:(ec + 1) * 128,
                                tt * 512:(tt + 1) * 512],
                    ).then_inc(S["p3xr0" if w % 2 == 0 else "p3xr1"], 16)
            if KPH < 5:
                return
            sy.wait_ge(S["p4go"], 16)
            sy.wait_ge(S["pcw1"], 16 * 8)
            sy.wait_ge(S["pcw2"], 16 * 8)
            sy.dma_start(out=grow[:], in_=g_dram).then_inc(S["p4gi"], 16)
            for tt in range(LTT):
                for f in range(64):
                    Fi = tt * 64 + f
                    if Fi >= 2:
                        sy.wait_ge(S["p5a"], Fi - 1)
                    sy.dma_start(
                        out=w1b[f % 2][:], in_=w1bf.ap()[f],
                    ).then_inc(S["p5w10" if Fi % 2 == 0 else "p5w11"], 16)
                for ec in range(CH):
                    Ei = tt * CH + ec
                    if Ei >= 2:
                        sy.wait_ge(S["p5o"], Ei - 1)
                    sy.dma_start(
                        out=w2b[ec % 2][:], in_=w2bf.ap()[ec],
                    ).then_inc(S["p5w20" if Ei % 2 == 0 else "p5w21"], 16)
            if KPH < 6:
                return
            sy.wait_ge(S["p5fo0"], BP3 + 16 * LTT * CH // 2)
            sy.wait_ge(S["p5fo1"], BP3 + 16 * LTT * CH // 2)
            for tt in range(LTT):
                if tt >= 1:
                    sy.wait_ge(S["p6y"], tt * CH)
                sy.dma_start(
                    out=x6ab[:].rearrange("p (k t) -> p k t", k=CH),
                    in_=x1f_dram.rearrange("(k p) t -> p k t", p=128)
                        [:, :, tt * 512:(tt + 1) * 512],
                ).then_inc(S["p6x0" if tt % 2 == 0 else "p6x1"], 16)
            sy.wait_ge(S["p6o0"], 16 * LTT * CH // 2)
            sy.wait_ge(S["p6o1"], 16 * LTT * CH // 2)

        @block.gpsimd
        def _(g):
            g.dma_start(out=nwr_bf[:], in_=nw_row).then_inc(S["cqv"], 16)
            for dst, srcw in ((wqd, wqT), (wkd, wkT), (wvd, wvT), (rwd, rwT)):
                g.dma_start(out=dst, in_=srcw).then_inc(S["cqd"], 16)
            g.wait_ge(S["p1vk"], CH * TTN)
            for i in range(4):
                g.dma_start(out=wobf.ap()[4 * i:4 * i + 4],
                            in_=woTt.ap()[4 * i:4 * i + 4]).then_inc(S["pcwo"], 16)
            for i in range(8):
                g.dma_start(out=w1bf.ap()[8 * i:8 * i + 8],
                            in_=w1t.ap()[8 * i:8 * i + 8]).then_inc(S["pcw1"], 16)
            for i in range(8):
                g.dma_start(out=w2bf.ap()[2 * i:2 * i + 2],
                            in_=w2t.ap()[2 * i:2 * i + 2]).then_inc(S["pcw2"], 16)
            if KDBG == "xn":
                g.wait_ge(S["p1o0"], 16 * TTN // 2)
                g.wait_ge(S["p1o1"], 16 * TTN // 2)
                g.dma_start(out=out[0:2048, :],
                            in_=xn_dram[:, 0:2048]).then_inc(S["dbg"], 16)
                g.wait_ge(S["dbg"], 16)
            if KPH < 2:
                return
            if KDBG == "qt" and KSUB == "qkv":
                g.wait_ge(S["p2e"], 3 * TTN * NH)
                g.dma_start(out=out[0:128, :],
                            in_=qT_s[:, 0:2048]).then_inc(S["dbg"], 16)
                g.dma_start(out=out[128:256, :],
                            in_=kT_s[:, 0:2048]).then_inc(S["dbg"], 16)
                g.dma_start(out=out[256:384, :],
                            in_=V_s[:, 0:2048]).then_inc(S["dbg"], 16)
                g.wait_ge(S["dbg"], 48)
            if KSUB == "qkv":
                return
            if KDBG == "ot":
                g.wait_ge(S["fnm"], Q2N)
                g.dma_start(out=out[0:128, :],
                            in_=OT_s[0][:, 0:2048]).then_inc(S["dbg"], 16)
                g.wait_ge(S["dbg"], 16)
            for h in range(NH):
                g.wait_ge(S["fnm"], Q2N * (h + 1))
                g.wait_ge(S["p2ao"], 64 + 16 * h)
                g.dma_start(out=ao_h[h].ap(),
                            in_=OT_s[0][:]).then_inc(S["p2ao"], 16)
                g.wait_ge(S["p2ao"], 64 + 16 * (h + 1))
                g.collective_compute(
                    "AllGather", ALU.bypass, replica_groups=G4,
                    ins=[ao_h[h].ap().opt()], outs=[aog_h[h].ap().opt()],
                ).then_inc(S["p2ag"])
            if KDBG == "p3":
                g.wait_ge(S["p3n"], LTT * CH)
                g.dma_start(out=out[0:128, :], in_=y_sb[:, 0:2048]).then_inc(S["dbg"], 16)
                g.dma_start(out=out[128:256, 0:512], in_=rb_sb[:]).then_inc(S["dbg"], 16)
                g.dma_start(out=out[256:384, 0:512], in_=mrb_sb[:]).then_inc(S["dbg"], 16)
                g.dma_start(out=out[384:512, 0:512], in_=ytmp[:]).then_inc(S["dbg"], 16)
                g.wait_ge(S["dbg"], 64)
            if KPH < 4:
                return
            g.wait_ge(S["p4rt"], CH)
            g.dma_start(out=rt_in.ap(), in_=runtot[CH % 2][:]).then_inc(S["p4ti"], 16)
            g.wait_ge(S["p4ti"], 16)
            g.collective_compute(
                "AllGather", ALU.bypass, replica_groups=G8,
                ins=[rt_in.ap().opt()], outs=[rt_out.ap().opt()],
            ).then_inc(S["p4ag"])
            g.wait_ge(S["p4ag"], 1)
            g.dma_start(out=rto_sb[:], in_=rt_out.ap()).then_inc(S["p4ro"], 16)
            g.wait_ge(S["p4gte"], 1)
            g.dma_start(out=g_dram.rearrange("o (p t) -> p o t", p=16),
                        in_=gt_sb[:].rearrange("p (o t) -> p o t", o=1)).then_inc(S["p4go"], 16)
            if KPH < 6:
                return
            for tt in range(LTT):
                for ec in range(CH):
                    w = tt * CH + ec
                    g.wait_ge(S["p6tv"], w + 1)
                    g.dma_start(
                        out=out.rearrange("(j p) e -> p j e", p=128)
                            [:, 4 * tt:4 * tt + 4, ec * 128:(ec + 1) * 128],
                        in_=xt2_sb[w % 2][:].rearrange("p (j c) -> p j c", j=4),
                    ).then_inc(S["p6o0" if w % 2 == 0 else "p6o1"], 16)
            g.wait_ge(S["p6o0"], 16 * LTT * CH // 2)
            g.wait_ge(S["p6o1"], 16 * LTT * CH // 2)

        @block.tensor
        def _(t):
            t.wait_ge(S["cst"], 16 * 9)
            t.wait_ge(S["dbg"], 1)
            t.wait_ge(S["cqv"], 16)
            for tt in range(TTN):
                if tt >= 1:
                    t.wait_ge(S["p1ms"], tt)
                for kc in range(CH):
                    W = CH * tt + kc
                    t.wait_ge(S["p1sq"], W + 1)
                    t.matmul(pC(0, p0=0, p1=1), ones_col_b[:], sqt2[kc % 2][:],
                             start=(kc == 0), stop=(kc == CH - 1),
                             skip_group_check=True).then_inc(S["p1ss"], 1)
                t.wait_ge(S["p1r"], tt + 1)
                for kc in range(CH):
                    W = CH * tt + kc
                    if W >= 2:
                        t.wait_ge(S["p1vk"], W - 1)
                    t.matmul(pA(kc % 2), nwr_bf[0:1, kc * 128:(kc + 1) * 128],
                             rstd_bf[0:1, :], start=True, stop=True,
                             skip_group_check=True).then_inc(S["p1wb"], 1)
            if KPH < 2:
                return
            t.wait_ge(S["p1vk"], CH * TTN)
            t.wait_ge(S["cqh"], 16 + 16 * 4)
            for h in range(NH):
                for tt in range(TTN):
                    m = TTN * h + tt
                    t.wait_ge(S["p2x0" if m % 2 == 0 else "p2x1"],
                              16 * (m // 2 + 1))
                    if h > 0 and tt == 0 and KSUB != "qkv":
                        t.wait_ge(S["fexp"], FT2 * h)
                    for pi, wsb in enumerate((wqb, wkb, wvb)):
                        if m >= 1:
                            t.wait_ge(S["p2e"], 3 * (m - 1) + pi + 1)
                        for kc in range(CH):
                            mm = t.matmul(
                                pA(pi),
                                wsb[:, kc * 512 + h * 128: kc * 512 + (h + 1) * 128],
                                xnb[m % 2][:, kc * 512:(kc + 1) * 512],
                                start=(kc == 0), stop=(kc == CH - 1),
                                skip_group_check=True)
                        mm.then_inc(S["p2mm"], 1)
                    t.wait_ge(S["p2e"], 3 * m + 3)
                    if m >= 1:
                        t.wait_ge(S["p2v"], m)
                    if h > 0 and tt == 0 and KSUB != "qkv":
                        t.wait_ge(S["flbc"], Q2N * h)
                    for j in range(4):
                        mm = t.transpose(pC(1, j * 128, (j + 1) * 128),
                                         vtmp[:, j * 128:(j + 1) * 128], id_f[:])
                    mm.then_inc(S["p2mm"], 1)
                    t.wait_ge(S["p2sq"], 2 * m + 1)
                    if m >= 1:
                        t.wait_ge(S["p2nm"], 2 * m)
                    t.matmul(pC(0, p0=0, p1=1), ones_col_b[:], sqh[:],
                             start=True, stop=True,
                             skip_group_check=True).then_inc(S["p2mm"], 1)
                    t.wait_ge(S["p2sq"], 2 * m + 2)
                    t.wait_ge(S["p2nm"], 2 * m + 1)
                    t.matmul(pC(0, p0=0, p1=1), ones_col_b[:], sqh[:],
                             start=True, stop=True,
                             skip_group_check=True).then_inc(S["p2mm"], 1)
                if KSUB == "qkv":
                    continue
                t.wait_ge(S["p2br"], h + 1)
                t.matmul(pC(0, 0, Q2N), ones_row_f[:], brow[0:1, 0:Q2N],
                         start=True, stop=True,
                         skip_group_check=True).then_inc(S["p2bb"], 1)
                g0 = FT2 * h
                t.wait_ge(S["p2e"], 3 * TTN * (h + 1))

                def emit_st(L):
                    qt2, kb2 = TILE2QK[L]
                    gg = g0 + L
                    if gg >= 2:
                        t.wait_ge(S["fexp"], gg - 1)
                    t.matmul(ps_a[gg % 2][:, 0:512],
                             kT_s[:, kb2 * 128:(kb2 + 1) * 128],
                             qT_s[:, qt2 * 1024:qt2 * 1024 + 512],
                             start=True, stop=True, skip_group_check=True)
                    t.matmul(ps_a[gg % 2][:, 512:1024],
                             kT_s[:, kb2 * 128:(kb2 + 1) * 128],
                             qT_s[:, qt2 * 1024 + 512:qt2 * 1024 + 1024],
                             start=True, stop=True,
                             skip_group_check=True).then_inc(S["fst"], 1)

                for L in range(2):
                    emit_st(L)
                for qt in range(Q2N):
                    nk = 8 * qt + 8
                    Q = Q2N * h + qt
                    for kb in range(nk):
                        L = lidx(qt, kb)
                        gg = g0 + L
                        if L + 2 < FT2:
                            emit_st(L + 2)
                        if kb == 0:
                            if Q >= 1:
                                t.wait_ge(S["fqd"], Q)
                                t.wait_ge(S["flbc"], Q)
                            t.wait_ge(S["p2b"], h + 1)
                            if qt == 0:
                                t.wait_ge(S["p2v"], TTN * (h + 1))
                        t.wait_ge(S["fexp"], gg + 1)
                        t.matmul(ps_b[:, 0:512], V_s[:, kb * 128:(kb + 1) * 128],
                                 pt[gg % 4][:, 0:512], start=(kb == 0),
                                 stop=(kb == nk - 1), skip_group_check=True)
                        t.matmul(ps_b[:, 512:1024],
                                 V_s[:, kb * 128:(kb + 1) * 128],
                                 pt[gg % 4][:, 512:1024], start=(kb == 0),
                                 stop=(kb == nk - 1),
                                 skip_group_check=True).then_inc(S["fpv"], 1)
                        if kb % 2 == 1:
                            PC = pcix(h, qt, kb)
                            t.wait_ge(S["fpp"], BFP + PC + 1)
                            t.matmul(pC(0, p0=0, p1=1), ones_col_b[:],
                                     pp_sb[PC % 2][:, 0:512],
                                     start=(kb == 1), stop=(kb == nk - 1),
                                     skip_group_check=True)
                            t.matmul(pC(1, p0=0, p1=1), ones_col_b[:],
                                     pp_sb[PC % 2][:, 512:1024],
                                     start=(kb == 1), stop=(kb == nk - 1),
                                     skip_group_check=True).then_inc(S["flp"], 1)
                    t.wait_ge(S["flv"], Q + 1)
                    t.matmul(pC(0), ones_row_f[:], linv_s[0:1, 0:512],
                             start=True, stop=True, skip_group_check=True)
                    t.matmul(pC(1), ones_row_f[:], linv_s[0:1, 512:1024],
                             start=True, stop=True,
                             skip_group_check=True).then_inc(S["flb"], 1)
            if KPH < 3:
                return
            t.wait_ge(S["fexp"], FT2 * NH)
            t.wait_ge(S["flbc"], Q2N * NH)
            t.wait_ge(S["p2v"], TTN * NH)
            for tt in range(LTT):
                t.wait_ge(S["p3x0" if tt % 2 == 0 else "p3x1"],
                          16 * NH * (tt // 2 + 1))
                for ec in range(CH):
                    w = tt * CH + ec
                    t.wait_ge(S["p3w0" if w % 2 == 0 else "p3w1"],
                              16 * (w // 2 + 1))
                    if w >= 2:
                        t.wait_ge(S["p3e"], w - 1)
                    for dc in range(CH):
                        mm = t.matmul(pA(ec % 2),
                                      wob[ec % 2][:, dc * 128:(dc + 1) * 128],
                                      aob[tt % 2][:, dc * 512:(dc + 1) * 512],
                                      start=(dc == 0), stop=(dc == CH - 1),
                                      skip_group_check=True)
                    mm.then_inc(S["p3m"], 1)
                    if ec > 0:
                        t.wait_ge(S["p3s"], w)
                        t.matmul(pC(0, p0=0, p1=1), ones_col_b[:], ybf[:],
                                 start=(ec == 1), stop=False,
                                 skip_group_check=True)
                        t.matmul(pC(1, p0=0, p1=1), ones_col_b[:], sqy[:],
                                 start=(ec == 1), stop=False,
                                 skip_group_check=True).then_inc(S["p3st"], 1)
                t.wait_ge(S["p3s"], (tt + 1) * CH)
                t.matmul(pC(0, p0=0, p1=1), ones_col_b[:], ybf[:],
                         start=False, stop=True, skip_group_check=True)
                t.matmul(pC(1, p0=0, p1=1), ones_col_b[:], sqy[:],
                         start=False, stop=True,
                         skip_group_check=True).then_inc(S["p3st"], 1)
                t.wait_ge(S["p3mr"], tt + 1)
                if tt >= 1:
                    t.wait_ge(S["p3bce"], tt)
                t.matmul(pC(0), ones_row_f[:], rst2_sb[0:1, :],
                         start=True, stop=True, skip_group_check=True)
                t.matmul(pC(1), ones_row_f[:], mr_sb[0:1, :],
                         start=True, stop=True,
                         skip_group_check=True).then_inc(S["p3bc"], 1)
            if KPH < 4:
                return
            t.wait_ge(S["p3e"], LTT * CH)
            t.wait_ge(S["p3bce"], LTT)
            for tc in range(CH):
                if tc >= 2:
                    t.wait_ge(S["p4ex"], tc - 1)
                t.wait_ge(S["p3n"], (tc // 4 + 1) * CH)
                for hc in range(CH):
                    mm = t.matmul(pA(tc % 2, 0, E),
                                  x1T_s[:, hc * LTOK + tc * 128: hc * LTOK + (tc + 1) * 128],
                                  rw_s[:, hc * E:(hc + 1) * E],
                                  start=(hc == 0), stop=(hc == CH - 1),
                                  skip_group_check=True)
                mm.then_inc(S["p4m"], 1)
                t.wait_ge(S["p4oh"], tc + 1)
                if tc >= 1:
                    t.wait_ge(S["p4ce"], 2 * tc)
                t.wait_ge(S["p4rt"], tc)
                t.wait_ge(S["msk0"], 17)
                t.matmul(pA(2, 0, E), tri_f[:], ohf_sb[:, 16 * tc:16 * tc + 8],
                         start=True, stop=False, skip_group_check=True)
                t.matmul(pA(2, 0, E), ones_row_f[:],
                         runtot[tc % 2][0:1, 0:E],
                         start=False, stop=True,
                         skip_group_check=True).then_inc(S["p4cm"], 1)
                t.matmul(pA(3, 0, E), tri_f[:],
                         ohf_sb[:, 16 * tc + 8:16 * tc + 16],
                         start=True, stop=False, skip_group_check=True)
                t.matmul(pA(3, 0, E), ones_row_f[:],
                         runtot[tc % 2][0:1, 8:16],
                         start=False, stop=True,
                         skip_group_check=True).then_inc(S["p4cm"], 1)
                t.matmul(pC(0, 0, 16, 0, 1), ones_col_f[:],
                         ohf_sb[:, 16 * tc:16 * (tc + 1)],
                         start=True, stop=True,
                         skip_group_check=True).then_inc(S["p4cm"], 1)
            t.wait_ge(S["p4ro"], 16)
            t.wait_ge(S["p4rt"], CH)
            t.matmul(pC(0, 0, 16, 0, 1), cmask[:, 0:1], rto_sb[:],
                     start=True, stop=True, skip_group_check=True)
            t.matmul(pC(1, 0, 16, 0, 1), cmask[:, 1:2], rto_sb[:],
                     start=True, stop=True,
                     skip_group_check=True).then_inc(S["p4om"], 1)
            t.wait_ge(S["p4ov"], 1)
            t.matmul(pC(1, 0, 16), ones_row_f[:], offr_sb[0:1, :],
                     start=True, stop=True,
                     skip_group_check=True).then_inc(S["p4ob"], 1)
            t.wait_ge(S["p4g"], CH)
            t.wait_ge(S["p4obe"], 1)
            t.wait_ge(S["p4rt"], CH)
            t.transpose(pC(0, 0, 128, 0, 16), g_sb[:, 0:16],
                        id_f[:]).then_inc(S["p4gt"], 1)
            if KPH < 5:
                return
            t.wait_ge(S["pcw1"], 16 * 8)
            t.wait_ge(S["pcw2"], 16 * 8)
            t.wait_ge(S["p4ex"], CH)
            for tt in range(LTT):
                for f in range(64):
                    Fi = tt * 64 + f
                    t.wait_ge(S["p5w10" if Fi % 2 == 0 else "p5w11"],
                              BP3 + 16 * (Fi // 2 + 1))
                    if Fi == 2:
                        t.wait_ge(S["p4ce"], 2 * CH)
                    if Fi >= 4:
                        t.wait_ge(S["p5h"], Fi - 3)
                    for hc in range(CH):
                        mm = t.matmul(
                            pA(Fi % 4),
                            w1b[f % 2][:, hc * 128:(hc + 1) * 128],
                            x1T_s[:, hc * LTOK + tt * 512: hc * LTOK + (tt + 1) * 512],
                            start=(hc == 0), stop=(hc == CH - 1),
                            skip_group_check=True)
                    mm.then_inc(S["p5a"], 1)
                t.wait_ge(S["p4gi"], 16)
                if tt >= 1:
                    t.wait_ge(S["p5e"], tt * CH)
                else:
                    t.wait_ge(S["p4gte"], 1)
                t.matmul(pC(0), ones_row_f[:],
                         grow[0:1, tt * 512:(tt + 1) * 512],
                         start=True, stop=True,
                         skip_group_check=True).then_inc(S["p5gm"], 1)
                for ec in range(CH):
                    Ei = tt * CH + ec
                    t.wait_ge(S["p5w20" if Ei % 2 == 0 else "p5w21"],
                              BP3 + 16 * (Ei // 2 + 1))
                    t.wait_ge(S["p5h"], 64 * (tt + 1))
                    if Ei >= 1:
                        t.wait_ge(S["p5ev"], Ei)
                    for fc in range(64):
                        mm = t.matmul(
                            ps_b[:, 0:512],
                            w2b[ec % 2][:, fc * 128:(fc + 1) * 128],
                            hT_s[:, fc * 512:(fc + 1) * 512],
                            start=(fc == 0), stop=(fc == 63),
                            skip_group_check=True)
                    mm.then_inc(S["p5o"], 1)
            if KPH < 6:
                return
            t.wait_ge(S["p5gb"], LTT)
            t.wait_ge(S["p4gte"], 1)
            for tt in range(LTT):
                for ec in range(CH):
                    w = tt * CH + ec
                    t.wait_ge(S["p6s"], w + 1)
                    if ec == 0 and tt >= 1:
                        t.wait_ge(S["p6bce"], tt)
                        t.wait_ge(S["p6tv"], tt * CH)
                    t.matmul(pC(0, p0=0, p1=1), ones_col_b[:], ybf[:],
                             start=(ec == 0), stop=(ec == CH - 1),
                             skip_group_check=True)
                    t.matmul(pC(1, p0=0, p1=1), ones_col_b[:], sqy[:],
                             start=(ec == 0), stop=(ec == CH - 1),
                             skip_group_check=True).then_inc(S["p6st"], 1)
                t.wait_ge(S["p6mr"], tt + 1)
                t.matmul(pC(0), ones_row_f[:], rst2_sb[0:1, :],
                         start=True, stop=True, skip_group_check=True)
                t.matmul(pC(1), ones_row_f[:], mr_sb[0:1, :],
                         start=True, stop=True,
                         skip_group_check=True).then_inc(S["p6bc"], 1)
                for ec in range(CH):
                    w = tt * CH + ec
                    t.wait_ge(S["p6n"], w + 1)
                    if w >= 2:
                        t.wait_ge(S["p6tv"], w - 1)
                    for j in range(4):
                        mm = t.transpose(ps_b[:, j * 128:(j + 1) * 128],
                                         x2_sb[:, j * 128:(j + 1) * 128], id_f[:])
                    mm.then_inc(S["p6tr"], 1)

        @block.scalar
        def _(s):
            sb = Bar(s, S["sbr"])
            s.dma_start(out=xb[1][:], in_=xTt.ap()[1]).then_inc(S["p1x1"], 16)
            for tt in range(TTN):
                s.wait_ge(S["p1x0" if tt % 2 == 0 else "p1x1"],
                          16 * (tt // 2 + 1))
                for kc in range(CH):
                    W = CH * tt + kc
                    if W >= 2:
                        s.wait_ge(S["p1ss"], W - 1)
                    s.activation(sqt2[kc % 2][:],
                                 xb[tt % 2][:, kc * 512:(kc + 1) * 512],
                                 AF.Square).then_inc(S["p1sq"], 1)
                s.wait_ge(S["p1ms"], tt + 1)
                if tt >= 1:
                    s.wait_ge(S["p1r"], tt)
                s.activation(sm_sb[:], ms_sb[:], AF.Sqrt).then_inc(S["p1sm"], 1)
                s.wait_ge(S["p1vk"], 16 * (tt + 1))
                s.dma_start(
                    out=xn_dram.ap()[tt],
                    in_=xnb[tt % 2][:],
                ).then_inc(S["p1o0" if tt % 2 == 0 else "p1o1"], 16)
                if tt % 2 == 1 and tt + 2 < TTN:
                    s.dma_start(out=xb[1][:],
                                in_=xTt.ap()[tt + 2]).then_inc(S["p1x1"], 16)
            if KPH < 2:
                return
            s.wait_ge(S["cqd"], 16 * 4)
            for dst, srcw in ((wqb, wqd), (wkb, wkd), (wvb, wvd)):
                s.dma_start(
                    out=dst[:].rearrange("p (k d) -> p k d", k=CH),
                    in_=srcw.rearrange("(k p) d -> p k d", p=128),
                ).then_inc(S["cqh"], 16)
            s.dma_start(
                out=rw_s[:].rearrange("p (k e) -> p k e", k=CH),
                in_=rwd.rearrange("(k p) e -> p k e", p=128),
            ).then_inc(S["cqh"], 16)
            for h in range(NH):
                for tt in range(TTN):
                    m = TTN * h + tt
                    for pi in range(3):
                        s.wait_ge(S["p2mm"], 6 * m + pi + 1)
                        bias = bqkv_s[:, pi * 4 + h: pi * 4 + h + 1]
                        if pi == 0:
                            s.activation(qT_s[:, tt * 512:(tt + 1) * 512],
                                         pA(0), AF.Identity,
                                         bias=bias).then_inc(S["p2e"], 1)
                        elif pi == 1:
                            s.activation(kT_s[:, tt * 512:(tt + 1) * 512],
                                         pA(1), AF.Identity,
                                         bias=bias).then_inc(S["p2e"], 1)
                        else:
                            s.activation(vtmp[:], pA(2), AF.Identity,
                                         bias=bias).then_inc(S["p2e"], 1)
                if KSUB == "qkv":
                    continue
                s.wait_ge(S["p2pm"], h + 1)
                sb.mark(s.activation(bbl[0:1, 0:Q2N], bbv[0:1, 0:Q2N], AF.Ln))
                sb.wait()
                s.activation(bbe[0:1, 0:Q2N], bbl[0:1, 0:Q2N], AF.Exp,
                             scale=0.5).then_inc(S["p2l2"], 1)
                s.wait_ge(S["p2bb"], h + 1)
                s.activation(biasq[:], pC(0, 0, Q2N), AF.Copy).then_inc(S["p2b"], 1)
                s.wait_ge(S["p2b"], h + 1)
                g0 = FT2 * h
                for qt in range(Q2N):
                    nk = 8 * qt + 8
                    Q = Q2N * h + qt
                    for kb in range(nk):
                        L = lidx(qt, kb)
                        gg = g0 + L
                        diag = kb >= 8 * qt
                        if diag:
                            s.wait_ge(S["fmsk"], dcnt(h, qt, kb) + 1)
                        else:
                            s.wait_ge(S["fst"], gg + 1)
                        if gg >= 4:
                            s.wait_ge(S["fpv"], gg - 3)
                            gp = gg - 4
                            hp = gp // FT2
                            qtp, kbp = TILE2QK[gp % FT2]
                            s.wait_ge(S["fpp"], BFP + pcix(hp, qtp, kbp) + 1)
                        src = stile[dcnt(h, qt, kb) % 2][:] if diag else ps_a[gg % 2][:]
                        s.activation(pt[gg % 4][:], src, AF.Exp,
                                     bias=biasq[:, qt:qt + 1],
                                     scale=SCALE).then_inc(S["fexp"], 1)
                    s.wait_ge(S["flb"], Q + 1)
                    s.activation(lb_s[:], ps_c[:], AF.Copy).then_inc(S["flbc"], 1)
            if KPH < 3:
                return
            for tt in range(LTT):
                for ec in range(CH):
                    w = tt * CH + ec
                    s.wait_ge(S["p3m"], w + 1)
                    if w >= 1:
                        s.wait_ge(S["p3y"], w)
                    s.activation(ytmp[:], pA(ec % 2), AF.Identity,
                                 bias=bo_s[:, ec:ec + 1]).then_inc(S["p3e"], 1)
                s.wait_ge(S["p3lnb"], tt + 1)
                sb.mark(s.activation(lvar_sb[:], var_sb[:], AF.Ln))
                sb.wait()
                s.activation(rst2_sb[:], lvar_sb[:], AF.Exp,
                             scale=-0.5).then_inc(S["p3ln"], 1)
                s.wait_ge(S["p3bc"], tt + 1)
                s.activation(rb_sb[:], pC(0), AF.Copy)
                s.activation(mrb_sb[:], pC(1), AF.Copy).then_inc(S["p3bce"], 1)
                for ec in range(CH):
                    w = tt * CH + ec
                    s.wait_ge(S["p3n"], w + 1)
                    s.dma_start(
                        out=x1f_dram.rearrange("(k p) t -> k p t", p=128)
                            [ec, :, tt * 512:(tt + 1) * 512],
                        in_=x1g_t[:]).then_inc(S["p3xo0" if w % 2 == 0 else "p3xo1"], 16)
            if KPH < 4:
                return
            for tc in range(CH):
                s.wait_ge(S["p4sf"], tc + 1)
                s.activation(pe_sb[:], pA(tc % 2, 0, E), AF.Exp,
                             bias=negmax[:]).then_inc(S["p4ex"], 1)
            if KPH < 5:
                return
            s.wait_ge(S["p3xo0"], 16 * 32)
            s.wait_ge(S["p3xo1"], 16 * 32)
            for tt in range(LTT):
                for f in range(64):
                    Fi = tt * 64 + f
                    s.wait_ge(S["p5a"], Fi + 1)
                    if tt >= 1 and f < 2:
                        s.wait_ge(S["p5o"], tt * CH)
                    s.activation(hT_s[:, f * 512:(f + 1) * 512], pA(Fi % 4),
                                 AF.Gelu,
                                 bias=b1_s[:, f:f + 1]).then_inc(S["p5h"], 1)
                s.wait_ge(S["p5gm"], tt + 1)
                s.activation(gbc_sb[:], pC(0), AF.Copy).then_inc(S["p5gb"], 1)
                for ec in range(CH):
                    Ei = tt * CH + ec
                    s.wait_ge(S["p5o"], Ei + 1)
                    if Ei >= 2:
                        s.wait_ge(S["p5e"], Ei - 1)
                    s.activation(fev[Ei % 2][:], ps_b[:, 0:512], AF.Identity,
                                 bias=b2_s[:, ec:ec + 1]).then_inc(S["p5ev"], 1)
                    s.wait_ge(S["p5e"], Ei + 1)
                    s.dma_start(
                        out=ffn_dram.rearrange("(k p) t -> k p t", p=128)
                            [ec, :, tt * 512:(tt + 1) * 512],
                        in_=ffb[Ei % 2][:]).then_inc(S["p5fo0" if Ei % 2 == 0 else "p5fo1"], 16)
            if KPH < 6:
                return
            for tt in range(LTT):
                if tt >= 1:
                    s.wait_ge(S["p6y"], tt * CH)
                s.dma_start(
                    out=x6bb[:].rearrange("p (k t) -> p k t", k=CH),
                    in_=ffn_dram.rearrange("(k p) t -> p k t", p=128)
                        [:, :, tt * 512:(tt + 1) * 512],
                ).then_inc(S["p6f0" if tt % 2 == 0 else "p6f1"], 16)
                s.wait_ge(S["p6lnb"], tt + 1)
                sb.mark(s.activation(lvar_sb[:], var_sb[:], AF.Ln))
                sb.wait()
                s.activation(rst2_sb[:], lvar_sb[:], AF.Exp,
                             scale=-0.5).then_inc(S["p6ln"], 1)
                s.wait_ge(S["p6bc"], tt + 1)
                s.activation(rb_sb[:], pC(0), AF.Copy)
                s.activation(mrb_sb[:], pC(1), AF.Copy).then_inc(S["p6bce"], 1)

        @block.vector
        def _(v):
            vb = Bar(v, S["vbr"])
            v.memset(ones_col_b[:], 1.0)
            v.memset(ones_col_f[:], 1.0)
            v.memset(ones_row_f[:], 1.0).then_inc(S["dbg"], 1)
            for tt in range(TTN):
                v.wait_ge(S["p1ss"], 16 * (tt + 1))
                if tt >= 1:
                    v.wait_ge(S["p1sm"], tt)
                v.tensor_scalar(ms_sb[:], pC(0, p0=0, p1=1), 1.0 / D, RMS_EPS,
                                ALU.mult, ALU.add).then_inc(S["p1ms"], 1)
                v.wait_ge(S["p1sm"], tt + 1)
                vb.mark(v.reciprocal(rstd_sb[:], sm_sb[:]))
                vb.wait()
                v.tensor_copy(rstd_bf[:], rstd_sb[:]).then_inc(S["p1r"], 1)
                if tt >= 2:
                    v.wait_ge(S["p1o0" if tt % 2 == 0 else "p1o1"],
                              16 * ((tt - 2) // 2 + 1))
                for kc in range(CH):
                    W = CH * tt + kc
                    v.wait_ge(S["p1wb"], W + 1)
                    v.tensor_mul(xnb[tt % 2][:, kc * 512:(kc + 1) * 512],
                                 xb[tt % 2][:, kc * 512:(kc + 1) * 512],
                                 pA(kc % 2)).then_inc(S["p1vk"], 1)
            if KPH < 2:
                return
            for h in range(NH):
                for tt in range(TTN):
                    m = TTN * h + tt
                    v.wait_ge(S["p2e"], 3 * m + 1)
                    if m >= 1:
                        v.wait_ge(S["p2mm"], 6 * m)
                    v.tensor_mul(sqh[:], qT_s[:, tt * 512:(tt + 1) * 512],
                                 qT_s[:, tt * 512:(tt + 1) * 512]).then_inc(S["p2sq"], 1)
                    v.wait_ge(S["p2mm"], 6 * m + 5)
                    v.reduce_max(qn2m[0:1, tt:tt + 1], pC(0, p0=0, p1=1),
                                 AX.X).then_inc(S["p2nm"], 1)
                    v.wait_ge(S["p2e"], 3 * m + 2)
                    v.tensor_mul(sqh[:], kT_s[:, tt * 512:(tt + 1) * 512],
                                 kT_s[:, tt * 512:(tt + 1) * 512]).then_inc(S["p2sq"], 1)
                    v.wait_ge(S["p2mm"], 6 * m + 6)
                    v.reduce_max(kn2m[0:1, tt:tt + 1], pC(0, p0=0, p1=1),
                                 AX.X).then_inc(S["p2nm"], 1)
                    v.wait_ge(S["p2mm"], 6 * m + 4)
                    v.tensor_copy(V_s[:, tt * 512:(tt + 1) * 512],
                                  pC(1)).then_inc(S["p2v"], 1)
                if KSUB == "qkv":
                    continue
                v.wait_ge(S["p2nm"], 2 * TTN * (h + 1))
                for qt in range(Q2N):
                    vb.mark(v.reduce_max(kprefm[0:1, qt:qt + 1],
                                         kn2m[0:1, 0:2 * qt + 2], AX.X))
                    vb.mark(v.reduce_max(q2m[0:1, qt:qt + 1],
                                         qn2m[0:1, 2 * qt:2 * qt + 2], AX.X))
                vb.wait()
                v.tensor_mul(bbv[0:1, 0:Q2N], q2m[:],
                             kprefm[0:1, 0:Q2N]).then_inc(S["p2pm"], 1)
                v.wait_ge(S["p2l2"], h + 1)
                v.tensor_scalar(brow[0:1, 0:Q2N], bbe[0:1, 0:Q2N],
                                -SCALE * 1.001, None,
                                ALU.mult).then_inc(S["p2br"], 1)
                if h == 0:
                    v.wait_ge(S["msk0"], 16)
                g0 = FT2 * h
                nmc = Q2N * h
                for qt in range(Q2N):
                    nk = 8 * qt + 8
                    Q = Q2N * h + qt
                    for kb in range(nk):
                        gg = g0 + lidx(qt, kb)
                        if kb >= 8 * qt:
                            dd = dcnt(h, qt, kb)
                            v.wait_ge(S["fst"], gg + 1)
                            if dd >= 2:
                                dprev = dd - 2
                                hh = dprev // 64
                                Lp, _ = DIAG_OF_HEAD[dprev % 64]
                                v.wait_ge(S["fexp"], FT2 * hh + Lp + 1)
                            jj = kb - 8 * qt
                            v.tensor_add(stile[dd % 2][:], ps_a[gg % 2][:],
                                         masks[:, 1024 - 128 * jj:2048 - 128 * jj]).then_inc(S["fmsk"], 1)
                        if kb % 2 == 1:
                            PC = pcix(h, qt, kb)
                            v.wait_ge(S["fexp"], gg + 1)
                            if PC >= 2:
                                v.wait_ge(S["flp"], BFP + PC - 1)
                            v.tensor_add(pp_sb[PC % 2][:], pt[(gg - 1) % 4][:],
                                         pt[gg % 4][:]).then_inc(S["fpp"], 1)
                    PCe = 144 * h + 2 * (qt + 1) * (qt + 2)
                    if qt >= 1:
                        v.wait_ge(S["flbc"], Q)
                        if qt == 1 and h >= 1:
                            v.wait_ge(S["p2ao"], 64 + 16 * h)
                        v.tensor_mul(OT_s[0][:, (qt - 1) * 1024:qt * 1024],
                                     praw[:], lb_s[:]).then_inc(S["fnm"], 1)
                        nmc += 1
                    v.wait_ge(S["flp"], BFP + PCe)
                    v.wait_ge(S["fnm"], nmc)
                    v.tensor_copy(praw[:], ps_b[:]).then_inc(S["fqd"], 1)
                    if Q >= 1:
                        v.wait_ge(S["flb"], Q)
                    v.reciprocal(linv_s[:], ps_c[0:1, 0:1024]).then_inc(S["flv"], 1)
                v.wait_ge(S["flbc"], Q2N * (h + 1))
                v.tensor_mul(OT_s[0][:, (Q2N - 1) * 1024:Q2N * 1024],
                             praw[:], lb_s[:]).then_inc(S["fnm"], 1)
            if KPH < 3:
                return
            for tt in range(LTT):
                for ec in range(CH):
                    w = tt * CH + ec
                    v.wait_ge(S["p3e"], w + 1)
                    v.wait_ge(S["p3xr0" if w % 2 == 0 else "p3xr1"],
                              16 * (w // 2 + 1))
                    if w >= 1:
                        v.wait_ge(S["p3st"], w)
                    v.tensor_add(y_sb[:, ec * 512:(ec + 1) * 512], ytmp[:],
                                 xrb[w % 2][:]).then_inc(S["p3y"], 1)
                    v.wait_ge(S["p3y"], w + 1)
                    v.tensor_copy(ybf[:], y_sb[:, ec * 512:(ec + 1) * 512])
                    v.tensor_mul(sqy[:], y_sb[:, ec * 512:(ec + 1) * 512],
                                 y_sb[:, ec * 512:(ec + 1) * 512]).then_inc(S["p3s"], 1)
                v.wait_ge(S["p3st"], (tt + 1) * CH)
                vb.mark(v.tensor_scalar(me_sb[:], pC(0, p0=0, p1=1), 1.0 / D,
                                        None, ALU.mult))
                vb.mark(v.tensor_scalar(v1_sb[:], pC(1, p0=0, p1=1), 1.0 / D,
                                        None, ALU.mult))
                vb.wait()
                vb.mark(v.tensor_mul(m2_sb[:], me_sb[:], me_sb[:]))
                vb.wait()
                vb.mark(v.tensor_sub(ms_sb[:], v1_sb[:], m2_sb[:]))
                vb.wait()
                v.tensor_scalar(var_sb[:], ms_sb[:], LN_EPS, None,
                                ALU.add).then_inc(S["p3lnb"], 1)
                v.wait_ge(S["p3ln"], tt + 1)
                v.tensor_mul(mr_sb[:], me_sb[:], rst2_sb[:]).then_inc(S["p3mr"], 1)
                for ec in range(CH):
                    w = tt * CH + ec
                    v.wait_ge(S["p3bce"], tt + 1)
                    if w >= 1:
                        v.wait_ge(S["p3xo0" if (w - 1) % 2 == 0 else "p3xo1"],
                                  16 * ((w - 1) // 2 + 1))
                    vb.mark(v.tensor_mul(x1f_t[:],
                                         y_sb[:, ec * 512:(ec + 1) * 512],
                                         rb_sb[:]))
                    vb.wait()
                    vb.mark(v.tensor_sub(x1g_t[:], x1f_t[:], mrb_sb[:]))
                    vb.wait()
                    v.tensor_copy(x1T_s[:, ec * LTOK + tt * 512: ec * LTOK + (tt + 1) * 512],
                                  x1g_t[:]).then_inc(S["p3n"], 1)
            if KPH < 4:
                return
            v.memset(runtot[0][:], 0.0)
            v.memset(runtot[1][:], 0.0).then_inc(S["msk0"], 1)
            for tc in range(CH):
                v.wait_ge(S["p4m"], tc + 1)
                vb.mark(v.reduce_max(nmx[:], pA(tc % 2, 0, E), AX.X))
                vb.wait()
                v.tensor_scalar(negmax[:], nmx[:], -1.0, None,
                                ALU.mult).then_inc(S["p4sf"], 1)
                v.wait_ge(S["p4ex"], tc + 1)
                vb.mark(v.reduce_sum(ssum[:], pe_sb[:], AX.X))
                vb.wait()
                vb.mark(v.reciprocal(si_sb[:], ssum[:]))
                vb.wait()
                vb.mark(v.tensor_scalar(probs[:], pe_sb[:], si_sb[:], None,
                                        ALU.mult))
                vb.wait()
                vb.mark(v.reduce_max(pv_sb[:, 2 * tc:2 * tc + 1], probs[:], AX.X))
                vb.wait()
                vb.mark(v.tensor_scalar(eqt[:], probs[:],
                                        pv_sb[:, 2 * tc:2 * tc + 1],
                                        None, ALU.is_equal))
                vb.wait()
                vb.mark(v.tensor_mul(at_sb[:], eqt[:], iota8[:]))
                vb.wait()
                vb.mark(v.scalar_tensor_tensor(t1_sb[:], eqt[:], -99.0,
                                               at_sb[:], ALU.mult, ALU.add))
                vb.wait()
                vb.mark(v.tensor_scalar(idxf_sb[:], t1_sb[:], 99.0, None,
                                        ALU.add))
                vb.wait()
                vb.mark(v.tensor_reduce(idx_sb[:], idxf_sb[:], AX.X, ALU.min))
                vb.wait()
                vb.mark(v.tensor_scalar(ohf_sb[:, 16 * tc:16 * tc + 8],
                                        iota8[:], idx_sb[:], None,
                                        ALU.is_equal))
                vb.wait()
                vb.mark(v.scalar_tensor_tensor(maskd[:],
                                               ohf_sb[:, 16 * tc:16 * tc + 8],
                                               -2.0, probs[:], ALU.mult,
                                               ALU.add))
                vb.wait()
                vb.mark(v.reduce_max(pv_sb[:, 2 * tc + 1:2 * tc + 2], maskd[:],
                                     AX.X))
                vb.wait()
                vb.mark(v.tensor_scalar(eqt[:], maskd[:],
                                        pv_sb[:, 2 * tc + 1:2 * tc + 2], None,
                                        ALU.is_equal))
                vb.wait()
                vb.mark(v.tensor_mul(at_sb[:], eqt[:], iota8[:]))
                vb.wait()
                vb.mark(v.scalar_tensor_tensor(t1_sb[:], eqt[:], -99.0,
                                               at_sb[:], ALU.mult, ALU.add))
                vb.wait()
                vb.mark(v.tensor_scalar(idxf_sb[:], t1_sb[:], 99.0, None,
                                        ALU.add))
                vb.wait()
                vb.mark(v.tensor_reduce(idx_sb[:], idxf_sb[:], AX.X, ALU.min))
                vb.wait()
                v.tensor_scalar(ohf_sb[:, 16 * tc + 8:16 * tc + 16], iota8[:],
                                idx_sb[:], None,
                                ALU.is_equal).then_inc(S["p4oh"], 1)
                v.wait_ge(S["p4cm"], 3 * tc + 2)
                v.tensor_copy(c_sb[:, 16 * tc:16 * tc + 8], pA(2, 0, E))
                v.tensor_copy(c_sb[:, 16 * tc + 8:16 * (tc + 1)],
                              pA(3, 0, E)).then_inc(S["p4ce"], 2)
                v.wait_ge(S["p4cm"], 3 * tc + 3)
                v.wait_ge(S["p4rt"], tc)
                v.tensor_add(runtot[(tc + 1) % 2][:], runtot[tc % 2][:],
                             pC(0, 0, 16, 0, 1)).then_inc(S["p4rt"], 1)
            v.wait_ge(S["p4om"], 1)
            vb.mark(v.tensor_copy(offr_sb[:], pC(0, 0, 16, 0, 1)))
            vb.wait()
            v.tensor_add(offr_sb[0:1, 8:16], offr_sb[0:1, 8:16],
                         pC(1, 0, 8, 0, 1)).then_inc(S["p4ov"], 1)
            v.wait_ge(S["p4ob"], 1)
            v.tensor_copy(offb_sb[:], pC(1, 0, 16)).then_inc(S["p4obe"], 1)
            v.wait_ge(S["p4obe"], 1)
            v.wait_ge(S["p4ce"], 2 * CH)
            v.wait_ge(S["p4oh"], CH)
            for tc in range(CH):
                vb.mark(v.tensor_add(glob_sb[:], c_sb[:, 16 * tc:16 * (tc + 1)],
                                     offb_sb[:]))
                vb.wait()
                vb.mark(v.tensor_scalar(keep_sb[:], glob_sb[:], 512.5, None,
                                        ALU.is_le))
                vb.wait()
                vb.mark(v.tensor_mul(sel_sb[:], keep_sb[:],
                                     ohf_sb[:, 16 * tc:16 * (tc + 1)]))
                vb.wait()
                vb.mark(v.reduce_sum(g1_sb[:], sel_sb[:, 0:8], AX.X))
                vb.mark(v.reduce_sum(g2_sb[:], sel_sb[:, 8:16], AX.X))
                vb.wait()
                vb.mark(v.tensor_mul(ga_sb[:], g1_sb[:],
                                     pv_sb[:, 2 * tc:2 * tc + 1]))
                vb.mark(v.tensor_mul(gb2_sb[:], g2_sb[:],
                                     pv_sb[:, 2 * tc + 1:2 * tc + 2]))
                vb.wait()
                v.tensor_add(g_sb[:, tc:tc + 1], ga_sb[:],
                             gb2_sb[:]).then_inc(S["p4g"], 1)
            v.wait_ge(S["p4gt"], 1)
            v.tensor_copy(gt_sb[:], pC(0, 0, 128, 0, 16)).then_inc(S["p4gte"], 1)
            if KPH < 5:
                return
            for tt in range(LTT):
                for ec in range(CH):
                    Ei = tt * CH + ec
                    v.wait_ge(S["p5ev"], Ei + 1)
                    v.wait_ge(S["p5gb"], tt + 1)
                    if Ei >= 2:
                        v.wait_ge(S["p5fo0" if Ei % 2 == 0 else "p5fo1"],
                                  BP3 + 16 * ((Ei - 2) // 2 + 1))
                    v.tensor_mul(ffb[Ei % 2][:], fev[Ei % 2][:],
                                 gbc_sb[:]).then_inc(S["p5e"], 1)
            if KPH < 6:
                return
            v.wait_ge(S["p5o"], LTT * CH)
            for tt in range(LTT):
                v.wait_ge(S["p6x0" if tt % 2 == 0 else "p6x1"],
                          BP1X + 16 * (tt // 2 + 1))
                v.wait_ge(S["p6f0" if tt % 2 == 0 else "p6f1"],
                          BP1O + 16 * (tt // 2 + 1))
                for ec in range(CH):
                    w = tt * CH + ec
                    if w >= 1:
                        v.wait_ge(S["p6st"], w)
                    v.tensor_add(y2_sb[:, ec * 512:(ec + 1) * 512],
                                 x6ab[:, ec * 512:(ec + 1) * 512],
                                 x6bb[:, ec * 512:(ec + 1) * 512]).then_inc(S["p6y"], 1)
                    v.wait_ge(S["p6y"], w + 1)
                    v.tensor_copy(ybf[:], y2_sb[:, ec * 512:(ec + 1) * 512])
                    v.tensor_mul(sqy[:], y2_sb[:, ec * 512:(ec + 1) * 512],
                                 y2_sb[:, ec * 512:(ec + 1) * 512]).then_inc(S["p6s"], 1)
                v.wait_ge(S["p6st"], (tt + 1) * CH)
                vb.mark(v.tensor_scalar(me_sb[:], pC(0, p0=0, p1=1), 1.0 / D,
                                        None, ALU.mult))
                vb.mark(v.tensor_scalar(v1_sb[:], pC(1, p0=0, p1=1), 1.0 / D, None,
                                        ALU.mult))
                vb.wait()
                vb.mark(v.tensor_mul(m2_sb[:], me_sb[:], me_sb[:]))
                vb.wait()
                vb.mark(v.tensor_sub(ms_sb[:], v1_sb[:], m2_sb[:]))
                vb.wait()
                v.tensor_scalar(var_sb[:], ms_sb[:], LN_EPS, None,
                                ALU.add).then_inc(S["p6lnb"], 1)
                v.wait_ge(S["p6ln"], tt + 1)
                v.tensor_mul(mr_sb[:], me_sb[:], rst2_sb[:]).then_inc(S["p6mr"], 1)
                for ec in range(CH):
                    w = tt * CH + ec
                    v.wait_ge(S["p6bce"], tt + 1)
                    vb.mark(v.tensor_mul(x1f_t[:],
                                         y2_sb[:, ec * 512:(ec + 1) * 512],
                                         rb_sb[:]))
                    vb.wait()
                    v.tensor_sub(x2_sb[:], x1f_t[:], mrb_sb[:]).then_inc(S["p6n"], 1)
                    v.wait_ge(S["p6tr"], w + 1)
                    if w >= 2:
                        v.wait_ge(S["p6o0" if w % 2 == 0 else "p6o1"],
                                  16 * ((w - 2) // 2 + 1))
                    v.tensor_copy(xt2_sb[w % 2][:],
                                  ps_b[:, 0:512]).then_inc(S["p6tv"], 1)
            v.wait_ge(S["p6o0"], 16 * LTT * CH // 2)
            v.wait_ge(S["p6o1"], 16 * LTT * CH // 2)

    return nc


_CACHE = {}


def kernel(L0_attn_x, L0_attn_norm_w, L0_attn_wq, L0_attn_bq, L0_attn_wk,
           L0_attn_bk, L0_attn_wv, L0_attn_bv, L0_attn_wo, L0_attn_bo,
           L0_ffn_router_w, L0_ffn_w1, L0_ffn_b1, L0_ffn_w2, L0_ffn_b2):
    x = np.ascontiguousarray(np.asarray(L0_attn_x, np.float32))
    norm_w = np.asarray(L0_attn_norm_w, np.float32)
    wq = np.asarray(L0_attn_wq, np.float32)
    bq = np.asarray(L0_attn_bq, np.float32)
    wk = np.asarray(L0_attn_wk, np.float32)
    bk = np.asarray(L0_attn_bk, np.float32)
    wv = np.asarray(L0_attn_wv, np.float32)
    bv = np.asarray(L0_attn_bv, np.float32)
    wo = np.asarray(L0_attn_wo, np.float32)
    bo = np.asarray(L0_attn_bo, np.float32)
    router_w = np.asarray(L0_ffn_router_w, np.float32)
    w1 = np.asarray(L0_ffn_w1, np.float32)
    b1 = np.asarray(L0_ffn_b1, np.float32)
    w2 = np.asarray(L0_ffn_w2, np.float32)
    b2 = np.asarray(L0_ffn_b2, np.float32)

    if "nc" not in _CACHE:
        _CACHE["nc"] = build_program()
    nc = _CACHE["nc"]

    woTt = np.ascontiguousarray(
        wo.T.reshape(16, 128, 16, 128).transpose(2, 1, 0, 3)
        .reshape(16, 128, CH * 128))
    w1t = np.ascontiguousarray(
        w1.T.reshape(16, 128, 64, 128).transpose(2, 1, 0, 3)
        .reshape(64, 128, CH * 128))
    w2t = np.ascontiguousarray(
        w2.T.reshape(64, 128, 16, 128).transpose(2, 1, 0, 3)
        .reshape(16, 128, 64 * 128))
    bo_cols = np.ascontiguousarray(bo.reshape(16, 128).T)
    b1_cols = np.ascontiguousarray(b1.reshape(64, 128).T)
    b2_cols = np.ascontiguousarray(b2.reshape(16, 128).T)
    nw_r = np.ascontiguousarray(norm_w.reshape(1, D))
    rwTc = np.ascontiguousarray(router_w.T)
    tri = np.triu(np.ones((128, 128), np.float32))
    _k = np.arange(128)[:, None]
    _u = np.arange(2048)[None, :]
    msk = np.where(_u >= _k + 1024, 0.0, -30000.0).astype(np.float32)
    iota = np.ascontiguousarray(
        np.tile(np.arange(E, dtype=np.float32), (128, 1)))
    ident = np.eye(128, dtype=np.float32)

    in_maps = []
    for c in range(8):
        b, hgr = c // 4, c % 4
        heads = [4 * j + hgr for j in range(NH)]
        xTb = np.ascontiguousarray(x[b].T)
        xTt_b = np.ascontiguousarray(
            xTb.reshape(16, 128, TTN, 512).transpose(2, 1, 0, 3)
            .reshape(TTN, 128, CH * 512))

        def _slice(w):
            rows = np.concatenate([w[h * 128:(h + 1) * 128] for h in heads], 0)
            return np.ascontiguousarray(rows.T)

        bq_c = np.stack([bq[h * 128:(h + 1) * 128] for h in heads], 1)
        bk_c = np.stack([bk[h * 128:(h + 1) * 128] for h in heads], 1)
        bv_c = np.stack([bv[h * 128:(h + 1) * 128] for h in heads], 1)
        bqkv_c = np.ascontiguousarray(
            np.concatenate([bq_c, bk_c, bv_c], 1))
        cm = np.zeros((8, 2), np.float32)
        cm[:, 0] = (np.arange(8) < c).astype(np.float32)
        cm[:, 1] = 1.0
        in_maps.append(dict(
            xTt=xTt_b, xTr=np.ascontiguousarray(xTb[:, hgr * LTOK:(hgr + 1) * LTOK]),
            wqT=_slice(wq), wkT=_slice(wk), wvT=_slice(wv),
            bqkv=bqkv_c, woTt=woTt, bo_c=bo_cols, nw_row=nw_r, rwT=rwTc,
            w1t=w1t, b1_c=b1_cols, w2t=w2t, b2_c=b2_cols, tri_in=tri,
            masks_in=msk, iota_in=iota, cmask_in=cm, id_in=ident))

    res = run_bass_kernel_spmd(nc, in_maps, core_ids=list(range(8)))
    y = np.empty((2, T, D), np.float32)
    for c in range(8):
        b, hgr = c // 4, c % 4
        y[b, hgr * LTOK:(hgr + 1) * LTOK, :] = np.asarray(res.results[c]["out"],
                                                          np.float32)
    return y


# version 16 (final)
# speedup vs baseline: 1.0302x; 1.0302x over previous
import os
import numpy as np
import concourse.bass as bass
import concourse.mybir as mybir
from concourse.bass_utils import run_bass_kernel_spmd

F32 = mybir.dt.float32
BF16 = mybir.dt.bfloat16
AF = mybir.ActivationFunctionType
ALU = mybir.AluOpType
AX = mybir.AxisListType

T = 8192
D = 2048
NH = 4
HD = 128
FF = 8192
E = 8
TTN = 16
CH = 16
QTN = 16
LTOK = 2048
LTT = 4
SCALE = float(HD) ** -0.5
RMS_EPS = 1e-6
LN_EPS = 1e-5

KPH = int(os.environ.get("KPH", "6"))
KSUB = os.environ.get("KSUB", "")
KDBG = os.environ.get("KDBG", "")
Q2N = 8
FT2 = 288
TILE2QK = []
for _qt in range(Q2N):
    for _kb in range(8 * _qt + 8):
        TILE2QK.append((_qt, _kb))
assert len(TILE2QK) == FT2


def lidx(qt, kb):
    return 4 * qt * (qt + 1) + kb


def dcnt(h, qt, kb):
    return 8 * (Q2N * h + qt) + max(0, kb - 8 * qt)


def pcix(h, qt, kb):
    return 144 * h + 2 * qt * (qt + 1) + kb // 2


DIAG_OF_HEAD = []
for _qt in range(Q2N):
    for _kb in range(8 * _qt, 8 * _qt + 8):
        DIAG_OF_HEAD.append((lidx(_qt, _kb), _kb - 8 * _qt))


class Bar:

    def __init__(self, eng, sem):
        self.eng, self.sem, self.n = eng, sem, 0

    def mark(self, instr):
        self.n += 1
        instr.then_inc(self.sem, 1)

    def wait(self):
        self.eng.wait_ge(self.sem, self.n)


def build_program():
    nc = bass.Bass("TRN2", target_bir_lowering=False)

    xTt = nc.dram_tensor("xTt", [TTN, 128, CH * 512], F32, kind="ExternalInput")
    xTr = nc.dram_tensor("xTr", [D, LTOK], F32, kind="ExternalInput").ap()
    wqT = nc.dram_tensor("wqT", [D, 512], F32, kind="ExternalInput").ap()
    wkT = nc.dram_tensor("wkT", [D, 512], F32, kind="ExternalInput").ap()
    wvT = nc.dram_tensor("wvT", [D, 512], F32, kind="ExternalInput").ap()
    bqkv = nc.dram_tensor("bqkv", [128, 12], F32, kind="ExternalInput").ap()
    woTt = nc.dram_tensor("woTt", [16, 128, CH * 128], F32, kind="ExternalInput")
    bo_c = nc.dram_tensor("bo_c", [128, 16], F32, kind="ExternalInput").ap()
    nw_row = nc.dram_tensor("nw_row", [1, D], F32, kind="ExternalInput").ap()
    rwT = nc.dram_tensor("rwT", [D, E], F32, kind="ExternalInput").ap()
    w1t = nc.dram_tensor("w1t", [64, 128, CH * 128], F32, kind="ExternalInput")
    b1_c = nc.dram_tensor("b1_c", [128, 64], F32, kind="ExternalInput").ap()
    w2t = nc.dram_tensor("w2t", [16, 128, 64 * 128], F32, kind="ExternalInput")
    b2_c = nc.dram_tensor("b2_c", [128, 16], F32, kind="ExternalInput").ap()
    tri_in = nc.dram_tensor("tri_in", [128, 128], F32, kind="ExternalInput").ap()
    masks_in = nc.dram_tensor("masks_in", [128, 2048], F32, kind="ExternalInput").ap()
    iota_in = nc.dram_tensor("iota_in", [128, E], F32, kind="ExternalInput").ap()
    cmask_in = nc.dram_tensor("cmask_in", [8, 2], F32, kind="ExternalInput").ap()
    id_in = nc.dram_tensor("id_in", [128, 128], F32, kind="ExternalInput").ap()

    out = nc.dram_tensor("out", [LTOK, D], F32, kind="ExternalOutput").ap()

    xn_dram = nc.dram_tensor("xn_dram", [TTN, 128, CH * 512], BF16, kind="Internal")
    wobf = nc.dram_tensor("wobf", [16, 128, CH * 128], BF16, kind="Internal")
    w1bf = nc.dram_tensor("w1bf", [64, 128, CH * 128], BF16, kind="Internal")
    w2bf = nc.dram_tensor("w2bf", [16, 128, 64 * 128], BF16, kind="Internal")
    ao_h = [nc.dram_tensor(f"ao{j}", [HD, T], BF16, kind="Internal") for j in range(NH)]
    aog_h = [nc.dram_tensor(f"aog{j}", [4 * HD, T], BF16, kind="Internal") for j in range(NH)]
    rt_in = nc.dram_tensor("rt_in", [1, 16], F32, kind="Internal")
    rt_out = nc.dram_tensor("rt_out", [8, 16], F32, kind="Internal")
    x1f_dram = nc.dram_tensor("x1f_dram", [D, LTOK], F32, kind="Internal").ap()
    ffn_dram = nc.dram_tensor("ffn_dram", [D, LTOK], F32, kind="Internal").ap()
    g_dram = nc.dram_tensor("g_dram", [1, LTOK], F32, kind="Internal").ap()

    G4 = [[0, 1, 2, 3], [4, 5, 6, 7]]
    G8 = [[0, 1, 2, 3, 4, 5, 6, 7]]
    CST_N = 13

    from contextlib import ExitStack
    with ExitStack() as ctx:
        En = ctx.enter_context

        def place(name, shape, dtype, off):
            assert off % 32 == 0, (name, off)
            return nc.alloc_sbuf_tensor_at(name, shape, dtype, offset=off)

        _A = [18432]
        def pa(name, shape, dtype, sz):
            off = _A[0]
            _A[0] = (off + sz + 63) & ~63
            return place(name, shape, dtype, off)

        tri_f = pa("tri_f", [128, 128], F32, 512)
        id_f = pa("id_f", [128, 128], F32, 512)
        iota8 = pa("iota8", [128, E], F32, 32)
        cmask = pa("cmask", [8, 2], F32, 32)
        ones_col_b = pa("ones_col_b", [128, 1], BF16, 32)
        ones_col_f = pa("ones_col_f", [128, 1], F32, 32)
        ones_row_f = pa("ones_row_f", [1, 128], F32, 512)
        bqkv_s = pa("bqkv_s", [128, 12], F32, 64)
        bo_s = pa("bo_s", [128, 16], F32, 64)
        b1_s = pa("b1_s", [128, 64], F32, 256)
        b2_s = pa("b2_s", [128, 16], F32, 64)
        rw_s = pa("rw_s", [128, CH * E], BF16, 256)
        ms_sb = pa("ms_sb", [1, 512], F32, 2048)
        sm_sb = pa("sm_sb", [1, 512], F32, 2048)
        rstd_sb = pa("rstd_sb", [1, 512], F32, 2048)
        me_sb = pa("me_sb", [1, 512], F32, 2048)
        v1_sb = pa("v1_sb", [1, 512], F32, 2048)
        var_sb = pa("var_sb", [1, 512], F32, 2048)
        mr_sb = pa("mr_sb", [1, 512], F32, 2048)
        lvar_sb = place("lvar_sb", [1, 512], F32,
                        nc.lookup_mloc(sm_sb).addr)
        m2_sb = place("m2_sb", [1, 512], F32, nc.lookup_mloc(sm_sb).addr)
        rst2_sb = place("rst2_sb", [1, 512], F32,
                        nc.lookup_mloc(rstd_sb).addr)
        _nm = _A[0]
        nwr = place("nwr", [1, D], F32, _nm)
        masks = place("masks", [128, 2048], F32, _nm)
        fev = [place(f"fev{i}", [128, 512], F32, _nm + i * 2048) for i in range(2)]
        ffb = [place(f"ffb{i}", [128, 512], F32, _nm + 4096 + i * 2048) for i in range(2)]
        x6a = [place(f"x6a{i}", [128, 512], F32, _nm + i * 2048) for i in range(2)]
        x6b = [place(f"x6b{i}", [128, 512], F32, _nm + 4096 + i * 2048) for i in range(2)]
        _A[0] = _nm + 8192
        AEND = _A[0]
        assert AEND <= 46336, AEND

        BB = 46336
        xnb = [place(f"xnb{i}", [128, CH * 512], BF16, BB + i * 16384)
               for i in range(2)]
        xb = [place(f"xb{i}", [128, CH * 512], F32, BB + 32768 + i * 32768)
              for i in range(2)]
        sqt2 = [place(f"sqt2{i}", [128, 512], BF16, BB + 98304 + i * 1024)
                for i in range(2)]
        nwr_bf = place("nwr_bf", [1, D], BF16, BB + 100352)
        rstd_bf = place("rstd_bf", [1, 512], BF16, BB + 104448)
        pp_sb = [place(f"pp{i}", [128, 1024], BF16, BB + 175616 + i * 2048)
                 for i in range(2)]
        praw = place("praw", [128, 1024], BF16, BB + 179712)
        wqb = place("wqb", [128, CH * 512], BF16, BB + 32768)
        wkb = place("wkb", [128, CH * 512], BF16, BB + 49152)
        wvb = place("wvb", [128, CH * 512], BF16, BB + 65536)
        qT_s = place("qT_s", [128, T], BF16, BB + 81920)
        kT_s = place("kT_s", [128, T], BF16, BB + 98304)
        V_s = place("V_s", [128, T], BF16, BB + 114688)
        OT_s = [place("OT0", [128, T], BF16, BB + 131072)]
        pt = [place(f"pt{i}", [128, 1024], BF16, BB + 147456 + i * 2048)
              for i in range(4)]
        stile = [place(f"stile{i}", [128, 1024], F32, BB + 155648 + i * 4096)
                 for i in range(2)]
        vtmp = place("vtmp", [128, 512], F32, BB + 163840)
        sqh = place("sqh", [128, 512], BF16, BB + 165888)
        lb_s = place("lb_s", [128, 1024], F32, BB + 166912)
        linv_s = place("linv_s", [1, 1024], F32, BB + 171008)
        q2m = place("q2m", [1, 8], F32, BB + 175104)
        _qb = BB + 175136
        qn2m = place("qn2m", [1, QTN], F32, _qb + 0)
        kn2m = place("kn2m", [1, QTN], F32, _qb + 64)
        kprefm = place("kprefm", [1, QTN], F32, _qb + 128)
        bbv = place("bbv", [1, QTN], F32, _qb + 192)
        bbl = place("bbl", [1, QTN], F32, _qb + 256)
        bbe = place("bbe", [1, QTN], F32, _qb + 320)
        brow = place("brow", [1, QTN], F32, _qb + 384)
        biasq = place("biasq", [128, 8], F32, _qb + 448)
        assert _qb + 512 <= 229344

        CC = 46336
        x1T_s = place("x1T_s", [128, CH * LTOK], BF16, CC + 0)
        y_sb = place("y_sb", [128, CH * 512], F32, CC + 65536)
        y2_sb = place("y2_sb", [128, CH * 512], F32, CC + 65536)
        hT_s = place("hT_s", [128, 64 * 512], BF16, CC + 65536)
        _sm = CC + 98304
        ytmp = place("ytmp", [128, 512], F32, _sm + 0)
        ybf = place("ybf", [128, 512], BF16, _sm + 2048)
        sqy = place("sqy", [128, 512], BF16, _sm + 3072)
        rb_sb = place("rb_sb", [128, 512], F32, _sm + 4096)
        mrb_sb = place("mrb_sb", [128, 512], F32, _sm + 6144)
        x1f_t = place("x1f_t", [128, 512], F32, _sm + 8192)
        x1g_t = place("x1g_t", [128, 512], F32, _sm + 10240)
        x2_sb = place("x2_sb", [128, 512], F32, _sm + 12288)
        _rt = CC + 112640
        _r1 = ("negmax", "nmx", "ssum", "si_sb", "idx_sb", "g1_sb", "g2_sb",
               "ga_sb", "gb2_sb")
        _rh = {}
        for _i, _n in enumerate(_r1):
            _rh[_n] = place(_n, [128, 1], F32, _rt + 32 * _i)
        negmax, nmx, ssum = _rh["negmax"], _rh["nmx"], _rh["ssum"]
        si_sb, idx_sb = _rh["si_sb"], _rh["idx_sb"]
        g1_sb, g2_sb, ga_sb, gb2_sb = (_rh["g1_sb"], _rh["g2_sb"],
                                       _rh["ga_sb"], _rh["gb2_sb"])
        _r2 = ("pe_sb", "probs", "eqt", "at_sb", "t1_sb", "idxf_sb", "maskd")
        for _i, _n in enumerate(_r2):
            _rh[_n] = place(_n, [128, E], F32, _rt + 288 + 32 * _i)
        pe_sb, probs, eqt = _rh["pe_sb"], _rh["probs"], _rh["eqt"]
        at_sb, t1_sb = _rh["at_sb"], _rh["t1_sb"]
        idxf_sb, maskd = _rh["idxf_sb"], _rh["maskd"]
        pv_sb = place("pv_sb", [128, 32], F32, _rt + 512)
        ohf_sb = place("ohf_sb", [128, 256], F32, _rt + 640)
        c_sb = place("c_sb", [128, 256], F32, _rt + 1664)
        runtot = [place(f"runtot{i}", [1, 16], F32, _rt + 2688 + i * 64)
                  for i in range(2)]
        rto_sb = place("rto_sb", [8, 16], F32, _rt + 2816)
        offr_sb = place("offr_sb", [1, 16], F32, _rt + 2880)
        offb_sb = place("offb_sb", [128, 16], F32, _rt + 2944)
        glob_sb = place("glob_sb", [128, 16], F32, _rt + 3008)
        keep_sb = place("keep_sb", [128, 16], F32, _rt + 3072)
        sel_sb = place("sel_sb", [128, 16], F32, _rt + 3136)
        g_sb = place("g_sb", [128, 16], F32, _rt + 3200)
        gt_sb = place("gt_sb", [16, 128], F32, _rt + 3264)
        wob = [place(f"wob{i}", [128, CH * 128], BF16, CC + 131072 + i * 4096)
               for i in range(2)]
        w1b = [place(f"w1b{i}", [128, CH * 128], BF16, CC + 131072 + i * 4096)
               for i in range(2)]
        aob = [place(f"aob{i}", [128, CH * 512], BF16, CC + 139264 + i * 16384)
               for i in range(2)]
        w2b = [place(f"w2b{i}", [128, 64 * 128], BF16, CC + 139264 + i * 16384)
               for i in range(2)]
        grow = place("grow", [1, LTOK], F32, CC + 172032)
        xrb = [place(f"xrb{i}", [128, 512], F32, CC + 172032 + i * 2048)
               for i in range(2)]
        xt2_sb = [place(f"xt2_{i}", [128, 512], F32, CC + 178176 + i * 2048)
                  for i in range(2)]
        x6ab = place("x6ab", [128, CH * 512], F32, CC + 112640)
        x6bb = place("x6bb", [128, CH * 512], F32, CC + 145408)
        gbc_sb = place("gbc_sb", [128, 512], F32, CC + 180224)
        assert CC + 182272 <= 229344

        ps_a = [En(nc.psum_tensor(f"ps_a{i}", [128, 1024], F32)) for i in range(2)]
        ps_b = En(nc.psum_tensor("ps_b", [128, 1024], F32))
        ps_c = En(nc.psum_tensor("ps_c", [128, 1024], F32))
        def pA(i, c0=0, c1=512, p0=0, p1=128):
            return ps_a[i % 2][p0:p1, 512 * (i // 2) + c0:512 * (i // 2) + c1]

        def pC(i, c0=0, c1=512, p0=0, p1=128):
            return ps_c[p0:p1, 512 * i + c0:512 * i + c1]

        S = {}
        for name in ("pcwo", "pcw1", "pcw2", "cst", "cqv", "p6o0", "p6o1",
                     "p1x0", "p1x1", "p1sq", "p1ss", "p1ms", "p1sm", "p1r", "p1wb",
                     "p1vk", "p1o0", "p1o1",
                     "p2x0", "p2x1", "p2mm", "p2e", "p2sq", "p2v", "p2nm",
                     "p2pm", "p2l2", "p2br", "p2bb", "p2b", "p2ao", "p2ag",
                     "fst", "fexp", "fpv", "fmsk", "flv", "flb", "flbc", "fqd",
                     "fnm",
                     "p3x0", "p3x1", "p3w0", "p3w1", "p3xr0", "p3xr1",
                     "p3m", "p3e", "p3y", "p3s", "p3st",
                     "p3lnb", "p3ln", "p3mr", "p3bc", "p3bce", "p3n",
                     "p3xo0", "p3xo1",
                     "p4m", "p4sf", "p4ex", "p4oh", "p4cm", "p4ce", "p4rt",
                     "p4ti", "p4ag", "p4ro", "p4ov", "p4om", "p4ob", "p4obe",
                     "p4g", "p4gt", "p4gte", "p4go", "p4gi",
                     "p5a", "p5h",
                     "p5o", "p5ev", "p5e",
                     "p5gm", "p5gb",
                     "p6y", "p6s", "p6st",
                     "p6lnb", "p6ln",
                     "p6mr", "p6bc", "p6bce", "p6n", "p6tr", "p6tv", "msk0", "dbg",
                     "vbr", "sbr"):
            S[name] = En(nc.semaphore(name))
        for a, b in (("p6x0", "p2x0"), ("p6x1", "p2x1"),
                     ("p6f0", "p1o0"), ("p6f1", "p1o1"),
                     ("p5fo0", "p3xo0"), ("p5fo1", "p3xo1"),
                     ("p5w10", "p3w0"), ("p5w11", "p3w1"),
                     ("p5w20", "p3xr0"), ("p5w21", "p3xr1"),
                     ("fpp", "p1sq"), ("flp", "p1wb")):
            S[a] = S[b]
        BFP = 256
        BP1X = 16 * 32
        BP1O = 16 * 8
        BP2X = 16 * 32
        BP3 = 16 * 32

        block = En(nc.Block())

        @block.sync
        def _(sy):
            for dst, src in ((tri_f, tri_in), (iota8, iota_in),
                             (cmask, cmask_in), (id_f, id_in), (nwr, nw_row),
                             (bqkv_s, bqkv), (bo_s, bo_c), (b1_s, b1_c),
                             (b2_s, b2_c)):
                sy.dma_start(out=dst[:], in_=src).then_inc(S["cst"], 16)
            for tt in range(0, TTN, 2):
                if tt >= 2:
                    sy.wait_ge(S["p1vk"], 16 * (tt - 1))
                    sy.wait_ge(S["p1sq"], 16 * (tt - 1))
                sy.dma_start(
                    out=xb[0][:], in_=xTt.ap()[tt],
                ).then_inc(S["p1x0"], 16)
            if KPH < 2:
                return
            sy.wait_ge(S["p1vk"], CH * TTN)
            sy.dma_start(out=masks[:], in_=masks_in).then_inc(S["msk0"], 16)
            for h in range(NH):
                for tt in range(TTN):
                    m = TTN * h + tt
                    sy.wait_ge(S["p1o0"], 16 * 8)
                    sy.wait_ge(S["p1o1"], 16 * 8)
                    if m >= 2:
                        sy.wait_ge(S["p2mm"], 6 * (m - 2) + 3)
                    sy.dma_start(
                        out=xnb[m % 2][:], in_=xn_dram.ap()[tt],
                    ).then_inc(S["p2x0" if m % 2 == 0 else "p2x1"], 16)
            if KPH < 3:
                return
            sy.wait_ge(S["p2ag"], NH)
            pid = nc.partition_id([mybir.EngineType.SP])
            tko = (pid % 4) * LTOK
            for tt in range(LTT):
                if tt >= 2:
                    sy.wait_ge(S["p3m"], (tt - 1) * CH)
                for j in range(NH):
                    sy.dma_start(
                        out=aob[tt % 2][:, (4 * j) * 512:(4 * j + 4) * 512]
                            .rearrange("p (r t) -> p r t", r=4),
                        in_=aog_h[j].ap().rearrange("(r p) t -> p r t", p=128)
                            [:, :, bass.ds(tko + tt * 512, 512)],
                    ).then_inc(S["p3x0" if tt % 2 == 0 else "p3x1"], 16)
                if tt == 0:
                    sy.wait_ge(S["pcwo"], 16 * 4)
                for ec in range(CH):
                    w = tt * CH + ec
                    if w >= 2:
                        sy.wait_ge(S["p3m"], w - 1)
                    sy.dma_start(
                        out=wob[ec % 2][:], in_=wobf.ap()[ec],
                    ).then_inc(S["p3w0" if w % 2 == 0 else "p3w1"], 16)
                    if w >= 2:
                        sy.wait_ge(S["p3y"], w - 1)
                    sy.dma_start(
                        out=xrb[w % 2][:],
                        in_=xTr[ec * 128:(ec + 1) * 128,
                                tt * 512:(tt + 1) * 512],
                    ).then_inc(S["p3xr0" if w % 2 == 0 else "p3xr1"], 16)
            if KPH < 5:
                return
            sy.wait_ge(S["p4go"], 16)
            sy.wait_ge(S["pcw1"], 16 * 8)
            sy.wait_ge(S["pcw2"], 16 * 8)
            sy.dma_start(out=grow[:], in_=g_dram).then_inc(S["p4gi"], 16)
            for tt in range(LTT):
                for f in range(64):
                    Fi = tt * 64 + f
                    if Fi >= 2:
                        sy.wait_ge(S["p5a"], Fi - 1)
                    sy.dma_start(
                        out=w1b[f % 2][:], in_=w1bf.ap()[f],
                    ).then_inc(S["p5w10" if Fi % 2 == 0 else "p5w11"], 16)
                for ec in range(CH):
                    Ei = tt * CH + ec
                    if Ei >= 2:
                        sy.wait_ge(S["p5o"], Ei - 1)
                    sy.dma_start(
                        out=w2b[ec % 2][:], in_=w2bf.ap()[ec],
                    ).then_inc(S["p5w20" if Ei % 2 == 0 else "p5w21"], 16)
            if KPH < 6:
                return
            sy.wait_ge(S["p5fo0"], BP3 + 16 * LTT * CH // 2)
            sy.wait_ge(S["p5fo1"], BP3 + 16 * LTT * CH // 2)
            for tt in range(LTT):
                if tt >= 1:
                    sy.wait_ge(S["p6y"], tt * CH)
                sy.dma_start(
                    out=x6ab[:].rearrange("p (k t) -> p k t", k=CH),
                    in_=x1f_dram.rearrange("(k p) t -> p k t", p=128)
                        [:, :, tt * 512:(tt + 1) * 512],
                ).then_inc(S["p6x0" if tt % 2 == 0 else "p6x1"], 16)
            sy.wait_ge(S["p6o0"], 16 * LTT * CH // 2)
            sy.wait_ge(S["p6o1"], 16 * LTT * CH // 2)

        @block.gpsimd
        def _(g):
            g.dma_start(out=nwr_bf[:], in_=nw_row).then_inc(S["cqv"], 16)
            g.wait_ge(S["p1vk"], CH * TTN)
            for dst, src in ((wqb, wqT), (wkb, wkT), (wvb, wvT)):
                g.dma_start(
                    out=dst[:].rearrange("p (k d) -> p k d", k=CH),
                    in_=src.rearrange("(k p) d -> p k d", p=128),
                ).then_inc(S["cqv"], 16)
            g.dma_start(
                out=rw_s[:].rearrange("p (k e) -> p k e", k=CH),
                in_=rwT.rearrange("(k p) e -> p k e", p=128),
            ).then_inc(S["cqv"], 16)
            for i in range(4):
                g.dma_start(out=wobf.ap()[4 * i:4 * i + 4],
                            in_=woTt.ap()[4 * i:4 * i + 4]).then_inc(S["pcwo"], 16)
            for i in range(8):
                g.dma_start(out=w1bf.ap()[8 * i:8 * i + 8],
                            in_=w1t.ap()[8 * i:8 * i + 8]).then_inc(S["pcw1"], 16)
            for i in range(8):
                g.dma_start(out=w2bf.ap()[2 * i:2 * i + 2],
                            in_=w2t.ap()[2 * i:2 * i + 2]).then_inc(S["pcw2"], 16)
            if KDBG == "xn":
                g.wait_ge(S["p1o0"], 16 * TTN // 2)
                g.wait_ge(S["p1o1"], 16 * TTN // 2)
                g.dma_start(out=out[0:2048, :],
                            in_=xn_dram[:, 0:2048]).then_inc(S["dbg"], 16)
                g.wait_ge(S["dbg"], 16)
            if KPH < 2:
                return
            if KDBG == "qt" and KSUB == "qkv":
                g.wait_ge(S["p2e"], 3 * TTN * NH)
                g.dma_start(out=out[0:128, :],
                            in_=qT_s[:, 0:2048]).then_inc(S["dbg"], 16)
                g.dma_start(out=out[128:256, :],
                            in_=kT_s[:, 0:2048]).then_inc(S["dbg"], 16)
                g.dma_start(out=out[256:384, :],
                            in_=V_s[:, 0:2048]).then_inc(S["dbg"], 16)
                g.wait_ge(S["dbg"], 48)
            if KSUB == "qkv":
                return
            if KDBG == "ot":
                g.wait_ge(S["fnm"], Q2N)
                g.dma_start(out=out[0:128, :],
                            in_=OT_s[0][:, 0:2048]).then_inc(S["dbg"], 16)
                g.wait_ge(S["dbg"], 16)
            for h in range(NH):
                g.wait_ge(S["fnm"], Q2N * (h + 1))
                g.dma_start(out=ao_h[h].ap(),
                            in_=OT_s[0][:]).then_inc(S["p2ao"], 16)
                g.wait_ge(S["p2ao"], 16 * (h + 1))
                g.collective_compute(
                    "AllGather", ALU.bypass, replica_groups=G4,
                    ins=[ao_h[h].ap().opt()], outs=[aog_h[h].ap().opt()],
                ).then_inc(S["p2ag"])
            if KDBG == "p3":
                g.wait_ge(S["p3n"], LTT * CH)
                g.dma_start(out=out[0:128, :], in_=y_sb[:, 0:2048]).then_inc(S["dbg"], 16)
                g.dma_start(out=out[128:256, 0:512], in_=rb_sb[:]).then_inc(S["dbg"], 16)
                g.dma_start(out=out[256:384, 0:512], in_=mrb_sb[:]).then_inc(S["dbg"], 16)
                g.dma_start(out=out[384:512, 0:512], in_=ytmp[:]).then_inc(S["dbg"], 16)
                g.wait_ge(S["dbg"], 64)
            if KPH < 4:
                return
            g.wait_ge(S["p4rt"], CH)
            g.dma_start(out=rt_in.ap(), in_=runtot[CH % 2][:]).then_inc(S["p4ti"], 16)
            g.wait_ge(S["p4ti"], 16)
            g.collective_compute(
                "AllGather", ALU.bypass, replica_groups=G8,
                ins=[rt_in.ap().opt()], outs=[rt_out.ap().opt()],
            ).then_inc(S["p4ag"])
            g.wait_ge(S["p4ag"], 1)
            g.dma_start(out=rto_sb[:], in_=rt_out.ap()).then_inc(S["p4ro"], 16)
            g.wait_ge(S["p4gte"], 1)
            g.dma_start(out=g_dram.rearrange("o (p t) -> p o t", p=16),
                        in_=gt_sb[:].rearrange("p (o t) -> p o t", o=1)).then_inc(S["p4go"], 16)
            if KPH < 6:
                return
            for tt in range(LTT):
                for ec in range(CH):
                    w = tt * CH + ec
                    g.wait_ge(S["p6tv"], w + 1)
                    g.dma_start(
                        out=out.rearrange("(j p) e -> p j e", p=128)
                            [:, 4 * tt:4 * tt + 4, ec * 128:(ec + 1) * 128],
                        in_=xt2_sb[w % 2][:].rearrange("p (j c) -> p j c", j=4),
                    ).then_inc(S["p6o0" if w % 2 == 0 else "p6o1"], 16)
            g.wait_ge(S["p6o0"], 16 * LTT * CH // 2)
            g.wait_ge(S["p6o1"], 16 * LTT * CH // 2)

        @block.tensor
        def _(t):
            t.wait_ge(S["cst"], 16 * 9)
            t.wait_ge(S["dbg"], 1)
            t.wait_ge(S["cqv"], 16)
            for tt in range(TTN):
                if tt >= 1:
                    t.wait_ge(S["p1ms"], tt)
                for kc in range(CH):
                    W = CH * tt + kc
                    t.wait_ge(S["p1sq"], W + 1)
                    t.matmul(pC(0, p0=0, p1=1), ones_col_b[:], sqt2[kc % 2][:],
                             start=(kc == 0), stop=(kc == CH - 1),
                             skip_group_check=True).then_inc(S["p1ss"], 1)
                t.wait_ge(S["p1r"], tt + 1)
                for kc in range(CH):
                    W = CH * tt + kc
                    if W >= 2:
                        t.wait_ge(S["p1vk"], W - 1)
                    t.matmul(pA(kc % 2), nwr_bf[0:1, kc * 128:(kc + 1) * 128],
                             rstd_bf[0:1, :], start=True, stop=True,
                             skip_group_check=True).then_inc(S["p1wb"], 1)
            if KPH < 2:
                return
            t.wait_ge(S["p1vk"], CH * TTN)
            t.wait_ge(S["cqv"], 16 * 5)
            for h in range(NH):
                for tt in range(TTN):
                    m = TTN * h + tt
                    t.wait_ge(S["p2x0" if m % 2 == 0 else "p2x1"],
                              16 * (m // 2 + 1))
                    if h > 0 and tt == 0 and KSUB != "qkv":
                        t.wait_ge(S["fexp"], FT2 * h)
                    for pi, wsb in enumerate((wqb, wkb, wvb)):
                        if m >= 1:
                            t.wait_ge(S["p2e"], 3 * (m - 1) + pi + 1)
                        for kc in range(CH):
                            mm = t.matmul(
                                pA(pi),
                                wsb[:, kc * 512 + h * 128: kc * 512 + (h + 1) * 128],
                                xnb[m % 2][:, kc * 512:(kc + 1) * 512],
                                start=(kc == 0), stop=(kc == CH - 1),
                                skip_group_check=True)
                        mm.then_inc(S["p2mm"], 1)
                    t.wait_ge(S["p2e"], 3 * m + 3)
                    if m >= 1:
                        t.wait_ge(S["p2v"], m)
                    if h > 0 and tt == 0 and KSUB != "qkv":
                        t.wait_ge(S["flbc"], Q2N * h)
                    for j in range(4):
                        mm = t.transpose(pC(1, j * 128, (j + 1) * 128),
                                         vtmp[:, j * 128:(j + 1) * 128], id_f[:])
                    mm.then_inc(S["p2mm"], 1)
                    t.wait_ge(S["p2sq"], 2 * m + 1)
                    if m >= 1:
                        t.wait_ge(S["p2nm"], 2 * m)
                    t.matmul(pC(0, p0=0, p1=1), ones_col_b[:], sqh[:],
                             start=True, stop=True,
                             skip_group_check=True).then_inc(S["p2mm"], 1)
                    t.wait_ge(S["p2sq"], 2 * m + 2)
                    t.wait_ge(S["p2nm"], 2 * m + 1)
                    t.matmul(pC(0, p0=0, p1=1), ones_col_b[:], sqh[:],
                             start=True, stop=True,
                             skip_group_check=True).then_inc(S["p2mm"], 1)
                if KSUB == "qkv":
                    continue
                t.wait_ge(S["p2br"], h + 1)
                t.matmul(pC(0, 0, Q2N), ones_row_f[:], brow[0:1, 0:Q2N],
                         start=True, stop=True,
                         skip_group_check=True).then_inc(S["p2bb"], 1)
                g0 = FT2 * h
                t.wait_ge(S["p2e"], 3 * TTN * (h + 1))

                def emit_st(L):
                    qt2, kb2 = TILE2QK[L]
                    gg = g0 + L
                    if gg >= 2:
                        t.wait_ge(S["fexp"], gg - 1)
                    t.matmul(ps_a[gg % 2][:, 0:512],
                             kT_s[:, kb2 * 128:(kb2 + 1) * 128],
                             qT_s[:, qt2 * 1024:qt2 * 1024 + 512],
                             start=True, stop=True, skip_group_check=True)
                    t.matmul(ps_a[gg % 2][:, 512:1024],
                             kT_s[:, kb2 * 128:(kb2 + 1) * 128],
                             qT_s[:, qt2 * 1024 + 512:qt2 * 1024 + 1024],
                             start=True, stop=True,
                             skip_group_check=True).then_inc(S["fst"], 1)

                for L in range(2):
                    emit_st(L)
                for qt in range(Q2N):
                    nk = 8 * qt + 8
                    Q = Q2N * h + qt
                    for kb in range(nk):
                        L = lidx(qt, kb)
                        gg = g0 + L
                        if L + 2 < FT2:
                            emit_st(L + 2)
                        if kb == 0:
                            if Q >= 1:
                                t.wait_ge(S["fqd"], Q)
                                t.wait_ge(S["flbc"], Q)
                            t.wait_ge(S["p2b"], h + 1)
                            if qt == 0:
                                t.wait_ge(S["p2v"], TTN * (h + 1))
                        t.wait_ge(S["fexp"], gg + 1)
                        t.matmul(ps_b[:, 0:512], V_s[:, kb * 128:(kb + 1) * 128],
                                 pt[gg % 4][:, 0:512], start=(kb == 0),
                                 stop=(kb == nk - 1), skip_group_check=True)
                        t.matmul(ps_b[:, 512:1024],
                                 V_s[:, kb * 128:(kb + 1) * 128],
                                 pt[gg % 4][:, 512:1024], start=(kb == 0),
                                 stop=(kb == nk - 1),
                                 skip_group_check=True).then_inc(S["fpv"], 1)
                        if kb % 2 == 1:
                            PC = pcix(h, qt, kb)
                            t.wait_ge(S["fpp"], BFP + PC + 1)
                            t.matmul(pC(0, p0=0, p1=1), ones_col_b[:],
                                     pp_sb[PC % 2][:, 0:512],
                                     start=(kb == 1), stop=(kb == nk - 1),
                                     skip_group_check=True)
                            t.matmul(pC(1, p0=0, p1=1), ones_col_b[:],
                                     pp_sb[PC % 2][:, 512:1024],
                                     start=(kb == 1), stop=(kb == nk - 1),
                                     skip_group_check=True).then_inc(S["flp"], 1)
                    t.wait_ge(S["flv"], Q + 1)
                    t.matmul(pC(0), ones_row_f[:], linv_s[0:1, 0:512],
                             start=True, stop=True, skip_group_check=True)
                    t.matmul(pC(1), ones_row_f[:], linv_s[0:1, 512:1024],
                             start=True, stop=True,
                             skip_group_check=True).then_inc(S["flb"], 1)
            if KPH < 3:
                return
            t.wait_ge(S["pcwo"], 16 * 4)
            t.wait_ge(S["fexp"], FT2 * NH)
            t.wait_ge(S["flbc"], Q2N * NH)
            t.wait_ge(S["p2v"], TTN * NH)
            for tt in range(LTT):
                t.wait_ge(S["p3x0" if tt % 2 == 0 else "p3x1"],
                          16 * NH * (tt // 2 + 1))
                for ec in range(CH):
                    w = tt * CH + ec
                    t.wait_ge(S["p3w0" if w % 2 == 0 else "p3w1"],
                              16 * (w // 2 + 1))
                    if w >= 2:
                        t.wait_ge(S["p3e"], w - 1)
                    for dc in range(CH):
                        mm = t.matmul(pA(ec % 2),
                                      wob[ec % 2][:, dc * 128:(dc + 1) * 128],
                                      aob[tt % 2][:, dc * 512:(dc + 1) * 512],
                                      start=(dc == 0), stop=(dc == CH - 1),
                                      skip_group_check=True)
                    mm.then_inc(S["p3m"], 1)
                    if ec > 0:
                        t.wait_ge(S["p3s"], w)
                        t.matmul(pC(0, p0=0, p1=1), ones_col_b[:], ybf[:],
                                 start=(ec == 1), stop=False,
                                 skip_group_check=True)
                        t.matmul(pC(1, p0=0, p1=1), ones_col_b[:], sqy[:],
                                 start=(ec == 1), stop=False,
                                 skip_group_check=True).then_inc(S["p3st"], 1)
                t.wait_ge(S["p3s"], (tt + 1) * CH)
                t.matmul(pC(0, p0=0, p1=1), ones_col_b[:], ybf[:],
                         start=False, stop=True, skip_group_check=True)
                t.matmul(pC(1, p0=0, p1=1), ones_col_b[:], sqy[:],
                         start=False, stop=True,
                         skip_group_check=True).then_inc(S["p3st"], 1)
                t.wait_ge(S["p3mr"], tt + 1)
                if tt >= 1:
                    t.wait_ge(S["p3bce"], tt)
                t.matmul(pC(0), ones_row_f[:], rst2_sb[0:1, :],
                         start=True, stop=True, skip_group_check=True)
                t.matmul(pC(1), ones_row_f[:], mr_sb[0:1, :],
                         start=True, stop=True,
                         skip_group_check=True).then_inc(S["p3bc"], 1)
            if KPH < 4:
                return
            t.wait_ge(S["p3e"], LTT * CH)
            t.wait_ge(S["p3bce"], LTT)
            for tc in range(CH):
                if tc >= 2:
                    t.wait_ge(S["p4ex"], tc - 1)
                t.wait_ge(S["p3n"], (tc // 4 + 1) * CH)
                for hc in range(CH):
                    mm = t.matmul(pA(tc % 2, 0, E),
                                  x1T_s[:, hc * LTOK + tc * 128: hc * LTOK + (tc + 1) * 128],
                                  rw_s[:, hc * E:(hc + 1) * E],
                                  start=(hc == 0), stop=(hc == CH - 1),
                                  skip_group_check=True)
                mm.then_inc(S["p4m"], 1)
                t.wait_ge(S["p4oh"], tc + 1)
                if tc >= 1:
                    t.wait_ge(S["p4ce"], 2 * tc)
                t.wait_ge(S["p4rt"], tc)
                t.wait_ge(S["msk0"], 17)
                t.matmul(pA(2, 0, E), tri_f[:], ohf_sb[:, 16 * tc:16 * tc + 8],
                         start=True, stop=False, skip_group_check=True)
                t.matmul(pA(2, 0, E), ones_row_f[:],
                         runtot[tc % 2][0:1, 0:E],
                         start=False, stop=True,
                         skip_group_check=True).then_inc(S["p4cm"], 1)
                t.matmul(pA(3, 0, E), tri_f[:],
                         ohf_sb[:, 16 * tc + 8:16 * tc + 16],
                         start=True, stop=False, skip_group_check=True)
                t.matmul(pA(3, 0, E), ones_row_f[:],
                         runtot[tc % 2][0:1, 8:16],
                         start=False, stop=True,
                         skip_group_check=True).then_inc(S["p4cm"], 1)
                t.matmul(pC(0, 0, 16, 0, 1), ones_col_f[:],
                         ohf_sb[:, 16 * tc:16 * (tc + 1)],
                         start=True, stop=True,
                         skip_group_check=True).then_inc(S["p4cm"], 1)
            t.wait_ge(S["p4ro"], 16)
            t.wait_ge(S["p4rt"], CH)
            t.matmul(pC(0, 0, 16, 0, 1), cmask[:, 0:1], rto_sb[:],
                     start=True, stop=True, skip_group_check=True)
            t.matmul(pC(1, 0, 16, 0, 1), cmask[:, 1:2], rto_sb[:],
                     start=True, stop=True,
                     skip_group_check=True).then_inc(S["p4om"], 1)
            t.wait_ge(S["p4ov"], 1)
            t.matmul(pC(1, 0, 16), ones_row_f[:], offr_sb[0:1, :],
                     start=True, stop=True,
                     skip_group_check=True).then_inc(S["p4ob"], 1)
            t.wait_ge(S["p4g"], CH)
            t.wait_ge(S["p4obe"], 1)
            t.wait_ge(S["p4rt"], CH)
            t.transpose(pC(0, 0, 128, 0, 16), g_sb[:, 0:16],
                        id_f[:]).then_inc(S["p4gt"], 1)
            if KPH < 5:
                return
            t.wait_ge(S["pcw1"], 16 * 8)
            t.wait_ge(S["pcw2"], 16 * 8)
            t.wait_ge(S["p4ex"], CH)
            for tt in range(LTT):
                for f in range(64):
                    Fi = tt * 64 + f
                    t.wait_ge(S["p5w10" if Fi % 2 == 0 else "p5w11"],
                              BP3 + 16 * (Fi // 2 + 1))
                    if Fi == 2:
                        t.wait_ge(S["p4ce"], 2 * CH)
                    if Fi >= 4:
                        t.wait_ge(S["p5h"], Fi - 3)
                    for hc in range(CH):
                        mm = t.matmul(
                            pA(Fi % 4),
                            w1b[f % 2][:, hc * 128:(hc + 1) * 128],
                            x1T_s[:, hc * LTOK + tt * 512: hc * LTOK + (tt + 1) * 512],
                            start=(hc == 0), stop=(hc == CH - 1),
                            skip_group_check=True)
                    mm.then_inc(S["p5a"], 1)
                t.wait_ge(S["p4gi"], 16)
                if tt >= 1:
                    t.wait_ge(S["p5e"], tt * CH)
                else:
                    t.wait_ge(S["p4gte"], 1)
                t.matmul(pC(0), ones_row_f[:],
                         grow[0:1, tt * 512:(tt + 1) * 512],
                         start=True, stop=True,
                         skip_group_check=True).then_inc(S["p5gm"], 1)
                for ec in range(CH):
                    Ei = tt * CH + ec
                    t.wait_ge(S["p5w20" if Ei % 2 == 0 else "p5w21"],
                              BP3 + 16 * (Ei // 2 + 1))
                    t.wait_ge(S["p5h"], 64 * (tt + 1))
                    if Ei >= 1:
                        t.wait_ge(S["p5ev"], Ei)
                    for fc in range(64):
                        mm = t.matmul(
                            ps_b[:, 0:512],
                            w2b[ec % 2][:, fc * 128:(fc + 1) * 128],
                            hT_s[:, fc * 512:(fc + 1) * 512],
                            start=(fc == 0), stop=(fc == 63),
                            skip_group_check=True)
                    mm.then_inc(S["p5o"], 1)
            if KPH < 6:
                return
            t.wait_ge(S["p5gb"], LTT)
            t.wait_ge(S["p4gte"], 1)
            for tt in range(LTT):
                for ec in range(CH):
                    w = tt * CH + ec
                    t.wait_ge(S["p6s"], w + 1)
                    if ec == 0 and tt >= 1:
                        t.wait_ge(S["p6bce"], tt)
                        t.wait_ge(S["p6tv"], tt * CH)
                    t.matmul(pC(0, p0=0, p1=1), ones_col_b[:], ybf[:],
                             start=(ec == 0), stop=(ec == CH - 1),
                             skip_group_check=True)
                    t.matmul(pC(1, p0=0, p1=1), ones_col_b[:], sqy[:],
                             start=(ec == 0), stop=(ec == CH - 1),
                             skip_group_check=True).then_inc(S["p6st"], 1)
                t.wait_ge(S["p6mr"], tt + 1)
                t.matmul(pC(0), ones_row_f[:], rst2_sb[0:1, :],
                         start=True, stop=True, skip_group_check=True)
                t.matmul(pC(1), ones_row_f[:], mr_sb[0:1, :],
                         start=True, stop=True,
                         skip_group_check=True).then_inc(S["p6bc"], 1)
                for ec in range(CH):
                    w = tt * CH + ec
                    t.wait_ge(S["p6n"], w + 1)
                    if w >= 2:
                        t.wait_ge(S["p6tv"], w - 1)
                    for j in range(4):
                        mm = t.transpose(ps_b[:, j * 128:(j + 1) * 128],
                                         x2_sb[:, j * 128:(j + 1) * 128], id_f[:])
                    mm.then_inc(S["p6tr"], 1)

        @block.scalar
        def _(s):
            sb = Bar(s, S["sbr"])
            s.dma_start(out=xb[1][:], in_=xTt.ap()[1]).then_inc(S["p1x1"], 16)
            for tt in range(TTN):
                s.wait_ge(S["p1x0" if tt % 2 == 0 else "p1x1"],
                          16 * (tt // 2 + 1))
                for kc in range(CH):
                    W = CH * tt + kc
                    if W >= 2:
                        s.wait_ge(S["p1ss"], W - 1)
                    s.activation(sqt2[kc % 2][:],
                                 xb[tt % 2][:, kc * 512:(kc + 1) * 512],
                                 AF.Square).then_inc(S["p1sq"], 1)
                s.wait_ge(S["p1ms"], tt + 1)
                if tt >= 1:
                    s.wait_ge(S["p1r"], tt)
                s.activation(sm_sb[:], ms_sb[:], AF.Sqrt).then_inc(S["p1sm"], 1)
                s.wait_ge(S["p1vk"], 16 * (tt + 1))
                s.dma_start(
                    out=xn_dram.ap()[tt],
                    in_=xnb[tt % 2][:],
                ).then_inc(S["p1o0" if tt % 2 == 0 else "p1o1"], 16)
                if tt % 2 == 1 and tt + 2 < TTN:
                    s.dma_start(out=xb[1][:],
                                in_=xTt.ap()[tt + 2]).then_inc(S["p1x1"], 16)
            if KPH < 2:
                return
            for h in range(NH):
                for tt in range(TTN):
                    m = TTN * h + tt
                    for pi in range(3):
                        s.wait_ge(S["p2mm"], 6 * m + pi + 1)
                        bias = bqkv_s[:, pi * 4 + h: pi * 4 + h + 1]
                        if pi == 0:
                            s.activation(qT_s[:, tt * 512:(tt + 1) * 512],
                                         pA(0), AF.Identity,
                                         bias=bias).then_inc(S["p2e"], 1)
                        elif pi == 1:
                            s.activation(kT_s[:, tt * 512:(tt + 1) * 512],
                                         pA(1), AF.Identity,
                                         bias=bias).then_inc(S["p2e"], 1)
                        else:
                            s.activation(vtmp[:], pA(2), AF.Identity,
                                         bias=bias).then_inc(S["p2e"], 1)
                if KSUB == "qkv":
                    continue
                s.wait_ge(S["p2pm"], h + 1)
                sb.mark(s.activation(bbl[0:1, 0:Q2N], bbv[0:1, 0:Q2N], AF.Ln))
                sb.wait()
                s.activation(bbe[0:1, 0:Q2N], bbl[0:1, 0:Q2N], AF.Exp,
                             scale=0.5).then_inc(S["p2l2"], 1)
                s.wait_ge(S["p2bb"], h + 1)
                s.activation(biasq[:], pC(0, 0, Q2N), AF.Copy).then_inc(S["p2b"], 1)
                s.wait_ge(S["p2b"], h + 1)
                g0 = FT2 * h
                for qt in range(Q2N):
                    nk = 8 * qt + 8
                    Q = Q2N * h + qt
                    for kb in range(nk):
                        L = lidx(qt, kb)
                        gg = g0 + L
                        diag = kb >= 8 * qt
                        if diag:
                            s.wait_ge(S["fmsk"], dcnt(h, qt, kb) + 1)
                        else:
                            s.wait_ge(S["fst"], gg + 1)
                        if gg >= 4:
                            s.wait_ge(S["fpv"], gg - 3)
                            gp = gg - 4
                            hp = gp // FT2
                            qtp, kbp = TILE2QK[gp % FT2]
                            s.wait_ge(S["fpp"], BFP + pcix(hp, qtp, kbp) + 1)
                        src = stile[dcnt(h, qt, kb) % 2][:] if diag else ps_a[gg % 2][:]
                        s.activation(pt[gg % 4][:], src, AF.Exp,
                                     bias=biasq[:, qt:qt + 1],
                                     scale=SCALE).then_inc(S["fexp"], 1)
                    s.wait_ge(S["flb"], Q + 1)
                    s.activation(lb_s[:], ps_c[:], AF.Copy).then_inc(S["flbc"], 1)
            if KPH < 3:
                return
            for tt in range(LTT):
                for ec in range(CH):
                    w = tt * CH + ec
                    s.wait_ge(S["p3m"], w + 1)
                    if w >= 1:
                        s.wait_ge(S["p3y"], w)
                    s.activation(ytmp[:], pA(ec % 2), AF.Identity,
                                 bias=bo_s[:, ec:ec + 1]).then_inc(S["p3e"], 1)
                s.wait_ge(S["p3lnb"], tt + 1)
                sb.mark(s.activation(lvar_sb[:], var_sb[:], AF.Ln))
                sb.wait()
                s.activation(rst2_sb[:], lvar_sb[:], AF.Exp,
                             scale=-0.5).then_inc(S["p3ln"], 1)
                s.wait_ge(S["p3bc"], tt + 1)
                s.activation(rb_sb[:], pC(0), AF.Copy)
                s.activation(mrb_sb[:], pC(1), AF.Copy).then_inc(S["p3bce"], 1)
                for ec in range(CH):
                    w = tt * CH + ec
                    s.wait_ge(S["p3n"], w + 1)
                    s.dma_start(
                        out=x1f_dram.rearrange("(k p) t -> k p t", p=128)
                            [ec, :, tt * 512:(tt + 1) * 512],
                        in_=x1g_t[:]).then_inc(S["p3xo0" if w % 2 == 0 else "p3xo1"], 16)
            if KPH < 4:
                return
            for tc in range(CH):
                s.wait_ge(S["p4sf"], tc + 1)
                s.activation(pe_sb[:], pA(tc % 2, 0, E), AF.Exp,
                             bias=negmax[:]).then_inc(S["p4ex"], 1)
            if KPH < 5:
                return
            s.wait_ge(S["p3xo0"], 16 * 32)
            s.wait_ge(S["p3xo1"], 16 * 32)
            for tt in range(LTT):
                for f in range(64):
                    Fi = tt * 64 + f
                    s.wait_ge(S["p5a"], Fi + 1)
                    if tt >= 1 and f < 2:
                        s.wait_ge(S["p5o"], tt * CH)
                    s.activation(hT_s[:, f * 512:(f + 1) * 512], pA(Fi % 4),
                                 AF.Gelu,
                                 bias=b1_s[:, f:f + 1]).then_inc(S["p5h"], 1)
                s.wait_ge(S["p5gm"], tt + 1)
                s.activation(gbc_sb[:], pC(0), AF.Copy).then_inc(S["p5gb"], 1)
                for ec in range(CH):
                    Ei = tt * CH + ec
                    s.wait_ge(S["p5o"], Ei + 1)
                    if Ei >= 2:
                        s.wait_ge(S["p5e"], Ei - 1)
                    s.activation(fev[Ei % 2][:], ps_b[:, 0:512], AF.Identity,
                                 bias=b2_s[:, ec:ec + 1]).then_inc(S["p5ev"], 1)
                    s.wait_ge(S["p5e"], Ei + 1)
                    s.dma_start(
                        out=ffn_dram.rearrange("(k p) t -> k p t", p=128)
                            [ec, :, tt * 512:(tt + 1) * 512],
                        in_=ffb[Ei % 2][:]).then_inc(S["p5fo0" if Ei % 2 == 0 else "p5fo1"], 16)
            if KPH < 6:
                return
            for tt in range(LTT):
                if tt >= 1:
                    s.wait_ge(S["p6y"], tt * CH)
                s.dma_start(
                    out=x6bb[:].rearrange("p (k t) -> p k t", k=CH),
                    in_=ffn_dram.rearrange("(k p) t -> p k t", p=128)
                        [:, :, tt * 512:(tt + 1) * 512],
                ).then_inc(S["p6f0" if tt % 2 == 0 else "p6f1"], 16)
                s.wait_ge(S["p6lnb"], tt + 1)
                sb.mark(s.activation(lvar_sb[:], var_sb[:], AF.Ln))
                sb.wait()
                s.activation(rst2_sb[:], lvar_sb[:], AF.Exp,
                             scale=-0.5).then_inc(S["p6ln"], 1)
                s.wait_ge(S["p6bc"], tt + 1)
                s.activation(rb_sb[:], pC(0), AF.Copy)
                s.activation(mrb_sb[:], pC(1), AF.Copy).then_inc(S["p6bce"], 1)

        @block.vector
        def _(v):
            vb = Bar(v, S["vbr"])
            v.memset(ones_col_b[:], 1.0)
            v.memset(ones_col_f[:], 1.0)
            v.memset(ones_row_f[:], 1.0).then_inc(S["dbg"], 1)
            for tt in range(TTN):
                v.wait_ge(S["p1ss"], 16 * (tt + 1))
                if tt >= 1:
                    v.wait_ge(S["p1sm"], tt)
                v.tensor_scalar(ms_sb[:], pC(0, p0=0, p1=1), 1.0 / D, RMS_EPS,
                                ALU.mult, ALU.add).then_inc(S["p1ms"], 1)
                v.wait_ge(S["p1sm"], tt + 1)
                vb.mark(v.reciprocal(rstd_sb[:], sm_sb[:]))
                vb.wait()
                v.tensor_copy(rstd_bf[:], rstd_sb[:]).then_inc(S["p1r"], 1)
                if tt >= 2:
                    v.wait_ge(S["p1o0" if tt % 2 == 0 else "p1o1"],
                              16 * ((tt - 2) // 2 + 1))
                for kc in range(CH):
                    W = CH * tt + kc
                    v.wait_ge(S["p1wb"], W + 1)
                    v.tensor_mul(xnb[tt % 2][:, kc * 512:(kc + 1) * 512],
                                 xb[tt % 2][:, kc * 512:(kc + 1) * 512],
                                 pA(kc % 2)).then_inc(S["p1vk"], 1)
            if KPH < 2:
                return
            for h in range(NH):
                for tt in range(TTN):
                    m = TTN * h + tt
                    v.wait_ge(S["p2e"], 3 * m + 1)
                    if m >= 1:
                        v.wait_ge(S["p2mm"], 6 * m)
                    v.tensor_mul(sqh[:], qT_s[:, tt * 512:(tt + 1) * 512],
                                 qT_s[:, tt * 512:(tt + 1) * 512]).then_inc(S["p2sq"], 1)
                    v.wait_ge(S["p2mm"], 6 * m + 5)
                    v.reduce_max(qn2m[0:1, tt:tt + 1], pC(0, p0=0, p1=1),
                                 AX.X).then_inc(S["p2nm"], 1)
                    v.wait_ge(S["p2e"], 3 * m + 2)
                    v.tensor_mul(sqh[:], kT_s[:, tt * 512:(tt + 1) * 512],
                                 kT_s[:, tt * 512:(tt + 1) * 512]).then_inc(S["p2sq"], 1)
                    v.wait_ge(S["p2mm"], 6 * m + 6)
                    v.reduce_max(kn2m[0:1, tt:tt + 1], pC(0, p0=0, p1=1),
                                 AX.X).then_inc(S["p2nm"], 1)
                    v.wait_ge(S["p2mm"], 6 * m + 4)
                    v.tensor_copy(V_s[:, tt * 512:(tt + 1) * 512],
                                  pC(1)).then_inc(S["p2v"], 1)
                if KSUB == "qkv":
                    continue
                v.wait_ge(S["p2nm"], 2 * TTN * (h + 1))
                for qt in range(Q2N):
                    vb.mark(v.reduce_max(kprefm[0:1, qt:qt + 1],
                                         kn2m[0:1, 0:2 * qt + 2], AX.X))
                    vb.mark(v.reduce_max(q2m[0:1, qt:qt + 1],
                                         qn2m[0:1, 2 * qt:2 * qt + 2], AX.X))
                vb.wait()
                v.tensor_mul(bbv[0:1, 0:Q2N], q2m[:],
                             kprefm[0:1, 0:Q2N]).then_inc(S["p2pm"], 1)
                v.wait_ge(S["p2l2"], h + 1)
                v.tensor_scalar(brow[0:1, 0:Q2N], bbe[0:1, 0:Q2N],
                                -SCALE * 1.001, None,
                                ALU.mult).then_inc(S["p2br"], 1)
                if h == 0:
                    v.wait_ge(S["msk0"], 16)
                g0 = FT2 * h
                nmc = Q2N * h
                for qt in range(Q2N):
                    nk = 8 * qt + 8
                    Q = Q2N * h + qt
                    for kb in range(nk):
                        gg = g0 + lidx(qt, kb)
                        if kb >= 8 * qt:
                            dd = dcnt(h, qt, kb)
                            v.wait_ge(S["fst"], gg + 1)
                            if dd >= 2:
                                dprev = dd - 2
                                hh = dprev // 64
                                Lp, _ = DIAG_OF_HEAD[dprev % 64]
                                v.wait_ge(S["fexp"], FT2 * hh + Lp + 1)
                            jj = kb - 8 * qt
                            v.tensor_add(stile[dd % 2][:], ps_a[gg % 2][:],
                                         masks[:, 1024 - 128 * jj:2048 - 128 * jj]).then_inc(S["fmsk"], 1)
                        if kb % 2 == 1:
                            PC = pcix(h, qt, kb)
                            v.wait_ge(S["fexp"], gg + 1)
                            if PC >= 2:
                                v.wait_ge(S["flp"], BFP + PC - 1)
                            v.tensor_add(pp_sb[PC % 2][:], pt[(gg - 1) % 4][:],
                                         pt[gg % 4][:]).then_inc(S["fpp"], 1)
                    PCe = 144 * h + 2 * (qt + 1) * (qt + 2)
                    if qt >= 1:
                        v.wait_ge(S["flbc"], Q)
                        if qt == 1 and h >= 1:
                            v.wait_ge(S["p2ao"], 16 * h)
                        v.tensor_mul(OT_s[0][:, (qt - 1) * 1024:qt * 1024],
                                     praw[:], lb_s[:]).then_inc(S["fnm"], 1)
                        nmc += 1
                    v.wait_ge(S["flp"], BFP + PCe)
                    v.wait_ge(S["fnm"], nmc)
                    v.tensor_copy(praw[:], ps_b[:]).then_inc(S["fqd"], 1)
                    if Q >= 1:
                        v.wait_ge(S["flb"], Q)
                    v.reciprocal(linv_s[:], ps_c[0:1, 0:1024]).then_inc(S["flv"], 1)
                v.wait_ge(S["flbc"], Q2N * (h + 1))
                v.tensor_mul(OT_s[0][:, (Q2N - 1) * 1024:Q2N * 1024],
                             praw[:], lb_s[:]).then_inc(S["fnm"], 1)
            if KPH < 3:
                return
            for tt in range(LTT):
                for ec in range(CH):
                    w = tt * CH + ec
                    v.wait_ge(S["p3e"], w + 1)
                    v.wait_ge(S["p3xr0" if w % 2 == 0 else "p3xr1"],
                              16 * (w // 2 + 1))
                    if w >= 1:
                        v.wait_ge(S["p3st"], w)
                    v.tensor_add(y_sb[:, ec * 512:(ec + 1) * 512], ytmp[:],
                                 xrb[w % 2][:]).then_inc(S["p3y"], 1)
                    v.wait_ge(S["p3y"], w + 1)
                    v.tensor_copy(ybf[:], y_sb[:, ec * 512:(ec + 1) * 512])
                    v.tensor_mul(sqy[:], y_sb[:, ec * 512:(ec + 1) * 512],
                                 y_sb[:, ec * 512:(ec + 1) * 512]).then_inc(S["p3s"], 1)
                v.wait_ge(S["p3st"], (tt + 1) * CH)
                vb.mark(v.tensor_scalar(me_sb[:], pC(0, p0=0, p1=1), 1.0 / D,
                                        None, ALU.mult))
                vb.mark(v.tensor_scalar(v1_sb[:], pC(1, p0=0, p1=1), 1.0 / D,
                                        None, ALU.mult))
                vb.wait()
                vb.mark(v.tensor_mul(m2_sb[:], me_sb[:], me_sb[:]))
                vb.wait()
                vb.mark(v.tensor_sub(ms_sb[:], v1_sb[:], m2_sb[:]))
                vb.wait()
                v.tensor_scalar(var_sb[:], ms_sb[:], LN_EPS, None,
                                ALU.add).then_inc(S["p3lnb"], 1)
                v.wait_ge(S["p3ln"], tt + 1)
                v.tensor_mul(mr_sb[:], me_sb[:], rst2_sb[:]).then_inc(S["p3mr"], 1)
                for ec in range(CH):
                    w = tt * CH + ec
                    v.wait_ge(S["p3bce"], tt + 1)
                    if w >= 1:
                        v.wait_ge(S["p3xo0" if (w - 1) % 2 == 0 else "p3xo1"],
                                  16 * ((w - 1) // 2 + 1))
                    vb.mark(v.tensor_mul(x1f_t[:],
                                         y_sb[:, ec * 512:(ec + 1) * 512],
                                         rb_sb[:]))
                    vb.wait()
                    vb.mark(v.tensor_sub(x1g_t[:], x1f_t[:], mrb_sb[:]))
                    vb.wait()
                    v.tensor_copy(x1T_s[:, ec * LTOK + tt * 512: ec * LTOK + (tt + 1) * 512],
                                  x1g_t[:]).then_inc(S["p3n"], 1)
            if KPH < 4:
                return
            v.memset(runtot[0][:], 0.0)
            v.memset(runtot[1][:], 0.0).then_inc(S["msk0"], 1)
            for tc in range(CH):
                v.wait_ge(S["p4m"], tc + 1)
                vb.mark(v.reduce_max(nmx[:], pA(tc % 2, 0, E), AX.X))
                vb.wait()
                v.tensor_scalar(negmax[:], nmx[:], -1.0, None,
                                ALU.mult).then_inc(S["p4sf"], 1)
                v.wait_ge(S["p4ex"], tc + 1)
                vb.mark(v.reduce_sum(ssum[:], pe_sb[:], AX.X))
                vb.wait()
                vb.mark(v.reciprocal(si_sb[:], ssum[:]))
                vb.wait()
                vb.mark(v.tensor_scalar(probs[:], pe_sb[:], si_sb[:], None,
                                        ALU.mult))
                vb.wait()
                vb.mark(v.reduce_max(pv_sb[:, 2 * tc:2 * tc + 1], probs[:], AX.X))
                vb.wait()
                vb.mark(v.tensor_scalar(eqt[:], probs[:],
                                        pv_sb[:, 2 * tc:2 * tc + 1],
                                        None, ALU.is_equal))
                vb.wait()
                vb.mark(v.tensor_mul(at_sb[:], eqt[:], iota8[:]))
                vb.wait()
                vb.mark(v.scalar_tensor_tensor(t1_sb[:], eqt[:], -99.0,
                                               at_sb[:], ALU.mult, ALU.add))
                vb.wait()
                vb.mark(v.tensor_scalar(idxf_sb[:], t1_sb[:], 99.0, None,
                                        ALU.add))
                vb.wait()
                vb.mark(v.tensor_reduce(idx_sb[:], idxf_sb[:], AX.X, ALU.min))
                vb.wait()
                vb.mark(v.tensor_scalar(ohf_sb[:, 16 * tc:16 * tc + 8],
                                        iota8[:], idx_sb[:], None,
                                        ALU.is_equal))
                vb.wait()
                vb.mark(v.scalar_tensor_tensor(maskd[:],
                                               ohf_sb[:, 16 * tc:16 * tc + 8],
                                               -2.0, probs[:], ALU.mult,
                                               ALU.add))
                vb.wait()
                vb.mark(v.reduce_max(pv_sb[:, 2 * tc + 1:2 * tc + 2], maskd[:],
                                     AX.X))
                vb.wait()
                vb.mark(v.tensor_scalar(eqt[:], maskd[:],
                                        pv_sb[:, 2 * tc + 1:2 * tc + 2], None,
                                        ALU.is_equal))
                vb.wait()
                vb.mark(v.tensor_mul(at_sb[:], eqt[:], iota8[:]))
                vb.wait()
                vb.mark(v.scalar_tensor_tensor(t1_sb[:], eqt[:], -99.0,
                                               at_sb[:], ALU.mult, ALU.add))
                vb.wait()
                vb.mark(v.tensor_scalar(idxf_sb[:], t1_sb[:], 99.0, None,
                                        ALU.add))
                vb.wait()
                vb.mark(v.tensor_reduce(idx_sb[:], idxf_sb[:], AX.X, ALU.min))
                vb.wait()
                v.tensor_scalar(ohf_sb[:, 16 * tc + 8:16 * tc + 16], iota8[:],
                                idx_sb[:], None,
                                ALU.is_equal).then_inc(S["p4oh"], 1)
                v.wait_ge(S["p4cm"], 3 * tc + 2)
                v.tensor_copy(c_sb[:, 16 * tc:16 * tc + 8], pA(2, 0, E))
                v.tensor_copy(c_sb[:, 16 * tc + 8:16 * (tc + 1)],
                              pA(3, 0, E)).then_inc(S["p4ce"], 2)
                v.wait_ge(S["p4cm"], 3 * tc + 3)
                v.wait_ge(S["p4rt"], tc)
                v.tensor_add(runtot[(tc + 1) % 2][:], runtot[tc % 2][:],
                             pC(0, 0, 16, 0, 1)).then_inc(S["p4rt"], 1)
            v.wait_ge(S["p4om"], 1)
            vb.mark(v.tensor_copy(offr_sb[:], pC(0, 0, 16, 0, 1)))
            vb.wait()
            v.tensor_add(offr_sb[0:1, 8:16], offr_sb[0:1, 8:16],
                         pC(1, 0, 8, 0, 1)).then_inc(S["p4ov"], 1)
            v.wait_ge(S["p4ob"], 1)
            v.tensor_copy(offb_sb[:], pC(1, 0, 16)).then_inc(S["p4obe"], 1)
            v.wait_ge(S["p4obe"], 1)
            v.wait_ge(S["p4ce"], 2 * CH)
            v.wait_ge(S["p4oh"], CH)
            for tc in range(CH):
                vb.mark(v.tensor_add(glob_sb[:], c_sb[:, 16 * tc:16 * (tc + 1)],
                                     offb_sb[:]))
                vb.wait()
                vb.mark(v.tensor_scalar(keep_sb[:], glob_sb[:], 512.5, None,
                                        ALU.is_le))
                vb.wait()
                vb.mark(v.tensor_mul(sel_sb[:], keep_sb[:],
                                     ohf_sb[:, 16 * tc:16 * (tc + 1)]))
                vb.wait()
                vb.mark(v.reduce_sum(g1_sb[:], sel_sb[:, 0:8], AX.X))
                vb.mark(v.reduce_sum(g2_sb[:], sel_sb[:, 8:16], AX.X))
                vb.wait()
                vb.mark(v.tensor_mul(ga_sb[:], g1_sb[:],
                                     pv_sb[:, 2 * tc:2 * tc + 1]))
                vb.mark(v.tensor_mul(gb2_sb[:], g2_sb[:],
                                     pv_sb[:, 2 * tc + 1:2 * tc + 2]))
                vb.wait()
                v.tensor_add(g_sb[:, tc:tc + 1], ga_sb[:],
                             gb2_sb[:]).then_inc(S["p4g"], 1)
            v.wait_ge(S["p4gt"], 1)
            v.tensor_copy(gt_sb[:], pC(0, 0, 128, 0, 16)).then_inc(S["p4gte"], 1)
            if KPH < 5:
                return
            for tt in range(LTT):
                for ec in range(CH):
                    Ei = tt * CH + ec
                    v.wait_ge(S["p5ev"], Ei + 1)
                    v.wait_ge(S["p5gb"], tt + 1)
                    if Ei >= 2:
                        v.wait_ge(S["p5fo0" if Ei % 2 == 0 else "p5fo1"],
                                  BP3 + 16 * ((Ei - 2) // 2 + 1))
                    v.tensor_mul(ffb[Ei % 2][:], fev[Ei % 2][:],
                                 gbc_sb[:]).then_inc(S["p5e"], 1)
            if KPH < 6:
                return
            v.wait_ge(S["p5o"], LTT * CH)
            for tt in range(LTT):
                v.wait_ge(S["p6x0" if tt % 2 == 0 else "p6x1"],
                          BP1X + 16 * (tt // 2 + 1))
                v.wait_ge(S["p6f0" if tt % 2 == 0 else "p6f1"],
                          BP1O + 16 * (tt // 2 + 1))
                for ec in range(CH):
                    w = tt * CH + ec
                    if w >= 1:
                        v.wait_ge(S["p6st"], w)
                    v.tensor_add(y2_sb[:, ec * 512:(ec + 1) * 512],
                                 x6ab[:, ec * 512:(ec + 1) * 512],
                                 x6bb[:, ec * 512:(ec + 1) * 512]).then_inc(S["p6y"], 1)
                    v.wait_ge(S["p6y"], w + 1)
                    v.tensor_copy(ybf[:], y2_sb[:, ec * 512:(ec + 1) * 512])
                    v.tensor_mul(sqy[:], y2_sb[:, ec * 512:(ec + 1) * 512],
                                 y2_sb[:, ec * 512:(ec + 1) * 512]).then_inc(S["p6s"], 1)
                v.wait_ge(S["p6st"], (tt + 1) * CH)
                vb.mark(v.tensor_scalar(me_sb[:], pC(0, p0=0, p1=1), 1.0 / D,
                                        None, ALU.mult))
                vb.mark(v.tensor_scalar(v1_sb[:], pC(1, p0=0, p1=1), 1.0 / D, None,
                                        ALU.mult))
                vb.wait()
                vb.mark(v.tensor_mul(m2_sb[:], me_sb[:], me_sb[:]))
                vb.wait()
                vb.mark(v.tensor_sub(ms_sb[:], v1_sb[:], m2_sb[:]))
                vb.wait()
                v.tensor_scalar(var_sb[:], ms_sb[:], LN_EPS, None,
                                ALU.add).then_inc(S["p6lnb"], 1)
                v.wait_ge(S["p6ln"], tt + 1)
                v.tensor_mul(mr_sb[:], me_sb[:], rst2_sb[:]).then_inc(S["p6mr"], 1)
                for ec in range(CH):
                    w = tt * CH + ec
                    v.wait_ge(S["p6bce"], tt + 1)
                    vb.mark(v.tensor_mul(x1f_t[:],
                                         y2_sb[:, ec * 512:(ec + 1) * 512],
                                         rb_sb[:]))
                    vb.wait()
                    v.tensor_sub(x2_sb[:], x1f_t[:], mrb_sb[:]).then_inc(S["p6n"], 1)
                    v.wait_ge(S["p6tr"], w + 1)
                    if w >= 2:
                        v.wait_ge(S["p6o0" if w % 2 == 0 else "p6o1"],
                                  16 * ((w - 2) // 2 + 1))
                    v.tensor_copy(xt2_sb[w % 2][:],
                                  ps_b[:, 0:512]).then_inc(S["p6tv"], 1)
            v.wait_ge(S["p6o0"], 16 * LTT * CH // 2)
            v.wait_ge(S["p6o1"], 16 * LTT * CH // 2)

    return nc


_CACHE = {}


def kernel(L0_attn_x, L0_attn_norm_w, L0_attn_wq, L0_attn_bq, L0_attn_wk,
           L0_attn_bk, L0_attn_wv, L0_attn_bv, L0_attn_wo, L0_attn_bo,
           L0_ffn_router_w, L0_ffn_w1, L0_ffn_b1, L0_ffn_w2, L0_ffn_b2):
    x = np.ascontiguousarray(np.asarray(L0_attn_x, np.float32))
    norm_w = np.asarray(L0_attn_norm_w, np.float32)
    wq = np.asarray(L0_attn_wq, np.float32)
    bq = np.asarray(L0_attn_bq, np.float32)
    wk = np.asarray(L0_attn_wk, np.float32)
    bk = np.asarray(L0_attn_bk, np.float32)
    wv = np.asarray(L0_attn_wv, np.float32)
    bv = np.asarray(L0_attn_bv, np.float32)
    wo = np.asarray(L0_attn_wo, np.float32)
    bo = np.asarray(L0_attn_bo, np.float32)
    router_w = np.asarray(L0_ffn_router_w, np.float32)
    w1 = np.asarray(L0_ffn_w1, np.float32)
    b1 = np.asarray(L0_ffn_b1, np.float32)
    w2 = np.asarray(L0_ffn_w2, np.float32)
    b2 = np.asarray(L0_ffn_b2, np.float32)

    if "nc" not in _CACHE:
        _CACHE["nc"] = build_program()
    nc = _CACHE["nc"]

    woTt = np.ascontiguousarray(
        wo.T.reshape(16, 128, 16, 128).transpose(2, 1, 0, 3)
        .reshape(16, 128, CH * 128))
    w1t = np.ascontiguousarray(
        w1.T.reshape(16, 128, 64, 128).transpose(2, 1, 0, 3)
        .reshape(64, 128, CH * 128))
    w2t = np.ascontiguousarray(
        w2.T.reshape(64, 128, 16, 128).transpose(2, 1, 0, 3)
        .reshape(16, 128, 64 * 128))
    bo_cols = np.ascontiguousarray(bo.reshape(16, 128).T)
    b1_cols = np.ascontiguousarray(b1.reshape(64, 128).T)
    b2_cols = np.ascontiguousarray(b2.reshape(16, 128).T)
    nw_r = np.ascontiguousarray(norm_w.reshape(1, D))
    rwTc = np.ascontiguousarray(router_w.T)
    tri = np.triu(np.ones((128, 128), np.float32))
    _k = np.arange(128)[:, None]
    _u = np.arange(2048)[None, :]
    msk = np.where(_u >= _k + 1024, 0.0, -30000.0).astype(np.float32)
    iota = np.ascontiguousarray(
        np.tile(np.arange(E, dtype=np.float32), (128, 1)))
    ident = np.eye(128, dtype=np.float32)

    in_maps = []
    for c in range(8):
        b, hgr = c // 4, c % 4
        heads = [4 * j + hgr for j in range(NH)]
        xTb = np.ascontiguousarray(x[b].T)
        xTt_b = np.ascontiguousarray(
            xTb.reshape(16, 128, TTN, 512).transpose(2, 1, 0, 3)
            .reshape(TTN, 128, CH * 512))

        def _slice(w):
            rows = np.concatenate([w[h * 128:(h + 1) * 128] for h in heads], 0)
            return np.ascontiguousarray(rows.T)

        bq_c = np.stack([bq[h * 128:(h + 1) * 128] for h in heads], 1)
        bk_c = np.stack([bk[h * 128:(h + 1) * 128] for h in heads], 1)
        bv_c = np.stack([bv[h * 128:(h + 1) * 128] for h in heads], 1)
        bqkv_c = np.ascontiguousarray(
            np.concatenate([bq_c, bk_c, bv_c], 1))
        cm = np.zeros((8, 2), np.float32)
        cm[:, 0] = (np.arange(8) < c).astype(np.float32)
        cm[:, 1] = 1.0
        in_maps.append(dict(
            xTt=xTt_b, xTr=np.ascontiguousarray(xTb[:, hgr * LTOK:(hgr + 1) * LTOK]),
            wqT=_slice(wq), wkT=_slice(wk), wvT=_slice(wv),
            bqkv=bqkv_c, woTt=woTt, bo_c=bo_cols, nw_row=nw_r, rwT=rwTc,
            w1t=w1t, b1_c=b1_cols, w2t=w2t, b2_c=b2_cols, tri_in=tri,
            masks_in=msk, iota_in=iota, cmask_in=cm, id_in=ident))

    res = run_bass_kernel_spmd(nc, in_maps, core_ids=list(range(8)))
    y = np.empty((2, T, D), np.float32)
    for c in range(8):
        b, hgr = c // 4, c % 4
        y[b, hgr * LTOK:(hgr + 1) * LTOK, :] = np.asarray(res.results[c]["out"],
                                                          np.float32)
    return y


# version 17
# speedup vs baseline: 1.0391x; 1.0086x over previous
import os
import numpy as np
import concourse.bass as bass
import concourse.mybir as mybir
from concourse.bass_utils import run_bass_kernel_spmd

F32 = mybir.dt.float32
BF16 = mybir.dt.bfloat16
AF = mybir.ActivationFunctionType
ALU = mybir.AluOpType
AX = mybir.AxisListType

T = 8192
D = 2048
NH = 4
HD = 128
FF = 8192
E = 8
TTN = 16
CH = 16
QTN = 16
LTOK = 2048
LTT = 4
SCALE = float(HD) ** -0.5
RMS_EPS = 1e-6
LN_EPS = 1e-5

KPH = int(os.environ.get("KPH", "6"))
KSUB = os.environ.get("KSUB", "")
KDBG = os.environ.get("KDBG", "")
Q2N = 8
FT2 = 288
TILE2QK = []
for _qt in range(Q2N):
    for _kb in range(8 * _qt + 8):
        TILE2QK.append((_qt, _kb))
assert len(TILE2QK) == FT2


def lidx(qt, kb):
    return 4 * qt * (qt + 1) + kb


def dcnt(h, qt, kb):
    return 8 * (Q2N * h + qt) + max(0, kb - 8 * qt)


def pcix(h, qt, kb):
    return 144 * h + 2 * qt * (qt + 1) + kb // 2


DIAG_OF_HEAD = []
for _qt in range(Q2N):
    for _kb in range(8 * _qt, 8 * _qt + 8):
        DIAG_OF_HEAD.append((lidx(_qt, _kb), _kb - 8 * _qt))


class Bar:

    def __init__(self, eng, sem):
        self.eng, self.sem, self.n = eng, sem, 0

    def mark(self, instr):
        self.n += 1
        instr.then_inc(self.sem, 1)

    def wait(self):
        self.eng.wait_ge(self.sem, self.n)


def build_program():
    nc = bass.Bass("TRN2", target_bir_lowering=False)

    xTt = nc.dram_tensor("xTt", [TTN, 128, CH * 512], F32, kind="ExternalInput")
    xTr = nc.dram_tensor("xTr", [D, LTOK], F32, kind="ExternalInput").ap()
    wqT = nc.dram_tensor("wqT", [D, 512], F32, kind="ExternalInput").ap()
    wkT = nc.dram_tensor("wkT", [D, 512], F32, kind="ExternalInput").ap()
    wvT = nc.dram_tensor("wvT", [D, 512], F32, kind="ExternalInput").ap()
    bqkv = nc.dram_tensor("bqkv", [128, 12], F32, kind="ExternalInput").ap()
    woTt = nc.dram_tensor("woTt", [16, 128, CH * 128], F32, kind="ExternalInput")
    bo_c = nc.dram_tensor("bo_c", [128, 16], F32, kind="ExternalInput").ap()
    nw_row = nc.dram_tensor("nw_row", [1, D], F32, kind="ExternalInput").ap()
    rwT = nc.dram_tensor("rwT", [D, E], F32, kind="ExternalInput").ap()
    w1t = nc.dram_tensor("w1t", [64, 128, CH * 128], F32, kind="ExternalInput")
    b1_c = nc.dram_tensor("b1_c", [128, 64], F32, kind="ExternalInput").ap()
    w2t = nc.dram_tensor("w2t", [16, 128, 64 * 128], F32, kind="ExternalInput")
    b2_c = nc.dram_tensor("b2_c", [128, 16], F32, kind="ExternalInput").ap()
    tri_in = nc.dram_tensor("tri_in", [128, 128], F32, kind="ExternalInput").ap()
    masks_in = nc.dram_tensor("masks_in", [128, 2048], F32, kind="ExternalInput").ap()
    iota_in = nc.dram_tensor("iota_in", [128, E], F32, kind="ExternalInput").ap()
    cmask_in = nc.dram_tensor("cmask_in", [8, 2], F32, kind="ExternalInput").ap()
    id_in = nc.dram_tensor("id_in", [128, 128], F32, kind="ExternalInput").ap()

    out = nc.dram_tensor("out", [LTOK, D], F32, kind="ExternalOutput").ap()

    xn_dram = nc.dram_tensor("xn_dram", [TTN, 128, CH * 512], BF16, kind="Internal")
    wobf = nc.dram_tensor("wobf", [16, 128, CH * 128], BF16, kind="Internal")
    w1bf = nc.dram_tensor("w1bf", [64, 128, CH * 128], BF16, kind="Internal")
    w2bf = nc.dram_tensor("w2bf", [16, 128, 64 * 128], BF16, kind="Internal")
    ao_h = [nc.dram_tensor(f"ao{j}", [HD, T], BF16, kind="Internal") for j in range(NH)]
    aog_h = [nc.dram_tensor(f"aog{j}", [4 * HD, T], BF16, kind="Internal") for j in range(NH)]
    rt_in = nc.dram_tensor("rt_in", [1, 16], F32, kind="Internal")
    rt_out = nc.dram_tensor("rt_out", [8, 16], F32, kind="Internal")
    x1f_dram = nc.dram_tensor("x1f_dram", [D, LTOK], F32, kind="Internal").ap()
    ffn_dram = nc.dram_tensor("ffn_dram", [D, LTOK], F32, kind="Internal").ap()
    g_dram = nc.dram_tensor("g_dram", [1, LTOK], F32, kind="Internal").ap()

    G4 = [[0, 1, 2, 3], [4, 5, 6, 7]]
    G8 = [[0, 1, 2, 3, 4, 5, 6, 7]]
    CST_N = 13

    from contextlib import ExitStack
    with ExitStack() as ctx:
        En = ctx.enter_context

        def place(name, shape, dtype, off):
            assert off % 32 == 0, (name, off)
            return nc.alloc_sbuf_tensor_at(name, shape, dtype, offset=off)

        _A = [18432]
        def pa(name, shape, dtype, sz):
            off = _A[0]
            _A[0] = (off + sz + 63) & ~63
            return place(name, shape, dtype, off)

        tri_f = pa("tri_f", [128, 128], F32, 512)
        id_f = pa("id_f", [128, 128], F32, 512)
        iota8 = pa("iota8", [128, E], F32, 32)
        cmask = pa("cmask", [8, 2], F32, 32)
        ones_col_b = pa("ones_col_b", [128, 1], BF16, 32)
        ones_col_f = pa("ones_col_f", [128, 1], F32, 32)
        ones_row_f = pa("ones_row_f", [1, 128], F32, 512)
        bqkv_s = pa("bqkv_s", [128, 12], F32, 64)
        bo_s = pa("bo_s", [128, 16], F32, 64)
        b1_s = pa("b1_s", [128, 64], F32, 256)
        b2_s = pa("b2_s", [128, 16], F32, 64)
        rw_s = pa("rw_s", [128, CH * E], BF16, 256)
        ms_sb = pa("ms_sb", [1, 512], F32, 2048)
        sm_sb = pa("sm_sb", [1, 512], F32, 2048)
        rstd_sb = pa("rstd_sb", [1, 512], F32, 2048)
        me_sb = pa("me_sb", [1, 512], F32, 2048)
        v1_sb = pa("v1_sb", [1, 512], F32, 2048)
        var_sb = pa("var_sb", [1, 512], F32, 2048)
        mr_sb = pa("mr_sb", [1, 512], F32, 2048)
        lvar_sb = place("lvar_sb", [1, 512], F32,
                        nc.lookup_mloc(sm_sb).addr)
        m2_sb = place("m2_sb", [1, 512], F32, nc.lookup_mloc(sm_sb).addr)
        rst2_sb = place("rst2_sb", [1, 512], F32,
                        nc.lookup_mloc(rstd_sb).addr)
        _nm = _A[0]
        nwr = place("nwr", [1, D], F32, _nm)
        masks = place("masks", [128, 2048], F32, _nm)
        fev = [place(f"fev{i}", [128, 512], F32, _nm + i * 2048) for i in range(2)]
        ffb = [place(f"ffb{i}", [128, 512], F32, _nm + 4096 + i * 2048) for i in range(2)]
        x6a = [place(f"x6a{i}", [128, 512], F32, _nm + i * 2048) for i in range(2)]
        x6b = [place(f"x6b{i}", [128, 512], F32, _nm + 4096 + i * 2048) for i in range(2)]
        _A[0] = _nm + 8192
        AEND = _A[0]
        assert AEND <= 46336, AEND

        BB = 46336
        xnb = [place(f"xnb{i}", [128, CH * 512], BF16, BB + i * 16384)
               for i in range(2)]
        xb = [place(f"xb{i}", [128, CH * 512], F32, BB + 32768 + i * 32768)
              for i in range(2)]
        sqt2 = [place(f"sqt2{i}", [128, 512], BF16, BB + 98304 + i * 1024)
                for i in range(2)]
        nwr_bf = place("nwr_bf", [1, D], BF16, BB + 100352)
        rstd_bf = place("rstd_bf", [1, 512], BF16, BB + 104448)
        pp_sb = [place(f"pp{i}", [128, 1024], BF16, BB + 175616 + i * 2048)
                 for i in range(2)]
        praw = place("praw", [128, 1024], BF16, BB + 179712)
        wqb = place("wqb", [128, CH * 512], BF16, BB + 32768)
        wkb = place("wkb", [128, CH * 512], BF16, BB + 49152)
        wvb = place("wvb", [128, CH * 512], BF16, BB + 65536)
        qT_s = place("qT_s", [128, T], BF16, BB + 81920)
        kT_s = place("kT_s", [128, T], BF16, BB + 98304)
        V_s = place("V_s", [128, T], BF16, BB + 114688)
        OT_s = [place("OT0", [128, T], BF16, BB + 131072)]
        pt = [place(f"pt{i}", [128, 1024], BF16, BB + 147456 + i * 2048)
              for i in range(4)]
        stile = [place(f"stile{i}", [128, 1024], F32, BB + 155648 + i * 4096)
                 for i in range(2)]
        vtmp = place("vtmp", [128, 512], F32, BB + 163840)
        sqh = place("sqh", [128, 512], BF16, BB + 165888)
        lb_s = place("lb_s", [128, 1024], F32, BB + 166912)
        linv_s = place("linv_s", [1, 1024], F32, BB + 171008)
        q2m = place("q2m", [1, 8], F32, BB + 175104)
        _qb = BB + 175136
        qn2m = place("qn2m", [1, QTN], F32, _qb + 0)
        kn2m = place("kn2m", [1, QTN], F32, _qb + 64)
        kprefm = place("kprefm", [1, QTN], F32, _qb + 128)
        bbv = place("bbv", [1, QTN], F32, _qb + 192)
        bbl = place("bbl", [1, QTN], F32, _qb + 256)
        bbe = place("bbe", [1, QTN], F32, _qb + 320)
        brow = place("brow", [1, QTN], F32, _qb + 384)
        biasq = place("biasq", [128, 8], F32, _qb + 448)
        assert _qb + 512 <= 229344

        CC = 46336
        x1T_s = place("x1T_s", [128, CH * LTOK], BF16, CC + 0)
        y_sb = place("y_sb", [128, CH * 512], F32, CC + 65536)
        y2_sb = place("y2_sb", [128, CH * 512], F32, CC + 65536)
        hT_s = place("hT_s", [128, 64 * 512], BF16, CC + 65536)
        _sm = CC + 98304
        ytmp = place("ytmp", [128, 512], F32, _sm + 0)
        ybf = place("ybf", [128, 512], BF16, _sm + 2048)
        sqy = place("sqy", [128, 512], BF16, _sm + 3072)
        rb_sb = place("rb_sb", [128, 512], F32, _sm + 4096)
        mrb_sb = place("mrb_sb", [128, 512], F32, _sm + 6144)
        x1f_t = place("x1f_t", [128, 512], F32, _sm + 8192)
        x1g_t = place("x1g_t", [128, 512], F32, _sm + 10240)
        x2_sb = place("x2_sb", [128, 512], F32, _sm + 12288)
        _rt = CC + 112640
        _r1 = ("negmax", "nmx", "ssum", "si_sb", "idx_sb", "g1_sb", "g2_sb",
               "ga_sb", "gb2_sb")
        _rh = {}
        for _i, _n in enumerate(_r1):
            _rh[_n] = place(_n, [128, 1], F32, _rt + 32 * _i)
        negmax, nmx, ssum = _rh["negmax"], _rh["nmx"], _rh["ssum"]
        si_sb, idx_sb = _rh["si_sb"], _rh["idx_sb"]
        g1_sb, g2_sb, ga_sb, gb2_sb = (_rh["g1_sb"], _rh["g2_sb"],
                                       _rh["ga_sb"], _rh["gb2_sb"])
        _r2 = ("pe_sb", "probs", "eqt", "at_sb", "t1_sb", "idxf_sb", "maskd")
        for _i, _n in enumerate(_r2):
            _rh[_n] = place(_n, [128, E], F32, _rt + 288 + 32 * _i)
        pe_sb, probs, eqt = _rh["pe_sb"], _rh["probs"], _rh["eqt"]
        at_sb, t1_sb = _rh["at_sb"], _rh["t1_sb"]
        idxf_sb, maskd = _rh["idxf_sb"], _rh["maskd"]
        pv_sb = place("pv_sb", [128, 32], F32, _rt + 512)
        ohf_sb = place("ohf_sb", [128, 256], F32, _rt + 640)
        c_sb = place("c_sb", [128, 256], F32, _rt + 1664)
        runtot = [place(f"runtot{i}", [1, 16], F32, _rt + 2688 + i * 64)
                  for i in range(2)]
        rto_sb = place("rto_sb", [8, 16], F32, _rt + 2816)
        offr_sb = place("offr_sb", [1, 16], F32, _rt + 2880)
        offb_sb = place("offb_sb", [128, 16], F32, _rt + 2944)
        glob_sb = place("glob_sb", [128, 16], F32, _rt + 3008)
        keep_sb = place("keep_sb", [128, 16], F32, _rt + 3072)
        sel_sb = place("sel_sb", [128, 16], F32, _rt + 3136)
        g_sb = place("g_sb", [128, 16], F32, _rt + 3200)
        gt_sb = place("gt_sb", [16, 128], F32, _rt + 3264)
        wob = [place(f"wob{i}", [128, CH * 128], BF16, CC + 131072 + i * 4096)
               for i in range(2)]
        w1b = [place(f"w1b{i}", [128, CH * 128], BF16, CC + 131072 + i * 4096)
               for i in range(2)]
        aob = [place(f"aob{i}", [128, CH * 512], BF16, CC + 139264 + i * 16384)
               for i in range(2)]
        w2b = [place(f"w2b{i}", [128, 64 * 128], BF16, CC + 139264 + i * 16384)
               for i in range(2)]
        grow = place("grow", [1, LTOK], F32, CC + 172032)
        xrb = [place(f"xrb{i}", [128, 512], F32, CC + 172032 + i * 2048)
               for i in range(2)]
        xt2_sb = [place(f"xt2_{i}", [128, 512], F32, CC + 178176 + i * 2048)
                  for i in range(2)]
        x6ab = place("x6ab", [128, CH * 512], F32, CC + 112640)
        x6bb = place("x6bb", [128, CH * 512], F32, CC + 145408)
        gbc_sb = place("gbc_sb", [128, 512], F32, CC + 180224)
        assert CC + 182272 <= 229344

        ps_a = [En(nc.psum_tensor(f"ps_a{i}", [128, 1024], F32)) for i in range(2)]
        ps_b = En(nc.psum_tensor("ps_b", [128, 1024], F32))
        ps_c = En(nc.psum_tensor("ps_c", [128, 1024], F32))
        def pA(i, c0=0, c1=512, p0=0, p1=128):
            return ps_a[i % 2][p0:p1, 512 * (i // 2) + c0:512 * (i // 2) + c1]

        def pC(i, c0=0, c1=512, p0=0, p1=128):
            return ps_c[p0:p1, 512 * i + c0:512 * i + c1]

        S = {}
        for name in ("pcwo", "pcw1", "pcw2", "cst", "cqv", "p6o0", "p6o1",
                     "p1x0", "p1x1", "p1sq", "p1ss", "p1ms", "p1sm", "p1r", "p1wb",
                     "p1vk", "p1o0", "p1o1",
                     "p2x0", "p2x1", "p2mm", "p2e", "p2sq", "p2v", "p2nm",
                     "p2pm", "p2l2", "p2br", "p2bb", "p2b", "p2ao", "p2ag",
                     "fst", "fexp", "fpv", "fmsk", "flv", "flb", "flbc", "fqd",
                     "fnm",
                     "p3x0", "p3x1", "p3w0", "p3w1", "p3xr0", "p3xr1",
                     "p3m", "p3e", "p3y", "p3s", "p3st",
                     "p3lnb", "p3ln", "p3mr", "p3bc", "p3bce", "p3n",
                     "p3xo0", "p3xo1",
                     "p4m", "p4sf", "p4ex", "p4oh", "p4cm", "p4ce", "p4rt",
                     "p4ti", "p4ag", "p4ro", "p4ov", "p4om", "p4ob", "p4obe",
                     "p4g", "p4gt", "p4gte", "p4go", "p4gi",
                     "p5a", "p5h",
                     "p5o", "p5ev", "p5e",
                     "p5gm", "p5gb",
                     "p6y", "p6s", "p6st",
                     "p6lnb", "p6ln",
                     "p6mr", "p6bc", "p6bce", "p6n", "p6tr", "p6tv", "msk0", "dbg",
                     "vbr", "sbr"):
            S[name] = En(nc.semaphore(name))
        for a, b in (("p6x0", "p2x0"), ("p6x1", "p2x1"),
                     ("p6f0", "p1o0"), ("p6f1", "p1o1"),
                     ("p5fo0", "p3xo0"), ("p5fo1", "p3xo1"),
                     ("p5w10", "p3w0"), ("p5w11", "p3w1"),
                     ("p5w20", "p3xr0"), ("p5w21", "p3xr1"),
                     ("fpp", "p1sq"), ("flp", "p1wb")):
            S[a] = S[b]
        BFP = 256
        BP1X = 16 * 32
        BP1O = 16 * 8
        BP2X = 16 * 32
        BP3 = 16 * 32

        block = En(nc.Block())

        @block.sync
        def _(sy):
            for dst, src in ((tri_f, tri_in), (iota8, iota_in),
                             (cmask, cmask_in), (id_f, id_in), (nwr, nw_row),
                             (bqkv_s, bqkv), (bo_s, bo_c), (b1_s, b1_c),
                             (b2_s, b2_c)):
                sy.dma_start(out=dst[:], in_=src).then_inc(S["cst"], 16)
            for tt in range(0, TTN, 2):
                if tt >= 2:
                    sy.wait_ge(S["p1vk"], 16 * (tt - 1))
                    sy.wait_ge(S["p1sq"], 16 * (tt - 1))
                sy.dma_start(
                    out=xb[0][:], in_=xTt.ap()[tt],
                ).then_inc(S["p1x0"], 16)
            if KPH < 2:
                return
            sy.wait_ge(S["p1vk"], CH * TTN)
            sy.dma_start(out=masks[:], in_=masks_in).then_inc(S["msk0"], 16)
            for h in range(NH):
                for tt in range(TTN):
                    m = TTN * h + tt
                    sy.wait_ge(S["p1o0"], 16 * 8)
                    sy.wait_ge(S["p1o1"], 16 * 8)
                    if m >= 2:
                        sy.wait_ge(S["p2mm"], 6 * (m - 2) + 3)
                    sy.dma_start(
                        out=xnb[m % 2][:], in_=xn_dram.ap()[tt],
                    ).then_inc(S["p2x0" if m % 2 == 0 else "p2x1"], 16)
            if KPH < 3:
                return
            sy.wait_ge(S["p2ag"], NH)
            pid = nc.partition_id([mybir.EngineType.SP])
            tko = (pid % 4) * LTOK
            for tt in range(LTT):
                if tt >= 2:
                    sy.wait_ge(S["p3m"], (tt - 1) * CH)
                for j in range(NH):
                    sy.dma_start(
                        out=aob[tt % 2][:, (4 * j) * 512:(4 * j + 4) * 512]
                            .rearrange("p (r t) -> p r t", r=4),
                        in_=aog_h[j].ap().rearrange("(r p) t -> p r t", p=128)
                            [:, :, bass.ds(tko + tt * 512, 512)],
                    ).then_inc(S["p3x0" if tt % 2 == 0 else "p3x1"], 16)
                if tt == 0:
                    sy.wait_ge(S["pcwo"], 16 * 4)
                for ec in range(CH):
                    w = tt * CH + ec
                    if w >= 2:
                        sy.wait_ge(S["p3m"], w - 1)
                    sy.dma_start(
                        out=wob[ec % 2][:], in_=wobf.ap()[ec],
                    ).then_inc(S["p3w0" if w % 2 == 0 else "p3w1"], 16)
                    if w >= 2:
                        sy.wait_ge(S["p3y"], w - 1)
                    sy.dma_start(
                        out=xrb[w % 2][:],
                        in_=xTr[ec * 128:(ec + 1) * 128,
                                tt * 512:(tt + 1) * 512],
                    ).then_inc(S["p3xr0" if w % 2 == 0 else "p3xr1"], 16)
            if KPH < 5:
                return
            sy.wait_ge(S["p4go"], 16)
            sy.wait_ge(S["pcw1"], 16 * 8)
            sy.wait_ge(S["pcw2"], 16 * 8)
            sy.dma_start(out=grow[:], in_=g_dram).then_inc(S["p4gi"], 16)
            for tt in range(LTT):
                for f in range(64):
                    Fi = tt * 64 + f
                    if Fi >= 2:
                        sy.wait_ge(S["p5a"], Fi - 1)
                    sy.dma_start(
                        out=w1b[f % 2][:], in_=w1bf.ap()[f],
                    ).then_inc(S["p5w10" if Fi % 2 == 0 else "p5w11"], 16)
                for ec in range(CH):
                    Ei = tt * CH + ec
                    if Ei >= 2:
                        sy.wait_ge(S["p5o"], Ei - 1)
                    sy.dma_start(
                        out=w2b[ec % 2][:], in_=w2bf.ap()[ec],
                    ).then_inc(S["p5w20" if Ei % 2 == 0 else "p5w21"], 16)
            if KPH < 6:
                return
            sy.wait_ge(S["p5fo0"], BP3 + 16 * LTT * CH // 2)
            sy.wait_ge(S["p5fo1"], BP3 + 16 * LTT * CH // 2)
            for tt in range(LTT):
                if tt >= 1:
                    sy.wait_ge(S["p6y"], tt * CH)
                sy.dma_start(
                    out=x6ab[:].rearrange("p (k t) -> p k t", k=CH),
                    in_=x1f_dram.rearrange("(k p) t -> p k t", p=128)
                        [:, :, tt * 512:(tt + 1) * 512],
                ).then_inc(S["p6x0" if tt % 2 == 0 else "p6x1"], 16)
            sy.wait_ge(S["p6o0"], 16 * LTT * CH // 2)
            sy.wait_ge(S["p6o1"], 16 * LTT * CH // 2)

        @block.gpsimd
        def _(g):
            g.dma_start(out=nwr_bf[:], in_=nw_row).then_inc(S["cqv"], 16)
            g.wait_ge(S["p1vk"], CH * TTN)
            for dst, src in ((wqb, wqT), (wkb, wkT), (wvb, wvT)):
                g.dma_start(
                    out=dst[:].rearrange("p (k d) -> p k d", k=CH),
                    in_=src.rearrange("(k p) d -> p k d", p=128),
                ).then_inc(S["cqv"], 16)
            g.dma_start(
                out=rw_s[:].rearrange("p (k e) -> p k e", k=CH),
                in_=rwT.rearrange("(k p) e -> p k e", p=128),
            ).then_inc(S["cqv"], 16)
            for i in range(4):
                g.dma_start(out=wobf.ap()[4 * i:4 * i + 4],
                            in_=woTt.ap()[4 * i:4 * i + 4]).then_inc(S["pcwo"], 16)
            for i in range(8):
                g.dma_start(out=w1bf.ap()[8 * i:8 * i + 8],
                            in_=w1t.ap()[8 * i:8 * i + 8]).then_inc(S["pcw1"], 16)
            for i in range(8):
                g.dma_start(out=w2bf.ap()[2 * i:2 * i + 2],
                            in_=w2t.ap()[2 * i:2 * i + 2]).then_inc(S["pcw2"], 16)
            if KDBG == "xn":
                g.wait_ge(S["p1o0"], 16 * TTN // 2)
                g.wait_ge(S["p1o1"], 16 * TTN // 2)
                g.dma_start(out=out[0:2048, :],
                            in_=xn_dram[:, 0:2048]).then_inc(S["dbg"], 16)
                g.wait_ge(S["dbg"], 16)
            if KPH < 2:
                return
            if KDBG == "qt" and KSUB == "qkv":
                g.wait_ge(S["p2e"], 3 * TTN * NH)
                g.dma_start(out=out[0:128, :],
                            in_=qT_s[:, 0:2048]).then_inc(S["dbg"], 16)
                g.dma_start(out=out[128:256, :],
                            in_=kT_s[:, 0:2048]).then_inc(S["dbg"], 16)
                g.dma_start(out=out[256:384, :],
                            in_=V_s[:, 0:2048]).then_inc(S["dbg"], 16)
                g.wait_ge(S["dbg"], 48)
            if KSUB == "qkv":
                return
            if KDBG == "ot":
                g.wait_ge(S["fnm"], Q2N)
                g.dma_start(out=out[0:128, :],
                            in_=OT_s[0][:, 0:2048]).then_inc(S["dbg"], 16)
                g.wait_ge(S["dbg"], 16)
            for h in range(NH):
                g.wait_ge(S["fnm"], Q2N * (h + 1))
                g.dma_start(out=ao_h[h].ap(),
                            in_=OT_s[0][:]).then_inc(S["p2ao"], 16)
                g.wait_ge(S["p2ao"], 16 * (h + 1))
                g.collective_compute(
                    "AllGather", ALU.bypass, replica_groups=G4,
                    ins=[ao_h[h].ap().opt()], outs=[aog_h[h].ap().opt()],
                ).then_inc(S["p2ag"])
            if KDBG == "p3":
                g.wait_ge(S["p3n"], LTT * CH)
                g.dma_start(out=out[0:128, :], in_=y_sb[:, 0:2048]).then_inc(S["dbg"], 16)
                g.dma_start(out=out[128:256, 0:512], in_=rb_sb[:]).then_inc(S["dbg"], 16)
                g.dma_start(out=out[256:384, 0:512], in_=mrb_sb[:]).then_inc(S["dbg"], 16)
                g.dma_start(out=out[384:512, 0:512], in_=ytmp[:]).then_inc(S["dbg"], 16)
                g.wait_ge(S["dbg"], 64)
            if KPH < 4:
                return
            g.wait_ge(S["p4rt"], CH)
            g.dma_start(out=rt_in.ap(), in_=runtot[CH % 2][:]).then_inc(S["p4ti"], 16)
            g.wait_ge(S["p4ti"], 16)
            g.collective_compute(
                "AllGather", ALU.bypass, replica_groups=G8,
                ins=[rt_in.ap().opt()], outs=[rt_out.ap().opt()],
            ).then_inc(S["p4ag"])
            g.wait_ge(S["p4ag"], 1)
            g.dma_start(out=rto_sb[:], in_=rt_out.ap()).then_inc(S["p4ro"], 16)
            g.wait_ge(S["p4gte"], 1)
            g.dma_start(out=g_dram.rearrange("o (p t) -> p o t", p=16),
                        in_=gt_sb[:].rearrange("p (o t) -> p o t", o=1)).then_inc(S["p4go"], 16)
            if KPH < 6:
                return
            for tt in range(LTT):
                for ec in range(CH):
                    w = tt * CH + ec
                    g.wait_ge(S["p6tv"], w + 1)
                    g.dma_start(
                        out=out.rearrange("(j p) e -> p j e", p=128)
                            [:, 4 * tt:4 * tt + 4, ec * 128:(ec + 1) * 128],
                        in_=xt2_sb[w % 2][:].rearrange("p (j c) -> p j c", j=4),
                    ).then_inc(S["p6o0" if w % 2 == 0 else "p6o1"], 16)
            g.wait_ge(S["p6o0"], 16 * LTT * CH // 2)
            g.wait_ge(S["p6o1"], 16 * LTT * CH // 2)

        @block.tensor
        def _(t):
            t.wait_ge(S["cst"], 16 * 9)
            t.wait_ge(S["dbg"], 1)
            t.wait_ge(S["cqv"], 16)
            for tt in range(TTN):
                if tt >= 1:
                    t.wait_ge(S["p1ms"], tt)
                for kc in range(CH):
                    W = CH * tt + kc
                    t.wait_ge(S["p1sq"], W + 1)
                    t.matmul(pC(0, p0=0, p1=1), ones_col_b[:], sqt2[kc % 2][:],
                             start=(kc == 0), stop=(kc == CH - 1),
                             skip_group_check=True).then_inc(S["p1ss"], 1)
                t.wait_ge(S["p1r"], tt + 1)
                for kc in range(CH):
                    W = CH * tt + kc
                    if W >= 2:
                        t.wait_ge(S["p1vk"], W - 1)
                    t.matmul(pA(kc % 2), nwr_bf[0:1, kc * 128:(kc + 1) * 128],
                             rstd_bf[0:1, :], start=True, stop=True,
                             skip_group_check=True).then_inc(S["p1wb"], 1)
            if KPH < 2:
                return
            t.wait_ge(S["p1vk"], CH * TTN)
            t.wait_ge(S["cqv"], 16 * 5)
            for h in range(NH):
                for tt in range(TTN):
                    m = TTN * h + tt
                    t.wait_ge(S["p2x0" if m % 2 == 0 else "p2x1"],
                              16 * (m // 2 + 1))
                    if h > 0 and tt == 0 and KSUB != "qkv":
                        t.wait_ge(S["fexp"], FT2 * h)
                    for pi, wsb in enumerate((wqb, wkb, wvb)):
                        if m >= 1:
                            t.wait_ge(S["p2e"], 3 * (m - 1) + pi + 1)
                        for kc in range(CH):
                            mm = t.matmul(
                                pA(pi),
                                wsb[:, kc * 512 + h * 128: kc * 512 + (h + 1) * 128],
                                xnb[m % 2][:, kc * 512:(kc + 1) * 512],
                                start=(kc == 0), stop=(kc == CH - 1),
                                skip_group_check=True)
                        mm.then_inc(S["p2mm"], 1)
                    t.wait_ge(S["p2e"], 3 * m + 3)
                    if m >= 1:
                        t.wait_ge(S["p2v"], m)
                    if h > 0 and tt == 0 and KSUB != "qkv":
                        t.wait_ge(S["flbc"], Q2N * h)
                    for j in range(4):
                        mm = t.transpose(pC(1, j * 128, (j + 1) * 128),
                                         vtmp[:, j * 128:(j + 1) * 128], id_f[:])
                    mm.then_inc(S["p2mm"], 1)
                    t.wait_ge(S["p2sq"], 2 * m + 1)
                    if m >= 1:
                        t.wait_ge(S["p2nm"], 2 * m)
                    t.matmul(pC(0, p0=0, p1=1), ones_col_b[:], sqh[:],
                             start=True, stop=True,
                             skip_group_check=True).then_inc(S["p2mm"], 1)
                    t.wait_ge(S["p2sq"], 2 * m + 2)
                    t.wait_ge(S["p2nm"], 2 * m + 1)
                    t.matmul(pC(0, p0=0, p1=1), ones_col_b[:], sqh[:],
                             start=True, stop=True,
                             skip_group_check=True).then_inc(S["p2mm"], 1)
                if KSUB == "qkv":
                    continue
                t.wait_ge(S["p2br"], h + 1)
                t.matmul(pC(0, 0, Q2N), ones_row_f[:], brow[0:1, 0:Q2N],
                         start=True, stop=True,
                         skip_group_check=True).then_inc(S["p2bb"], 1)
                g0 = FT2 * h
                t.wait_ge(S["p2e"], 3 * TTN * (h + 1))

                def emit_st(L):
                    qt2, kb2 = TILE2QK[L]
                    gg = g0 + L
                    if gg >= 2:
                        t.wait_ge(S["fexp"], gg - 1)
                    t.matmul(ps_a[gg % 2][:, 0:512],
                             kT_s[:, kb2 * 128:(kb2 + 1) * 128],
                             qT_s[:, qt2 * 1024:qt2 * 1024 + 512],
                             start=True, stop=True, skip_group_check=True)
                    t.matmul(ps_a[gg % 2][:, 512:1024],
                             kT_s[:, kb2 * 128:(kb2 + 1) * 128],
                             qT_s[:, qt2 * 1024 + 512:qt2 * 1024 + 1024],
                             start=True, stop=True,
                             skip_group_check=True).then_inc(S["fst"], 1)

                for L in range(2):
                    emit_st(L)
                def emit_lb():
                    t.matmul(pC(0), ones_row_f[:], linv_s[0:1, 0:512],
                             start=True, stop=True, skip_group_check=True)
                    t.matmul(pC(1), ones_row_f[:], linv_s[0:1, 512:1024],
                             start=True, stop=True,
                             skip_group_check=True).then_inc(S["flb"], 1)

                for qt in range(Q2N):
                    nk = 8 * qt + 8
                    Q = Q2N * h + qt
                    for kb in range(nk):
                        L = lidx(qt, kb)
                        gg = g0 + L
                        if L + 2 < FT2:
                            emit_st(L + 2)
                        if kb == 0:
                            if Q >= 1:
                                t.wait_ge(S["fqd"], Q)
                            t.wait_ge(S["p2b"], h + 1)
                            if qt == 0:
                                t.wait_ge(S["p2v"], TTN * (h + 1))
                        t.wait_ge(S["fexp"], gg + 1)
                        t.matmul(ps_b[:, 0:512], V_s[:, kb * 128:(kb + 1) * 128],
                                 pt[gg % 4][:, 0:512], start=(kb == 0),
                                 stop=(kb == nk - 1), skip_group_check=True)
                        t.matmul(ps_b[:, 512:1024],
                                 V_s[:, kb * 128:(kb + 1) * 128],
                                 pt[gg % 4][:, 512:1024], start=(kb == 0),
                                 stop=(kb == nk - 1),
                                 skip_group_check=True).then_inc(S["fpv"], 1)
                        if kb == 4:
                            if qt >= 1:
                                t.wait_ge(S["flv"], Q)
                                emit_lb()
                            if Q >= 1:
                                t.wait_ge(S["flbc"], Q)
                            for p in (0, 1):
                                PCp = 144 * h + 2 * qt * (qt + 1) + p
                                t.wait_ge(S["fpp"], BFP + PCp + 1)
                                t.matmul(pC(0, p0=0, p1=1), ones_col_b[:],
                                         pp_sb[PCp % 2][:, 0:512],
                                         start=(p == 0), stop=False,
                                         skip_group_check=True)
                                t.matmul(pC(1, p0=0, p1=1), ones_col_b[:],
                                         pp_sb[PCp % 2][:, 512:1024],
                                         start=(p == 0), stop=False,
                                         skip_group_check=True).then_inc(S["flp"], 1)
                        if kb % 2 == 1 and kb >= 5:
                            PC = pcix(h, qt, kb)
                            t.wait_ge(S["fpp"], BFP + PC + 1)
                            t.matmul(pC(0, p0=0, p1=1), ones_col_b[:],
                                     pp_sb[PC % 2][:, 0:512],
                                     start=False, stop=(kb == nk - 1),
                                     skip_group_check=True)
                            t.matmul(pC(1, p0=0, p1=1), ones_col_b[:],
                                     pp_sb[PC % 2][:, 512:1024],
                                     start=False, stop=(kb == nk - 1),
                                     skip_group_check=True).then_inc(S["flp"], 1)
                t.wait_ge(S["flv"], Q2N * (h + 1))
                emit_lb()
            if KPH < 3:
                return
            t.wait_ge(S["pcwo"], 16 * 4)
            t.wait_ge(S["fexp"], FT2 * NH)
            t.wait_ge(S["flbc"], Q2N * NH)
            t.wait_ge(S["p2v"], TTN * NH)
            for tt in range(LTT):
                t.wait_ge(S["p3x0" if tt % 2 == 0 else "p3x1"],
                          16 * NH * (tt // 2 + 1))
                for ec in range(CH):
                    w = tt * CH + ec
                    t.wait_ge(S["p3w0" if w % 2 == 0 else "p3w1"],
                              16 * (w // 2 + 1))
                    if w >= 2:
                        t.wait_ge(S["p3e"], w - 1)
                    for dc in range(CH):
                        mm = t.matmul(pA(ec % 2),
                                      wob[ec % 2][:, dc * 128:(dc + 1) * 128],
                                      aob[tt % 2][:, dc * 512:(dc + 1) * 512],
                                      start=(dc == 0), stop=(dc == CH - 1),
                                      skip_group_check=True)
                    mm.then_inc(S["p3m"], 1)
                    if ec > 0:
                        t.wait_ge(S["p3s"], w)
                        t.matmul(pC(0, p0=0, p1=1), ones_col_b[:], ybf[:],
                                 start=(ec == 1), stop=False,
                                 skip_group_check=True)
                        t.matmul(pC(1, p0=0, p1=1), ones_col_b[:], sqy[:],
                                 start=(ec == 1), stop=False,
                                 skip_group_check=True).then_inc(S["p3st"], 1)
                t.wait_ge(S["p3s"], (tt + 1) * CH)
                t.matmul(pC(0, p0=0, p1=1), ones_col_b[:], ybf[:],
                         start=False, stop=True, skip_group_check=True)
                t.matmul(pC(1, p0=0, p1=1), ones_col_b[:], sqy[:],
                         start=False, stop=True,
                         skip_group_check=True).then_inc(S["p3st"], 1)
                t.wait_ge(S["p3mr"], tt + 1)
                if tt >= 1:
                    t.wait_ge(S["p3bce"], tt)
                t.matmul(pC(0), ones_row_f[:], rst2_sb[0:1, :],
                         start=True, stop=True, skip_group_check=True)
                t.matmul(pC(1), ones_row_f[:], mr_sb[0:1, :],
                         start=True, stop=True,
                         skip_group_check=True).then_inc(S["p3bc"], 1)
            if KPH < 4:
                return
            t.wait_ge(S["p3e"], LTT * CH)
            t.wait_ge(S["p3bce"], LTT)
            for tc in range(CH):
                if tc >= 2:
                    t.wait_ge(S["p4ex"], tc - 1)
                t.wait_ge(S["p3n"], (tc // 4 + 1) * CH)
                for hc in range(CH):
                    mm = t.matmul(pA(tc % 2, 0, E),
                                  x1T_s[:, hc * LTOK + tc * 128: hc * LTOK + (tc + 1) * 128],
                                  rw_s[:, hc * E:(hc + 1) * E],
                                  start=(hc == 0), stop=(hc == CH - 1),
                                  skip_group_check=True)
                mm.then_inc(S["p4m"], 1)
                t.wait_ge(S["p4oh"], tc + 1)
                if tc >= 1:
                    t.wait_ge(S["p4ce"], 2 * tc)
                t.wait_ge(S["p4rt"], tc)
                t.wait_ge(S["msk0"], 17)
                t.matmul(pA(2, 0, E), tri_f[:], ohf_sb[:, 16 * tc:16 * tc + 8],
                         start=True, stop=False, skip_group_check=True)
                t.matmul(pA(2, 0, E), ones_row_f[:],
                         runtot[tc % 2][0:1, 0:E],
                         start=False, stop=True,
                         skip_group_check=True).then_inc(S["p4cm"], 1)
                t.matmul(pA(3, 0, E), tri_f[:],
                         ohf_sb[:, 16 * tc + 8:16 * tc + 16],
                         start=True, stop=False, skip_group_check=True)
                t.matmul(pA(3, 0, E), ones_row_f[:],
                         runtot[tc % 2][0:1, 8:16],
                         start=False, stop=True,
                         skip_group_check=True).then_inc(S["p4cm"], 1)
                t.matmul(pC(0, 0, 16, 0, 1), ones_col_f[:],
                         ohf_sb[:, 16 * tc:16 * (tc + 1)],
                         start=True, stop=True,
                         skip_group_check=True).then_inc(S["p4cm"], 1)
            t.wait_ge(S["p4ro"], 16)
            t.wait_ge(S["p4rt"], CH)
            t.matmul(pC(0, 0, 16, 0, 1), cmask[:, 0:1], rto_sb[:],
                     start=True, stop=True, skip_group_check=True)
            t.matmul(pC(1, 0, 16, 0, 1), cmask[:, 1:2], rto_sb[:],
                     start=True, stop=True,
                     skip_group_check=True).then_inc(S["p4om"], 1)
            t.wait_ge(S["p4ov"], 1)
            t.matmul(pC(1, 0, 16), ones_row_f[:], offr_sb[0:1, :],
                     start=True, stop=True,
                     skip_group_check=True).then_inc(S["p4ob"], 1)
            t.wait_ge(S["p4g"], CH)
            t.wait_ge(S["p4obe"], 1)
            t.wait_ge(S["p4rt"], CH)
            t.transpose(pC(0, 0, 128, 0, 16), g_sb[:, 0:16],
                        id_f[:]).then_inc(S["p4gt"], 1)
            if KPH < 5:
                return
            t.wait_ge(S["pcw1"], 16 * 8)
            t.wait_ge(S["pcw2"], 16 * 8)
            t.wait_ge(S["p4ex"], CH)
            for tt in range(LTT):
                for f in range(64):
                    Fi = tt * 64 + f
                    t.wait_ge(S["p5w10" if Fi % 2 == 0 else "p5w11"],
                              BP3 + 16 * (Fi // 2 + 1))
                    if Fi == 2:
                        t.wait_ge(S["p4ce"], 2 * CH)
                    if Fi >= 4:
                        t.wait_ge(S["p5h"], Fi - 3)
                    for hc in range(CH):
                        mm = t.matmul(
                            pA(Fi % 4),
                            w1b[f % 2][:, hc * 128:(hc + 1) * 128],
                            x1T_s[:, hc * LTOK + tt * 512: hc * LTOK + (tt + 1) * 512],
                            start=(hc == 0), stop=(hc == CH - 1),
                            skip_group_check=True)
                    mm.then_inc(S["p5a"], 1)
                t.wait_ge(S["p4gi"], 16)
                if tt >= 1:
                    t.wait_ge(S["p5e"], tt * CH)
                else:
                    t.wait_ge(S["p4gte"], 1)
                t.matmul(pC(0), ones_row_f[:],
                         grow[0:1, tt * 512:(tt + 1) * 512],
                         start=True, stop=True,
                         skip_group_check=True).then_inc(S["p5gm"], 1)
                for ec in range(CH):
                    Ei = tt * CH + ec
                    t.wait_ge(S["p5w20" if Ei % 2 == 0 else "p5w21"],
                              BP3 + 16 * (Ei // 2 + 1))
                    t.wait_ge(S["p5h"], 64 * (tt + 1))
                    if Ei >= 1:
                        t.wait_ge(S["p5ev"], Ei)
                    for fc in range(64):
                        mm = t.matmul(
                            ps_b[:, 0:512],
                            w2b[ec % 2][:, fc * 128:(fc + 1) * 128],
                            hT_s[:, fc * 512:(fc + 1) * 512],
                            start=(fc == 0), stop=(fc == 63),
                            skip_group_check=True)
                    mm.then_inc(S["p5o"], 1)
            if KPH < 6:
                return
            t.wait_ge(S["p5gb"], LTT)
            t.wait_ge(S["p4gte"], 1)
            for tt in range(LTT):
                for ec in range(CH):
                    w = tt * CH + ec
                    t.wait_ge(S["p6s"], w + 1)
                    if ec == 0 and tt >= 1:
                        t.wait_ge(S["p6bce"], tt)
                        t.wait_ge(S["p6tv"], tt * CH)
                    t.matmul(pC(0, p0=0, p1=1), ones_col_b[:], ybf[:],
                             start=(ec == 0), stop=(ec == CH - 1),
                             skip_group_check=True)
                    t.matmul(pC(1, p0=0, p1=1), ones_col_b[:], sqy[:],
                             start=(ec == 0), stop=(ec == CH - 1),
                             skip_group_check=True).then_inc(S["p6st"], 1)
                t.wait_ge(S["p6mr"], tt + 1)
                t.matmul(pC(0), ones_row_f[:], rst2_sb[0:1, :],
                         start=True, stop=True, skip_group_check=True)
                t.matmul(pC(1), ones_row_f[:], mr_sb[0:1, :],
                         start=True, stop=True,
                         skip_group_check=True).then_inc(S["p6bc"], 1)
                for ec in range(CH):
                    w = tt * CH + ec
                    t.wait_ge(S["p6n"], w + 1)
                    if w >= 2:
                        t.wait_ge(S["p6tv"], w - 1)
                    for j in range(4):
                        mm = t.transpose(ps_b[:, j * 128:(j + 1) * 128],
                                         x2_sb[:, j * 128:(j + 1) * 128], id_f[:])
                    mm.then_inc(S["p6tr"], 1)

        @block.scalar
        def _(s):
            sb = Bar(s, S["sbr"])
            s.dma_start(out=xb[1][:], in_=xTt.ap()[1]).then_inc(S["p1x1"], 16)
            for tt in range(TTN):
                s.wait_ge(S["p1x0" if tt % 2 == 0 else "p1x1"],
                          16 * (tt // 2 + 1))
                for kc in range(CH):
                    W = CH * tt + kc
                    if W >= 2:
                        s.wait_ge(S["p1ss"], W - 1)
                    s.activation(sqt2[kc % 2][:],
                                 xb[tt % 2][:, kc * 512:(kc + 1) * 512],
                                 AF.Square).then_inc(S["p1sq"], 1)
                s.wait_ge(S["p1ms"], tt + 1)
                if tt >= 1:
                    s.wait_ge(S["p1r"], tt)
                s.activation(sm_sb[:], ms_sb[:], AF.Sqrt).then_inc(S["p1sm"], 1)
                s.wait_ge(S["p1vk"], 16 * (tt + 1))
                s.dma_start(
                    out=xn_dram.ap()[tt],
                    in_=xnb[tt % 2][:],
                ).then_inc(S["p1o0" if tt % 2 == 0 else "p1o1"], 16)
                if tt % 2 == 1 and tt + 2 < TTN:
                    s.dma_start(out=xb[1][:],
                                in_=xTt.ap()[tt + 2]).then_inc(S["p1x1"], 16)
            if KPH < 2:
                return
            for h in range(NH):
                for tt in range(TTN):
                    m = TTN * h + tt
                    for pi in range(3):
                        s.wait_ge(S["p2mm"], 6 * m + pi + 1)
                        bias = bqkv_s[:, pi * 4 + h: pi * 4 + h + 1]
                        if pi == 0:
                            s.activation(qT_s[:, tt * 512:(tt + 1) * 512],
                                         pA(0), AF.Identity,
                                         bias=bias).then_inc(S["p2e"], 1)
                        elif pi == 1:
                            s.activation(kT_s[:, tt * 512:(tt + 1) * 512],
                                         pA(1), AF.Identity,
                                         bias=bias).then_inc(S["p2e"], 1)
                        else:
                            s.activation(vtmp[:], pA(2), AF.Identity,
                                         bias=bias).then_inc(S["p2e"], 1)
                if KSUB == "qkv":
                    continue
                s.wait_ge(S["p2pm"], h + 1)
                sb.mark(s.activation(bbl[0:1, 0:Q2N], bbv[0:1, 0:Q2N], AF.Ln))
                sb.wait()
                s.activation(bbe[0:1, 0:Q2N], bbl[0:1, 0:Q2N], AF.Exp,
                             scale=0.5).then_inc(S["p2l2"], 1)
                s.wait_ge(S["p2bb"], h + 1)
                s.activation(biasq[:], pC(0, 0, Q2N), AF.Copy).then_inc(S["p2b"], 1)
                s.wait_ge(S["p2b"], h + 1)
                g0 = FT2 * h
                for qt in range(Q2N):
                    nk = 8 * qt + 8
                    Q = Q2N * h + qt
                    for kb in range(nk):
                        L = lidx(qt, kb)
                        gg = g0 + L
                        diag = kb >= 8 * qt
                        if diag:
                            s.wait_ge(S["fmsk"], dcnt(h, qt, kb) + 1)
                        else:
                            s.wait_ge(S["fst"], gg + 1)
                        if gg >= 4:
                            s.wait_ge(S["fpv"], gg - 3)
                            gp = gg - 4
                            hp = gp // FT2
                            qtp, kbp = TILE2QK[gp % FT2]
                            s.wait_ge(S["fpp"], BFP + pcix(hp, qtp, kbp) + 1)
                        src = stile[dcnt(h, qt, kb) % 2][:] if diag else ps_a[gg % 2][:]
                        s.activation(pt[gg % 4][:], src, AF.Exp,
                                     bias=biasq[:, qt:qt + 1],
                                     scale=SCALE).then_inc(S["fexp"], 1)
                        if kb == 4 and qt >= 1:
                            s.wait_ge(S["flb"], Q)
                            s.activation(lb_s[:], ps_c[:],
                                         AF.Copy).then_inc(S["flbc"], 1)
                s.wait_ge(S["flb"], Q2N * (h + 1))
                s.activation(lb_s[:], ps_c[:], AF.Copy).then_inc(S["flbc"], 1)
            if KPH < 3:
                return
            for tt in range(LTT):
                for ec in range(CH):
                    w = tt * CH + ec
                    s.wait_ge(S["p3m"], w + 1)
                    if w >= 1:
                        s.wait_ge(S["p3y"], w)
                    s.activation(ytmp[:], pA(ec % 2), AF.Identity,
                                 bias=bo_s[:, ec:ec + 1]).then_inc(S["p3e"], 1)
                s.wait_ge(S["p3lnb"], tt + 1)
                sb.mark(s.activation(lvar_sb[:], var_sb[:], AF.Ln))
                sb.wait()
                s.activation(rst2_sb[:], lvar_sb[:], AF.Exp,
                             scale=-0.5).then_inc(S["p3ln"], 1)
                s.wait_ge(S["p3bc"], tt + 1)
                s.activation(rb_sb[:], pC(0), AF.Copy)
                s.activation(mrb_sb[:], pC(1), AF.Copy).then_inc(S["p3bce"], 1)
                for ec in range(CH):
                    w = tt * CH + ec
                    s.wait_ge(S["p3n"], w + 1)
                    s.dma_start(
                        out=x1f_dram.rearrange("(k p) t -> k p t", p=128)
                            [ec, :, tt * 512:(tt + 1) * 512],
                        in_=x1g_t[:]).then_inc(S["p3xo0" if w % 2 == 0 else "p3xo1"], 16)
            if KPH < 4:
                return
            for tc in range(CH):
                s.wait_ge(S["p4sf"], tc + 1)
                s.activation(pe_sb[:], pA(tc % 2, 0, E), AF.Exp,
                             bias=negmax[:]).then_inc(S["p4ex"], 1)
            if KPH < 5:
                return
            s.wait_ge(S["p3xo0"], 16 * 32)
            s.wait_ge(S["p3xo1"], 16 * 32)
            for tt in range(LTT):
                for f in range(64):
                    Fi = tt * 64 + f
                    s.wait_ge(S["p5a"], Fi + 1)
                    if tt >= 1 and f < 2:
                        s.wait_ge(S["p5o"], tt * CH)
                    s.activation(hT_s[:, f * 512:(f + 1) * 512], pA(Fi % 4),
                                 AF.Gelu,
                                 bias=b1_s[:, f:f + 1]).then_inc(S["p5h"], 1)
                s.wait_ge(S["p5gm"], tt + 1)
                s.activation(gbc_sb[:], pC(0), AF.Copy).then_inc(S["p5gb"], 1)
                for ec in range(CH):
                    Ei = tt * CH + ec
                    s.wait_ge(S["p5o"], Ei + 1)
                    if Ei >= 2:
                        s.wait_ge(S["p5e"], Ei - 1)
                    s.activation(fev[Ei % 2][:], ps_b[:, 0:512], AF.Identity,
                                 bias=b2_s[:, ec:ec + 1]).then_inc(S["p5ev"], 1)
                    s.wait_ge(S["p5e"], Ei + 1)
                    s.dma_start(
                        out=ffn_dram.rearrange("(k p) t -> k p t", p=128)
                            [ec, :, tt * 512:(tt + 1) * 512],
                        in_=ffb[Ei % 2][:]).then_inc(S["p5fo0" if Ei % 2 == 0 else "p5fo1"], 16)
            if KPH < 6:
                return
            for tt in range(LTT):
                if tt >= 1:
                    s.wait_ge(S["p6y"], tt * CH)
                s.dma_start(
                    out=x6bb[:].rearrange("p (k t) -> p k t", k=CH),
                    in_=ffn_dram.rearrange("(k p) t -> p k t", p=128)
                        [:, :, tt * 512:(tt + 1) * 512],
                ).then_inc(S["p6f0" if tt % 2 == 0 else "p6f1"], 16)
                s.wait_ge(S["p6lnb"], tt + 1)
                sb.mark(s.activation(lvar_sb[:], var_sb[:], AF.Ln))
                sb.wait()
                s.activation(rst2_sb[:], lvar_sb[:], AF.Exp,
                             scale=-0.5).then_inc(S["p6ln"], 1)
                s.wait_ge(S["p6bc"], tt + 1)
                s.activation(rb_sb[:], pC(0), AF.Copy)
                s.activation(mrb_sb[:], pC(1), AF.Copy).then_inc(S["p6bce"], 1)

        @block.vector
        def _(v):
            vb = Bar(v, S["vbr"])
            v.memset(ones_col_b[:], 1.0)
            v.memset(ones_col_f[:], 1.0)
            v.memset(ones_row_f[:], 1.0).then_inc(S["dbg"], 1)
            for tt in range(TTN):
                v.wait_ge(S["p1ss"], 16 * (tt + 1))
                if tt >= 1:
                    v.wait_ge(S["p1sm"], tt)
                v.tensor_scalar(ms_sb[:], pC(0, p0=0, p1=1), 1.0 / D, RMS_EPS,
                                ALU.mult, ALU.add).then_inc(S["p1ms"], 1)
                v.wait_ge(S["p1sm"], tt + 1)
                vb.mark(v.reciprocal(rstd_sb[:], sm_sb[:]))
                vb.wait()
                v.tensor_copy(rstd_bf[:], rstd_sb[:]).then_inc(S["p1r"], 1)
                if tt >= 2:
                    v.wait_ge(S["p1o0" if tt % 2 == 0 else "p1o1"],
                              16 * ((tt - 2) // 2 + 1))
                for kc in range(CH):
                    W = CH * tt + kc
                    v.wait_ge(S["p1wb"], W + 1)
                    v.tensor_mul(xnb[tt % 2][:, kc * 512:(kc + 1) * 512],
                                 xb[tt % 2][:, kc * 512:(kc + 1) * 512],
                                 pA(kc % 2)).then_inc(S["p1vk"], 1)
            if KPH < 2:
                return
            for h in range(NH):
                for tt in range(TTN):
                    m = TTN * h + tt
                    v.wait_ge(S["p2e"], 3 * m + 1)
                    if m >= 1:
                        v.wait_ge(S["p2mm"], 6 * m)
                    v.tensor_mul(sqh[:], qT_s[:, tt * 512:(tt + 1) * 512],
                                 qT_s[:, tt * 512:(tt + 1) * 512]).then_inc(S["p2sq"], 1)
                    v.wait_ge(S["p2mm"], 6 * m + 5)
                    v.reduce_max(qn2m[0:1, tt:tt + 1], pC(0, p0=0, p1=1),
                                 AX.X).then_inc(S["p2nm"], 1)
                    v.wait_ge(S["p2e"], 3 * m + 2)
                    v.tensor_mul(sqh[:], kT_s[:, tt * 512:(tt + 1) * 512],
                                 kT_s[:, tt * 512:(tt + 1) * 512]).then_inc(S["p2sq"], 1)
                    v.wait_ge(S["p2mm"], 6 * m + 6)
                    v.reduce_max(kn2m[0:1, tt:tt + 1], pC(0, p0=0, p1=1),
                                 AX.X).then_inc(S["p2nm"], 1)
                    v.wait_ge(S["p2mm"], 6 * m + 4)
                    v.tensor_copy(V_s[:, tt * 512:(tt + 1) * 512],
                                  pC(1)).then_inc(S["p2v"], 1)
                if KSUB == "qkv":
                    continue
                v.wait_ge(S["p2nm"], 2 * TTN * (h + 1))
                for qt in range(Q2N):
                    vb.mark(v.reduce_max(kprefm[0:1, qt:qt + 1],
                                         kn2m[0:1, 0:2 * qt + 2], AX.X))
                    vb.mark(v.reduce_max(q2m[0:1, qt:qt + 1],
                                         qn2m[0:1, 2 * qt:2 * qt + 2], AX.X))
                vb.wait()
                v.tensor_mul(bbv[0:1, 0:Q2N], q2m[:],
                             kprefm[0:1, 0:Q2N]).then_inc(S["p2pm"], 1)
                v.wait_ge(S["p2l2"], h + 1)
                v.tensor_scalar(brow[0:1, 0:Q2N], bbe[0:1, 0:Q2N],
                                -SCALE * 1.001, None,
                                ALU.mult).then_inc(S["p2br"], 1)
                if h == 0:
                    v.wait_ge(S["msk0"], 16)
                g0 = FT2 * h
                nmc = Q2N * h
                for qt in range(Q2N):
                    nk = 8 * qt + 8
                    Q = Q2N * h + qt
                    for kb in range(nk):
                        gg = g0 + lidx(qt, kb)
                        if kb >= 8 * qt:
                            dd = dcnt(h, qt, kb)
                            v.wait_ge(S["fst"], gg + 1)
                            if dd >= 2:
                                dprev = dd - 2
                                hh = dprev // 64
                                Lp, _ = DIAG_OF_HEAD[dprev % 64]
                                v.wait_ge(S["fexp"], FT2 * hh + Lp + 1)
                            jj = kb - 8 * qt
                            v.tensor_add(stile[dd % 2][:], ps_a[gg % 2][:],
                                         masks[:, 1024 - 128 * jj:2048 - 128 * jj]).then_inc(S["fmsk"], 1)
                        if kb % 2 == 1:
                            PC = pcix(h, qt, kb)
                            v.wait_ge(S["fexp"], gg + 1)
                            if PC >= 2:
                                v.wait_ge(S["flp"], BFP + PC - 1)
                            v.tensor_add(pp_sb[PC % 2][:], pt[(gg - 1) % 4][:],
                                         pt[gg % 4][:]).then_inc(S["fpp"], 1)
                    PCe = 144 * h + 2 * (qt + 1) * (qt + 2)
                    if qt >= 1:
                        v.wait_ge(S["flbc"], Q)
                        if qt == 1 and h >= 1:
                            v.wait_ge(S["p2ao"], 16 * h)
                        v.tensor_mul(OT_s[0][:, (qt - 1) * 1024:qt * 1024],
                                     praw[:], lb_s[:]).then_inc(S["fnm"], 1)
                        nmc += 1
                    v.wait_ge(S["flp"], BFP + PCe)
                    v.wait_ge(S["fnm"], nmc)
                    v.tensor_copy(praw[:], ps_b[:]).then_inc(S["fqd"], 1)
                    if Q >= 1:
                        v.wait_ge(S["flb"], Q)
                    v.reciprocal(linv_s[:], ps_c[0:1, 0:1024]).then_inc(S["flv"], 1)
                v.wait_ge(S["flbc"], Q2N * (h + 1))
                v.tensor_mul(OT_s[0][:, (Q2N - 1) * 1024:Q2N * 1024],
                             praw[:], lb_s[:]).then_inc(S["fnm"], 1)
            if KPH < 3:
                return
            for tt in range(LTT):
                for ec in range(CH):
                    w = tt * CH + ec
                    v.wait_ge(S["p3e"], w + 1)
                    v.wait_ge(S["p3xr0" if w % 2 == 0 else "p3xr1"],
                              16 * (w // 2 + 1))
                    if w >= 1:
                        v.wait_ge(S["p3st"], w)
                    v.tensor_add(y_sb[:, ec * 512:(ec + 1) * 512], ytmp[:],
                                 xrb[w % 2][:]).then_inc(S["p3y"], 1)
                    v.wait_ge(S["p3y"], w + 1)
                    v.tensor_copy(ybf[:], y_sb[:, ec * 512:(ec + 1) * 512])
                    v.tensor_mul(sqy[:], y_sb[:, ec * 512:(ec + 1) * 512],
                                 y_sb[:, ec * 512:(ec + 1) * 512]).then_inc(S["p3s"], 1)
                v.wait_ge(S["p3st"], (tt + 1) * CH)
                vb.mark(v.tensor_scalar(me_sb[:], pC(0, p0=0, p1=1), 1.0 / D,
                                        None, ALU.mult))
                vb.mark(v.tensor_scalar(v1_sb[:], pC(1, p0=0, p1=1), 1.0 / D,
                                        None, ALU.mult))
                vb.wait()
                vb.mark(v.tensor_mul(m2_sb[:], me_sb[:], me_sb[:]))
                vb.wait()
                vb.mark(v.tensor_sub(ms_sb[:], v1_sb[:], m2_sb[:]))
                vb.wait()
                v.tensor_scalar(var_sb[:], ms_sb[:], LN_EPS, None,
                                ALU.add).then_inc(S["p3lnb"], 1)
                v.wait_ge(S["p3ln"], tt + 1)
                v.tensor_mul(mr_sb[:], me_sb[:], rst2_sb[:]).then_inc(S["p3mr"], 1)
                for ec in range(CH):
                    w = tt * CH + ec
                    v.wait_ge(S["p3bce"], tt + 1)
                    if w >= 1:
                        v.wait_ge(S["p3xo0" if (w - 1) % 2 == 0 else "p3xo1"],
                                  16 * ((w - 1) // 2 + 1))
                    vb.mark(v.tensor_mul(x1f_t[:],
                                         y_sb[:, ec * 512:(ec + 1) * 512],
                                         rb_sb[:]))
                    vb.wait()
                    vb.mark(v.tensor_sub(x1g_t[:], x1f_t[:], mrb_sb[:]))
                    vb.wait()
                    v.tensor_copy(x1T_s[:, ec * LTOK + tt * 512: ec * LTOK + (tt + 1) * 512],
                                  x1g_t[:]).then_inc(S["p3n"], 1)
            if KPH < 4:
                return
            v.memset(runtot[0][:], 0.0)
            v.memset(runtot[1][:], 0.0).then_inc(S["msk0"], 1)
            for tc in range(CH):
                v.wait_ge(S["p4m"], tc + 1)
                vb.mark(v.reduce_max(nmx[:], pA(tc % 2, 0, E), AX.X))
                vb.wait()
                v.tensor_scalar(negmax[:], nmx[:], -1.0, None,
                                ALU.mult).then_inc(S["p4sf"], 1)
                v.wait_ge(S["p4ex"], tc + 1)
                vb.mark(v.reduce_sum(ssum[:], pe_sb[:], AX.X))
                vb.wait()
                vb.mark(v.reciprocal(si_sb[:], ssum[:]))
                vb.wait()
                vb.mark(v.tensor_scalar(probs[:], pe_sb[:], si_sb[:], None,
                                        ALU.mult))
                vb.wait()
                vb.mark(v.reduce_max(pv_sb[:, 2 * tc:2 * tc + 1], probs[:], AX.X))
                vb.wait()
                vb.mark(v.tensor_scalar(eqt[:], probs[:],
                                        pv_sb[:, 2 * tc:2 * tc + 1],
                                        None, ALU.is_equal))
                vb.wait()
                vb.mark(v.tensor_mul(at_sb[:], eqt[:], iota8[:]))
                vb.wait()
                vb.mark(v.scalar_tensor_tensor(t1_sb[:], eqt[:], -99.0,
                                               at_sb[:], ALU.mult, ALU.add))
                vb.wait()
                vb.mark(v.tensor_scalar(idxf_sb[:], t1_sb[:], 99.0, None,
                                        ALU.add))
                vb.wait()
                vb.mark(v.tensor_reduce(idx_sb[:], idxf_sb[:], AX.X, ALU.min))
                vb.wait()
                vb.mark(v.tensor_scalar(ohf_sb[:, 16 * tc:16 * tc + 8],
                                        iota8[:], idx_sb[:], None,
                                        ALU.is_equal))
                vb.wait()
                vb.mark(v.scalar_tensor_tensor(maskd[:],
                                               ohf_sb[:, 16 * tc:16 * tc + 8],
                                               -2.0, probs[:], ALU.mult,
                                               ALU.add))
                vb.wait()
                vb.mark(v.reduce_max(pv_sb[:, 2 * tc + 1:2 * tc + 2], maskd[:],
                                     AX.X))
                vb.wait()
                vb.mark(v.tensor_scalar(eqt[:], maskd[:],
                                        pv_sb[:, 2 * tc + 1:2 * tc + 2], None,
                                        ALU.is_equal))
                vb.wait()
                vb.mark(v.tensor_mul(at_sb[:], eqt[:], iota8[:]))
                vb.wait()
                vb.mark(v.scalar_tensor_tensor(t1_sb[:], eqt[:], -99.0,
                                               at_sb[:], ALU.mult, ALU.add))
                vb.wait()
                vb.mark(v.tensor_scalar(idxf_sb[:], t1_sb[:], 99.0, None,
                                        ALU.add))
                vb.wait()
                vb.mark(v.tensor_reduce(idx_sb[:], idxf_sb[:], AX.X, ALU.min))
                vb.wait()
                v.tensor_scalar(ohf_sb[:, 16 * tc + 8:16 * tc + 16], iota8[:],
                                idx_sb[:], None,
                                ALU.is_equal).then_inc(S["p4oh"], 1)
                v.wait_ge(S["p4cm"], 3 * tc + 2)
                v.tensor_copy(c_sb[:, 16 * tc:16 * tc + 8], pA(2, 0, E))
                v.tensor_copy(c_sb[:, 16 * tc + 8:16 * (tc + 1)],
                              pA(3, 0, E)).then_inc(S["p4ce"], 2)
                v.wait_ge(S["p4cm"], 3 * tc + 3)
                v.wait_ge(S["p4rt"], tc)
                v.tensor_add(runtot[(tc + 1) % 2][:], runtot[tc % 2][:],
                             pC(0, 0, 16, 0, 1)).then_inc(S["p4rt"], 1)
            v.wait_ge(S["p4om"], 1)
            vb.mark(v.tensor_copy(offr_sb[:], pC(0, 0, 16, 0, 1)))
            vb.wait()
            v.tensor_add(offr_sb[0:1, 8:16], offr_sb[0:1, 8:16],
                         pC(1, 0, 8, 0, 1)).then_inc(S["p4ov"], 1)
            v.wait_ge(S["p4ob"], 1)
            v.tensor_copy(offb_sb[:], pC(1, 0, 16)).then_inc(S["p4obe"], 1)
            v.wait_ge(S["p4obe"], 1)
            v.wait_ge(S["p4ce"], 2 * CH)
            v.wait_ge(S["p4oh"], CH)
            for tc in range(CH):
                vb.mark(v.tensor_add(glob_sb[:], c_sb[:, 16 * tc:16 * (tc + 1)],
                                     offb_sb[:]))
                vb.wait()
                vb.mark(v.tensor_scalar(keep_sb[:], glob_sb[:], 512.5, None,
                                        ALU.is_le))
                vb.wait()
                vb.mark(v.tensor_mul(sel_sb[:], keep_sb[:],
                                     ohf_sb[:, 16 * tc:16 * (tc + 1)]))
                vb.wait()
                vb.mark(v.reduce_sum(g1_sb[:], sel_sb[:, 0:8], AX.X))
                vb.mark(v.reduce_sum(g2_sb[:], sel_sb[:, 8:16], AX.X))
                vb.wait()
                vb.mark(v.tensor_mul(ga_sb[:], g1_sb[:],
                                     pv_sb[:, 2 * tc:2 * tc + 1]))
                vb.mark(v.tensor_mul(gb2_sb[:], g2_sb[:],
                                     pv_sb[:, 2 * tc + 1:2 * tc + 2]))
                vb.wait()
                v.tensor_add(g_sb[:, tc:tc + 1], ga_sb[:],
                             gb2_sb[:]).then_inc(S["p4g"], 1)
            v.wait_ge(S["p4gt"], 1)
            v.tensor_copy(gt_sb[:], pC(0, 0, 128, 0, 16)).then_inc(S["p4gte"], 1)
            if KPH < 5:
                return
            for tt in range(LTT):
                for ec in range(CH):
                    Ei = tt * CH + ec
                    v.wait_ge(S["p5ev"], Ei + 1)
                    v.wait_ge(S["p5gb"], tt + 1)
                    if Ei >= 2:
                        v.wait_ge(S["p5fo0" if Ei % 2 == 0 else "p5fo1"],
                                  BP3 + 16 * ((Ei - 2) // 2 + 1))
                    v.tensor_mul(ffb[Ei % 2][:], fev[Ei % 2][:],
                                 gbc_sb[:]).then_inc(S["p5e"], 1)
            if KPH < 6:
                return
            v.wait_ge(S["p5o"], LTT * CH)
            for tt in range(LTT):
                v.wait_ge(S["p6x0" if tt % 2 == 0 else "p6x1"],
                          BP1X + 16 * (tt // 2 + 1))
                v.wait_ge(S["p6f0" if tt % 2 == 0 else "p6f1"],
                          BP1O + 16 * (tt // 2 + 1))
                for ec in range(CH):
                    w = tt * CH + ec
                    if w >= 1:
                        v.wait_ge(S["p6st"], w)
                    v.tensor_add(y2_sb[:, ec * 512:(ec + 1) * 512],
                                 x6ab[:, ec * 512:(ec + 1) * 512],
                                 x6bb[:, ec * 512:(ec + 1) * 512]).then_inc(S["p6y"], 1)
                    v.wait_ge(S["p6y"], w + 1)
                    v.tensor_copy(ybf[:], y2_sb[:, ec * 512:(ec + 1) * 512])
                    v.tensor_mul(sqy[:], y2_sb[:, ec * 512:(ec + 1) * 512],
                                 y2_sb[:, ec * 512:(ec + 1) * 512]).then_inc(S["p6s"], 1)
                v.wait_ge(S["p6st"], (tt + 1) * CH)
                vb.mark(v.tensor_scalar(me_sb[:], pC(0, p0=0, p1=1), 1.0 / D,
                                        None, ALU.mult))
                vb.mark(v.tensor_scalar(v1_sb[:], pC(1, p0=0, p1=1), 1.0 / D, None,
                                        ALU.mult))
                vb.wait()
                vb.mark(v.tensor_mul(m2_sb[:], me_sb[:], me_sb[:]))
                vb.wait()
                vb.mark(v.tensor_sub(ms_sb[:], v1_sb[:], m2_sb[:]))
                vb.wait()
                v.tensor_scalar(var_sb[:], ms_sb[:], LN_EPS, None,
                                ALU.add).then_inc(S["p6lnb"], 1)
                v.wait_ge(S["p6ln"], tt + 1)
                v.tensor_mul(mr_sb[:], me_sb[:], rst2_sb[:]).then_inc(S["p6mr"], 1)
                for ec in range(CH):
                    w = tt * CH + ec
                    v.wait_ge(S["p6bce"], tt + 1)
                    vb.mark(v.tensor_mul(x1f_t[:],
                                         y2_sb[:, ec * 512:(ec + 1) * 512],
                                         rb_sb[:]))
                    vb.wait()
                    v.tensor_sub(x2_sb[:], x1f_t[:], mrb_sb[:]).then_inc(S["p6n"], 1)
                    v.wait_ge(S["p6tr"], w + 1)
                    if w >= 2:
                        v.wait_ge(S["p6o0" if w % 2 == 0 else "p6o1"],
                                  16 * ((w - 2) // 2 + 1))
                    v.tensor_copy(xt2_sb[w % 2][:],
                                  ps_b[:, 0:512]).then_inc(S["p6tv"], 1)
            v.wait_ge(S["p6o0"], 16 * LTT * CH // 2)
            v.wait_ge(S["p6o1"], 16 * LTT * CH // 2)

    return nc


_CACHE = {}


def kernel(L0_attn_x, L0_attn_norm_w, L0_attn_wq, L0_attn_bq, L0_attn_wk,
           L0_attn_bk, L0_attn_wv, L0_attn_bv, L0_attn_wo, L0_attn_bo,
           L0_ffn_router_w, L0_ffn_w1, L0_ffn_b1, L0_ffn_w2, L0_ffn_b2):
    x = np.ascontiguousarray(np.asarray(L0_attn_x, np.float32))
    norm_w = np.asarray(L0_attn_norm_w, np.float32)
    wq = np.asarray(L0_attn_wq, np.float32)
    bq = np.asarray(L0_attn_bq, np.float32)
    wk = np.asarray(L0_attn_wk, np.float32)
    bk = np.asarray(L0_attn_bk, np.float32)
    wv = np.asarray(L0_attn_wv, np.float32)
    bv = np.asarray(L0_attn_bv, np.float32)
    wo = np.asarray(L0_attn_wo, np.float32)
    bo = np.asarray(L0_attn_bo, np.float32)
    router_w = np.asarray(L0_ffn_router_w, np.float32)
    w1 = np.asarray(L0_ffn_w1, np.float32)
    b1 = np.asarray(L0_ffn_b1, np.float32)
    w2 = np.asarray(L0_ffn_w2, np.float32)
    b2 = np.asarray(L0_ffn_b2, np.float32)

    if "nc" not in _CACHE:
        _CACHE["nc"] = build_program()
    nc = _CACHE["nc"]

    woTt = np.ascontiguousarray(
        wo.T.reshape(16, 128, 16, 128).transpose(2, 1, 0, 3)
        .reshape(16, 128, CH * 128))
    w1t = np.ascontiguousarray(
        w1.T.reshape(16, 128, 64, 128).transpose(2, 1, 0, 3)
        .reshape(64, 128, CH * 128))
    w2t = np.ascontiguousarray(
        w2.T.reshape(64, 128, 16, 128).transpose(2, 1, 0, 3)
        .reshape(16, 128, 64 * 128))
    bo_cols = np.ascontiguousarray(bo.reshape(16, 128).T)
    b1_cols = np.ascontiguousarray(b1.reshape(64, 128).T)
    b2_cols = np.ascontiguousarray(b2.reshape(16, 128).T)
    nw_r = np.ascontiguousarray(norm_w.reshape(1, D))
    rwTc = np.ascontiguousarray(router_w.T)
    tri = np.triu(np.ones((128, 128), np.float32))
    _k = np.arange(128)[:, None]
    _u = np.arange(2048)[None, :]
    msk = np.where(_u >= _k + 1024, 0.0, -30000.0).astype(np.float32)
    iota = np.ascontiguousarray(
        np.tile(np.arange(E, dtype=np.float32), (128, 1)))
    ident = np.eye(128, dtype=np.float32)

    in_maps = []
    for c in range(8):
        b, hgr = c // 4, c % 4
        heads = [4 * j + hgr for j in range(NH)]
        xTb = np.ascontiguousarray(x[b].T)
        xTt_b = np.ascontiguousarray(
            xTb.reshape(16, 128, TTN, 512).transpose(2, 1, 0, 3)
            .reshape(TTN, 128, CH * 512))

        def _slice(w):
            rows = np.concatenate([w[h * 128:(h + 1) * 128] for h in heads], 0)
            return np.ascontiguousarray(rows.T)

        bq_c = np.stack([bq[h * 128:(h + 1) * 128] for h in heads], 1)
        bk_c = np.stack([bk[h * 128:(h + 1) * 128] for h in heads], 1)
        bv_c = np.stack([bv[h * 128:(h + 1) * 128] for h in heads], 1)
        bqkv_c = np.ascontiguousarray(
            np.concatenate([bq_c, bk_c, bv_c], 1))
        cm = np.zeros((8, 2), np.float32)
        cm[:, 0] = (np.arange(8) < c).astype(np.float32)
        cm[:, 1] = 1.0
        in_maps.append(dict(
            xTt=xTt_b, xTr=np.ascontiguousarray(xTb[:, hgr * LTOK:(hgr + 1) * LTOK]),
            wqT=_slice(wq), wkT=_slice(wk), wvT=_slice(wv),
            bqkv=bqkv_c, woTt=woTt, bo_c=bo_cols, nw_row=nw_r, rwT=rwTc,
            w1t=w1t, b1_c=b1_cols, w2t=w2t, b2_c=b2_cols, tri_in=tri,
            masks_in=msk, iota_in=iota, cmask_in=cm, id_in=ident))

    res = run_bass_kernel_spmd(nc, in_maps, core_ids=list(range(8)))
    y = np.empty((2, T, D), np.float32)
    for c in range(8):
        b, hgr = c // 4, c % 4
        y[b, hgr * LTOK:(hgr + 1) * LTOK, :] = np.asarray(res.results[c]["out"],
                                                          np.float32)
    return y
